# version 11
# baseline (speedup 1.0000x reference)
import numpy as np
import concourse.bass as bass
import concourse.mybir as mybir
from concourse.bass_utils import run_bass_kernel_spmd

F32 = mybir.dt.float32
BF16 = mybir.dt.bfloat16
I32 = mybir.dt.int32
U16 = mybir.dt.uint16
AF = mybir.ActivationFunctionType
OP = mybir.AluOpType
AX = mybir.AxisListType.X

MRND = 1.5 * 2.0 ** 23
E = 1024
HCNT = 16
D = 64
TOK = 2048
NMAC = 4
MT = 512
NSUB = 8
ST = 256
INNER = 4096
EPS = 1e-5

DUMPS = ()
STAGE = 3
BSTOP = 8
DUMP_U = 0
ASTOP = 5


class P:
    ENG = ("pe", "act", "dve", "pool", "sp", "gp")

    def __init__(self, nc):
        self.nc = nc
        self.ops = {e: [] for e in self.ENG}
        self.cnt = {}
        self.sems = {}
        self.seq = 0

    def _app(self, eng, fn):
        self.ops[eng].append((self.seq, fn))
        self.seq += 1

    def sem(self, name):
        h = self.nc.alloc_semaphore(name=name)
        self.sems[name] = h
        self.cnt[name] = 0

    def wait(self, eng, sname, val=None):
        v = self.cnt[sname] if val is None else val
        if v <= 0:
            return
        h = self.sems[sname]
        self._app(eng, lambda e: e.wait_ge(h, v))

    def step(self, eng, build_fn, sname=None):
        sname = sname or eng
        self.cnt[sname] += 1
        h = self.sems[sname]

        def fn(e):
            build_fn(e).then_inc(h, 1)
        self._app(eng, fn)
        return self.cnt[sname]

    def plain(self, eng, build_fn):
        self._app(eng, build_fn)

    NPOOL = 16

    def dma(self, eng, q, out_ap, in_ap):
        i = self.rot[q]
        self.rot[q] = (i + 1) % self.NPOOL
        s = f"{q}{i}"
        h = self.sems[s]
        prev = self.cnt[s]
        if prev > 0:
            self._app(eng, lambda e: e.wait_ge(h, prev))
        self.cnt[s] = prev + 16
        tgt = self.cnt[s]

        def fn(e):
            e.dma_start(out=out_ap, in_=in_ap).then_inc(h, 16)
        self._app(eng, fn)
        return (s, tgt)

    def wdma(self, eng, *handles):
        for s, tgt in handles:
            self.wait(eng, s, tgt)

    def wdma_all(self, eng, q):
        for i in range(self.NPOOL):
            self.wait(eng, f"{q}{i}")


def build_program(dumps=()):
    nc = bass.Bass("TRN2", target_bir_lowering=False)
    p = P(nc)

    xT = nc.dram_tensor("xT", [E, TOK], F32, kind="ExternalInput").ap()
    Wq = nc.dram_tensor("Wq", [E, E], F32, kind="ExternalInput").ap()
    Wk = nc.dram_tensor("Wk", [E, E], F32, kind="ExternalInput").ap()
    Wv = nc.dram_tensor("Wv", [E, E], F32, kind="ExternalInput").ap()
    Wo = nc.dram_tensor("Wo", [E, E], F32, kind="ExternalInput").ap()
    Wg1 = nc.dram_tensor("Wg1", [E, D], F32, kind="ExternalInput").ap()
    Wg2 = nc.dram_tensor("Wg2", [D, E], F32, kind="ExternalInput").ap()
    wgT = nc.dram_tensor("wgT", [E, INNER], F32, kind="ExternalInput").ap()
    wuSL = nc.dram_tensor("wuSL", [32, E, 128], F32, kind="ExternalInput").ap()
    wdT = nc.dram_tensor("wdT", [INNER, E], F32, kind="ExternalInput").ap()
    gam = nc.dram_tensor("gam", [128, 8], F32, kind="ExternalInput").ap()
    bet = nc.dram_tensor("bet", [128, 8], F32, kind="ExternalInput").ap()
    cosd = nc.dram_tensor("cosd", [128, TOK], F32, kind="ExternalInput").ap()
    ssin = nc.dram_tensor("ssin", [128, TOK], F32, kind="ExternalInput").ap()
    selc = nc.dram_tensor("selc", [64, 8], F32, kind="ExternalInput").ap()
    y = nc.dram_tensor("y", [E, TOK], F32, kind="ExternalOutput").ap()

    kfm_sp = nc.dram_tensor("kfm_sp", [128, 16384], BF16).ap()
    ktm_sp = nc.dram_tensor("ktm_sp", [128, 16384], BF16).ap()
    v_sp = nc.dram_tensor("v_sp", [128, 16384], BF16).ap()
    xmid_sp = nc.dram_tensor("xmid_sp", [128, 16384], F32).ap()
    wd_t = nc.dram_tensor("wd_t", [INNER, E], BF16).ap()
    wu_t = nc.dram_tensor("wu_t", [32, 128, E], BF16).ap()
    cc_in = nc.dram_tensor("cc_in_t", [64, 1024], F32).ap()
    cc_out = nc.dram_tensor("cc_out_t", [512, 1024], F32,
                            addr_space="Shared").ap()

    dbg = {}
    for name, shape in dumps:
        dbg[name] = nc.dram_tensor("dbg_" + name, list(shape), F32,
                                   kind="ExternalOutput").ap()

    for s in ("cc", "pe", "act", "dve", "pool", "gpc"):
        p.sem(s)
    p.rot = {"dsp": 0, "dgp": 0}
    for q in ("dsp", "dgp"):
        for i in range(P.NPOOL):
            p.sem(f"{q}{i}")

    from contextlib import ExitStack
    octx = ExitStack()

    def osb(name, shape, dt=F32):
        return octx.enter_context(nc.sbuf_tensor(name, shape, dt))

    ident = osb("ident", [128, 128], BF16)
    mask4 = osb("mask4", [128, 512], F32)
    ones_col = osb("ones_col", [128, 1], BF16)
    ones_row = osb("ones_row", [1, 128], BF16)
    gam_sb = osb("gam_sb", [128, 8])
    bet_sb = osb("bet_sb", [128, 8])
    mcol = osb("mcol", [128, 1])
    mrow = osb("mrow", [128, 512], BF16)
    wsc = osb("wsc", [128, 8])
    wacc = osb("wacc", [128, 192])
    wab_bf = osb("wab_bf", [128, 128], BF16)
    tiny = osb("tiny", [1, 16])
    tiny_bf = osb("tiny_bf", [1, 16], BF16)
    selc_sb = osb("selc_sb", [64, 8])
    zrow_bf = osb("zrow_bf", [1, 512], BF16)
    cc_bf = osb("cc_bf", [64, 1024], BF16)
    seldg = osb("seldg", [64, 512], BF16)

    def setup(g):
        g.memset(ident[:, :], 0.0)
        g.affine_select(out=ident[:, :], in_=ident[:, :], compare_op=OP.not_equal,
                        fill=1.0, base=0, pattern=[[-1, 128]], channel_multiplier=1)
        g.memset(mask4[:, :], 1.0)
        g.affine_select(out=mask4[:, :].rearrange("p (r n) -> p r n", r=4),
                        in_=mask4[:, :].rearrange("p (r n) -> p r n", r=4),
                        compare_op=OP.is_ge, fill=0.0, base=0,
                        pattern=[[0, 4], [1, 128]], channel_multiplier=-1)
        g.memset(ones_col[:, :], 1.0)
        g.memset(ones_row[:, :], 1.0)
        g.memset(wacc[:, :], 0.0)
        g.memset(zrow_bf[:, :], 0.0)
        g.memset(mrow[:, :], MRND)
        g.memset(mcol[:, :], MRND).then_inc(p.sems["gpc"], 1)
    p.plain("gp", setup)
    p.cnt["gpc"] += 1
    h_cst = [p.dma("gp", "dgp", gam_sb[:, :], gam),
             p.dma("gp", "dgp", bet_sb[:, :], bet),
             p.dma("gp", "dgp", selc_sb[:, :], selc)]
    p.wait("dve", "gpc", 1)
    p.wait("pool", "gpc", 1)
    p.wait("act", "gpc", 1)
    p.wdma("dve", *h_cst)
    p.wdma("pe", *h_cst)

    ab = ExitStack()

    def absb(name, shape, dt=F32):
        return ab.enter_context(nc.sbuf_tensor(name, shape, dt))

    cos_m = absb("cos_m", [128, 512])
    sin_m = absb("sin_m", [128, 512])
    W1 = absb("W1", [128, 8192], BF16)
    W2 = absb("W2", [128, 8192], BF16)
    Wg1_sb = absb("Wg1_sb", [128, 512], BF16)
    Wg2_sb = absb("Wg2_sb", [64, 1024], BF16)
    xt_bf = absb("xt_bf", [128, 4096], BF16)
    FA = absb("FA", [128, 4096])
    FB = absb("FB", [128, 4096])
    FC = absb("FC", [128, 4096])
    krot = absb("krot", [128, 4096], BF16)
    ktm = absb("ktm", [128, 4096], BF16)
    vt = absb("vt", [128, 4096], BF16)
    kfmB = absb("kfmB", [128, 4096], BF16)
    q_oddM = absb("q_oddM", [64, 4096], BF16)
    k_oddM = absb("k_oddM", [64, 4096], BF16)
    eT_sb = absb("eT_sb", [128, 2048], BF16)
    S_bf = absb("S_bf", [64, 1024], BF16)
    stat_bf = absb("stat_bf", [128, 2048], BF16)
    xn_bf = absb("xn_bf", [128, 4096], BF16)
    swd = absb("swd", [128, 512])
    t1d = absb("t1d", [128, 512])
    t2d = absb("t2d", [128, 512])
    swp = absb("swp", [128, 512])
    t1p = absb("t1p", [128, 512])
    t2p = absb("t2p", [128, 512])
    xg1_bf = absb("xg1_bf", [64, 512], BF16)
    lnr = absb("lnr", [1, 2048])
    cc_sb = FB[0:64, 0:1024]
    sin_f = FB[0:64, 1024:2048]
    lnr_bf = absb("lnr_bf", [1, 2048], BF16)
    brstd = absb("brstd", [128, 512])
    bmrs = absb("bmrs", [128, 512])
    wmb = absb("wmb", [128, 1024])
    jA = absb("jA", [128, 512])

    wtb = absb("wtb", [128, 512], BF16)
    wsb_bf = absb("wsb_bf", [128, 2], BF16)
    oscr_b = absb("oscr_b", [128, 512])

    abps = ExitStack()
    ps_S = abps.enter_context(nc.psum_tensor("ps_S", [64, 1024], F32))
    ps_W1 = abps.enter_context(nc.psum_tensor("ps_W1", [128, 512], F32))
    ps_W2 = abps.enter_context(nc.psum_tensor("ps_W2", [128, 512], F32))

    aps = ExitStack()
    ps_V = aps.enter_context(nc.psum_tensor("ps_V", [128, 1024], F32))
    ps_T = aps.enter_context(nc.psum_tensor("ps_T", [128, 128], BF16))

    h_w = [p.dma("gp", "dgp", W1[:, :].rearrange("p (f o) -> p f o", f=8),
                 Wk.rearrange("(f p) o -> p f o", p=128)),
           p.dma("gp", "dgp", W2[:, :].rearrange("p (f o) -> p f o", f=8),
                 Wv.rearrange("(f p) o -> p f o", p=128)),
           p.dma("gp", "dgp", Wg1_sb[:, :].rearrange("p (f o) -> p f o", f=8),
                 Wg1.rearrange("(f p) o -> p f o", p=128)),
           p.dma("gp", "dgp", Wg2_sb[:, :], Wg2)]

    def emit_rotary(eng, cols, src, dst_bf, sw, t1, t2):
        def fn(e):
            e.tensor_copy(sw[0:32, :], src[32:64, cols])
            e.tensor_copy(sw[32:64, :], src[0:32, cols])
            e.tensor_copy(sw[64:96, :], src[96:128, cols])
            e.tensor_copy(sw[96:128, :], src[64:96, cols])
            e.tensor_tensor(t1[:, :], src[:, cols], cos_m[:, :], OP.mult)
            e.tensor_tensor(t2[:, :], sw[:, :], sin_m[:, :], OP.mult)
            return e.tensor_tensor(dst_bf[:, cols], t1[:, :], t2[:, :], OP.add)
        p.step(eng, fn)

    def spacer(e):
        return e.tensor_scalar(jA[:, 256:512], jA[:, 0:256], 1.0, None, OP.mult)

    wj_hist = []

    def emit_wmean_job(src_ap, jcol):
        j = emit_wmean_job.i
        emit_wmean_job.i += 1
        buf = wmb[:, (j % 2) * 512:(j % 2) * 512 + 512]
        q, dq = ("sp", "dsp") if j % 2 == 0 else ("gp", "dgp")
        if j >= 2:
            p.wait(q, wj_hist[j - 2][0], wj_hist[j - 2][1])
        else:
            p.wait(q, "act")
            p.wait(q, "dve")
        hb = p.dma(q, dq, buf, src_ap)
        p.wdma("act", hb)
        p.wait("act", "pool")

        def f1(e, buf=buf, jcol=jcol):
            return e.activation(jA[:, 0:512], buf, AF.Abs,
                                accum_out=wacc[:, jcol:jcol + 1])
        wj_hist.append(("act", p.step("act", f1)))
    emit_wmean_job.i = 0

    def emit_wscale(jc0, n_elems, scol, icol):
        p.wait("dve", "act")

        def sp1(e, jc0=jc0):
            a, b = t1d, t2d
            e.tensor_tensor(a[:, 0:32], wacc[:, jc0:jc0 + 32],
                            wacc[:, jc0 + 32:jc0 + 64], OP.add)
            spacer(e)
            e.tensor_tensor(a[:, 32:48], a[:, 0:16], a[:, 16:32], OP.add)
            spacer(e)
            e.tensor_tensor(a[:, 48:56], a[:, 32:40], a[:, 40:48], OP.add)
            spacer(e)
            e.tensor_tensor(a[:, 56:60], a[:, 48:52], a[:, 52:56], OP.add)
            spacer(e)
            e.tensor_tensor(a[:, 60:62], a[:, 56:58], a[:, 58:60], OP.add)
            spacer(e)
            e.tensor_tensor(a[:, 62:63], a[:, 60:61], a[:, 61:62], OP.add)
            spacer(e)
            e.tensor_copy(b[0:64, 0:1], a[64:128, 62:63])
            spacer(e)
            e.tensor_tensor(b[0:64, 1:2], a[0:64, 62:63], b[0:64, 0:1], OP.add)
            spacer(e)
            e.tensor_copy(b[0:32, 2:3], b[32:64, 1:2])
            spacer(e)
            e.tensor_tensor(b[0:32, 3:4], b[0:32, 1:2], b[0:32, 2:3], OP.add)
            spacer(e)
            e.tensor_scalar(b[0:32, 4:5], b[0:32, 3:4], MRND, None, OP.add)
            spacer(e)
            e.tensor_scalar(b[0:32, 10:11], b[0:32, 4:5], -MRND, None, OP.add)
            spacer(e)
            e.tensor_tensor(b[0:32, 5:6], b[0:32, 3:4], b[0:32, 10:11], OP.subtract)
            spacer(e)
            e.tensor_scalar(b[0:32, 6:7], b[0:32, 10:11], 1.0 / 256.0, MRND,
                            OP.mult, OP.add)
            spacer(e)
            e.tensor_scalar(b[0:32, 7:8], b[0:32, 6:7], -MRND, 256.0,
                            OP.add, OP.mult)
            spacer(e)
            e.tensor_tensor(b[0:32, 8:9], b[0:32, 10:11], b[0:32, 7:8], OP.subtract)
            spacer(e)
            e.tensor_copy(wab_bf[0:32, 0:1], b[0:32, 7:8])
            e.tensor_copy(wab_bf[0:32, 1:2], b[0:32, 8:9])
            e.tensor_copy(wab_bf[0:32, 2:3], b[0:32, 5:6])
            spacer(e)
            e.scalar_tensor_tensor(b[0:32, 9:10], wab_bf[0:32, 2:3], -1.0,
                                   b[0:32, 5:6], OP.mult, OP.add)
            spacer(e)
            return e.tensor_copy(wab_bf[0:32, 3:4], b[0:32, 9:10])
        p.step("dve", sp1)
        p.wait("pe", "dve")

        def sm(e):
            return e.matmul(ps_W1[0:1, 0:4], ones_col[0:32, 0:1],
                            wab_bf[0:32, 0:4], start=True, stop=True)
        p.step("pe", sm)
        p.wait("act", "pe")

        def sacc_f(e):
            return e.activation(jA[0:1, 0:4], ps_W1[0:1, 0:4], AF.Identity,
                                accum_out=tiny[0:1, 0:1])
        p.step("act", sacc_f)
        p.wait("dve", "act")

        def sc(e, n_elems=n_elems):
            e.tensor_scalar(tiny[0:1, 2:3], tiny[0:1, 0:1], 1.0 / n_elems, 1e-5,
                            OP.mult, OP.max)
            spacer(e)
            e.reciprocal(tiny[0:1, 3:4], tiny[0:1, 2:3])
            spacer(e)
            e.tensor_tensor(tiny[0:1, 5:6], tiny[0:1, 2:3], tiny[0:1, 3:4], OP.mult)
            spacer(e)
            e.tensor_scalar(tiny[0:1, 6:7], tiny[0:1, 5:6], -1.0, 2.0,
                            OP.mult, OP.add)
            spacer(e)
            e.tensor_tensor(tiny[0:1, 7:8], tiny[0:1, 3:4], tiny[0:1, 6:7], OP.mult)
            spacer(e)
            e.tensor_copy(tiny_bf[0:1, 0:1], tiny[0:1, 7:8])
            spacer(e)
            e.scalar_tensor_tensor(tiny[0:1, 4:5], tiny_bf[0:1, 0:1], -1.0,
                                   tiny[0:1, 7:8], OP.mult, OP.add)
            spacer(e)
            e.tensor_copy(tiny_bf[0:1, 1:2], tiny[0:1, 4:5])
            e.tensor_copy(tiny_bf[0:1, 2:3], tiny[0:1, 2:3])
            spacer(e)
            e.scalar_tensor_tensor(tiny[0:1, 9:10], tiny_bf[0:1, 1:2], -1.0,
                                   tiny[0:1, 4:5], OP.mult, OP.add)
            e.scalar_tensor_tensor(tiny[0:1, 8:9], tiny_bf[0:1, 2:3], -1.0,
                                   tiny[0:1, 2:3], OP.mult, OP.add)
            spacer(e)
            e.tensor_copy(tiny_bf[0:1, 4:5], tiny[0:1, 9:10])
            e.tensor_copy(tiny_bf[0:1, 3:4], tiny[0:1, 8:9])
            spacer(e)
            e.scalar_tensor_tensor(tiny[0:1, 10:11], tiny_bf[0:1, 3:4], -1.0,
                                   tiny[0:1, 8:9], OP.mult, OP.add)
            spacer(e)
            return e.tensor_copy(tiny_bf[0:1, 5:6], tiny[0:1, 10:11])
        p.step("dve", sc)
        p.wait("pe", "dve")

        def bc(e):
            e.matmul(ps_W2[:, 0:1], ones_row[:, :], tiny_bf[0:1, 0:1], start=True, stop=False)
            e.matmul(ps_W2[:, 0:1], ones_row[:, :], tiny_bf[0:1, 1:2], start=False, stop=False)
            e.matmul(ps_W2[:, 0:1], ones_row[:, :], tiny_bf[0:1, 4:5], start=False, stop=True)
            e.matmul(ps_W2[:, 1:2], ones_row[:, :], tiny_bf[0:1, 2:3],
                     start=True, stop=False)
            e.matmul(ps_W2[:, 1:2], ones_row[:, :], tiny_bf[0:1, 3:4],
                     start=False, stop=False)
            return e.matmul(ps_W2[:, 1:2], ones_row[:, :], tiny_bf[0:1, 5:6],
                            start=False, stop=True)
        p.step("pe", bc)
        p.wait("dve", "pe")

        def cv(e, scol=scol, icol=icol):
            e.tensor_copy(wsc[:, scol:scol + 1], ps_W2[:, 0:1])
            return e.tensor_copy(wsc[:, icol:icol + 1], ps_W2[:, 1:2])
        p.step("dve", cv)

    wd_flatI = wdT.rearrange("a b -> (a b)").rearrange("(c p t) -> c p t",
                                                        p=128, t=512)
    wd_flatO = wd_t.rearrange("a b -> (a b)").rearrange("(c p t) -> c p t",
                                                        p=128, t=512)

    def emit_wtern_job(j):
        ri = emit_wmean_job.i
        emit_wmean_job.i += 1
        buf = wmb[:, (ri % 2) * 512:(ri % 2) * 512 + 512]
        q, dq = ("sp", "dsp") if ri % 2 == 0 else ("gp", "dgp")
        if ri >= 2:
            p.wait(q, wj_hist[ri - 2][0], wj_hist[ri - 2][1])
        else:
            p.wait(q, "act")
            p.wait(q, "dve")
            p.wait(q, "pool")
        hb = p.dma(q, dq, buf, wd_flatI[j, :, :])
        p.wdma("dve", hb)
        p.wait("dve", "pool")

        def f1(e, buf=buf):
            return e.scalar_tensor_tensor(swd[:, :], buf, wsc[:, 0:1],
                                          mrow[:, :], OP.mult, OP.add)
        wj_hist.append(("dve", p.step("dve", f1)))
        p.wait("pool", "dve")

        def f2(e):
            e.tensor_scalar(oscr_b[:, :], swd[:, :], MRND + 1.0, MRND - 1.0,
                            OP.min, OP.max)
            return e.tensor_scalar(wtb[:, :], oscr_b[:, :], -MRND, None, OP.add)
        p.step("pool", f2)
        p.wait("sp", "pool")
        p.dma("sp", "dsp", wd_flatO[j, :, :], wtb[:, :])

    def emit_ln(src, gamma, out_bf=None, out_f32=None):
        for c in range(8):
            cl = slice(c * 512, (c + 1) * 512)
            r = (c % 2) * 512

            def cp(e, cl=cl, r=r):
                return e.tensor_copy(stat_bf[:, r:r + 512], src[:, cl])
            p.step("dve", cp)
            p.wait("act", "dve")

            def sq(e, cl=cl, r=r):
                return e.activation(stat_bf[:, 1024 + r:1024 + r + 512], src[:, cl],
                                    AF.Square)
            p.step("act", sq)
            p.wait("pe", "dve")
            p.wait("pe", "act")

            def st(e, c=c, r=r):
                e.matmul(ps_W1[0:1, :], ones_col[:, :], stat_bf[:, r:r + 512],
                         start=(c == 0), stop=(c == 7))
                return e.matmul(ps_W2[0:1, :], ones_col[:, :],
                                stat_bf[:, 1024 + r:1024 + r + 512],
                                start=(c == 0), stop=(c == 7))
            v = p.step("pe", st)
            if c < 7:
                p.wait("dve", "pe", v)
                p.wait("act", "pe", v)
        p.wait("dve", "pe")

        def chain(e):
            L = lnr
            mu, var, yy, tmp = (L[0:1, i * 512:(i + 1) * 512] for i in range(4))
            mrs = tmp
            t2 = var
            e.tensor_scalar(mu, ps_W1[0:1, :], 1.0 / E, None, OP.mult)
            e.tensor_scalar(tmp, ps_W2[0:1, :], 1.0 / E, EPS, OP.mult, OP.add)
            e.tensor_tensor(var, mu, mu, OP.mult)
            e.tensor_tensor(var, tmp, var, OP.subtract)
            e.tensor_copy(lnr_bf[0:1, 0:512], var)
            e.tensor_copy(tmp, lnr_bf[0:1, 0:512].bitcast(U16))
            e.tensor_scalar(tmp, tmp, 0.5, MRND, OP.mult, OP.add)
            e.tensor_scalar(tmp, tmp, -MRND, None, OP.add)
            e.tensor_scalar(tmp, tmp, -1.0, 24375.0, OP.mult, OP.add)
            e.tensor_copy(lnr_bf[0:1, 0:512].bitcast(U16), tmp)
            e.tensor_copy(yy, lnr_bf[0:1, 0:512])
            for _ in range(3):
                e.tensor_tensor(tmp, yy, yy, OP.mult)
                e.tensor_tensor(tmp, tmp, var, OP.mult)
                e.tensor_scalar(tmp, tmp, -0.5, 1.5, OP.mult, OP.add)
                e.tensor_tensor(yy, yy, tmp, OP.mult)
            e.tensor_tensor(mrs, mu, yy, OP.mult)
            e.tensor_copy(lnr_bf[0:1, 0:512], yy)
            e.tensor_tensor(t2, yy, lnr_bf[0:1, 0:512], OP.subtract)
            e.tensor_copy(lnr_bf[0:1, 512:1024], t2)
            e.tensor_copy(lnr_bf[0:1, 1024:1536], mrs)
            e.tensor_tensor(t2, mrs, lnr_bf[0:1, 1024:1536], OP.subtract)
            return e.tensor_copy(lnr_bf[0:1, 1536:2048], t2)
        p.step("dve", chain)
        p.wait("pe", "dve")

        def bc(e):
            e.matmul(ps_W1[:, :], ones_row[:, :], lnr_bf[0:1, 0:512], start=True, stop=False)
            e.matmul(ps_W1[:, :], ones_row[:, :], lnr_bf[0:1, 512:1024], start=False, stop=True)
            e.matmul(ps_W2[:, :], ones_row[:, :], lnr_bf[0:1, 1024:1536], start=True, stop=False)
            return e.matmul(ps_W2[:, :], ones_row[:, :], lnr_bf[0:1, 1536:2048],
                            start=False, stop=True)
        p.step("pe", bc)
        p.wait("dve", "pe")

        def bev(e):
            e.tensor_copy(brstd[:, :], ps_W1[:, :])
            return e.tensor_copy(bmrs[:, :], ps_W2[:, :])
        p.step("dve", bev)
        p.wait("pool", "dve")
        for c in range(8):
            cl = slice(c * 512, (c + 1) * 512)
            eng = "dve" if c % 2 == 0 else "pool"
            tb = t1d if eng == "dve" else t1p

            def ap_fn(e, cl=cl, c=c, tb=tb):
                e.tensor_tensor(tb[:, :], src[:, cl], brstd[:, :], OP.mult)
                if gamma:
                    e.tensor_tensor(tb[:, :], tb[:, :], bmrs[:, :], OP.subtract)
                    return e.tensor_scalar(out_bf[:, cl], tb[:, :],
                                           gam_sb[:, c:c + 1], bet_sb[:, c:c + 1],
                                           OP.mult, OP.add)
                dst = out_f32 if out_f32 is not None else out_bf
                return e.tensor_tensor(dst[:, cl], tb[:, :], bmrs[:, :], OP.subtract)
            p.step(eng, ap_fn)

    p.wait("pe", "gpc", 1)

    def s_dummy(e):
        e.matmul(ps_S[:, 0:512], ones_row[0:1, 0:64], zrow_bf[0:1, 0:512],
                 start=True, stop=True)
        return e.matmul(ps_S[:, 512:1024], ones_row[0:1, 0:64], zrow_bf[0:1, 0:512],
                        start=True, stop=True)
    p.step("pe", s_dummy)

    for m in range(NMAC):
        mc = slice(m * MT, (m + 1) * MT)
        p.wait("gp", "pe")
        h_xt = p.dma("gp", "dgp", xt_bf[:, :].rearrange("p (f t) -> p f t", f=8),
                     xT.rearrange("(f p) t -> p f t", p=128)[:, :, mc])
        p.wait("sp", "dve")
        p.wait("sp", "pool")
        h_tab = [p.dma("sp", "dsp", cos_m[:, :], cosd[:, mc]),
                 p.dma("sp", "dsp", sin_m[:, :], ssin[:, mc])]

        p.wdma("pe", h_xt)
        if m == 0:
            p.wdma("pe", *h_w)
            p.wait("pe", "gpc", 1)
        for oc in range(8):
            pst = ps_W1 if oc % 2 == 0 else ps_W2

            def kproj(e, oc=oc, pst=pst):
                ins = None
                for f in range(8):
                    ins = e.matmul(pst[:, :],
                                   W1[:, f * 1024 + oc * 128: f * 1024 + (oc + 1) * 128],
                                   xt_bf[:, f * 512:(f + 1) * 512],
                                   start=(f == 0), stop=(f == 7))
                return ins
            if oc >= 2:
                p.wait("pe", "act")
            p.step("pe", kproj)
            p.wait("act", "pe")

            def ev(e, oc=oc, pst=pst):
                return e.activation(FA[:, oc * 512:(oc + 1) * 512], pst[:, :], AF.Silu)
            p.step("act", ev)

        p.wait("dve", "act")
        p.wdma("dve", *h_tab)
        p.wait("pool", "act")
        p.wdma("pool", *h_tab)
        if m > 0:
            p.wdma("dve", *h_spill)
            p.wdma("pool", *h_spill)
        for c in range(8):
            cols = slice(c * 512, (c + 1) * 512)
            if c < 5:
                emit_rotary("dve", cols, FA, krot, swd, t1d, t2d)
            else:
                emit_rotary("pool", cols, FA, krot, swp, t1p, oscr_b)

        p.wait("pe", "dve")
        p.wait("pe", "pool")
        for s in range(4):
            for c in range(8):
                def tp(e, s=s, c=c):
                    return e.transpose(ps_T[:, :],
                                       krot[:, c * 512 + s * 128: c * 512 + (s + 1) * 128],
                                       ident[:, :])
                p.step("pe", tp)
                p.wait("dve", "pe")

                def tpe(e, s=s, c=c):
                    return e.tensor_copy(
                        ktm[:, s * 1024 + c * 128: s * 1024 + (c + 1) * 128], ps_T[:, :])
                v = p.step("dve", tpe)
                p.wait("pe", "dve", v)

        p.wait("sp", "dve")
        p.wait("sp", "pool")
        h_spill = [p.dma("sp", "dsp", kfm_sp[:, m * 4096:(m + 1) * 4096], krot[:, :])]
        p.wait("gp", "dve")
        p.wait("gp", "pe")
        p.wait("gp", "pool")
        h_spill.append(p.dma("gp", "dgp", ktm_sp[:, m * 4096:(m + 1) * 4096],
                             ktm[:, :]))

        for s in range(4):
            def vproj(e, s=s):
                ins = None
                for f in range(8):
                    for hf in range(2):
                        ins = e.matmul(ps_V[:, hf * 512:(hf + 1) * 512],
                                       xt_bf[:, f * 512 + s * 128: f * 512 + (s + 1) * 128],
                                       W2[:, f * 1024 + hf * 512: f * 1024 + (hf + 1) * 512],
                                       start=(f == 0), stop=(f == 7))
                return ins
            if s >= 1 or m >= 1:
                p.wait("pe", "dve")
            p.step("pe", vproj)
            p.wait("dve", "pe")

            def vev(e, s=s):
                return e.tensor_copy(vt[:, s * 1024:(s + 1) * 1024], ps_V[:, :])
            p.step("dve", vev)

        p.wait("sp", "dve")
        h_spill.append(p.dma("sp", "dsp", v_sp[:, m * 4096:(m + 1) * 4096], vt[:, :]))
        if "vsp" in dbg:
            p.wait("gp", "dve")
            p.dma("gp", "dgp", dbg["vsp"][:, m * 4096:(m + 1) * 4096], vt[:, :])
        if "kfm" in dbg:
            p.wait("gp", "dve")
            p.wait("gp", "pool")
            p.dma("gp", "dgp", dbg["kfm"][:, m * 4096:(m + 1) * 4096], krot[:, :])
        if "ktm" in dbg:
            p.wait("gp", "dve")
            p.dma("gp", "dgp", dbg["ktm"][:, m * 4096:(m + 1) * 4096], ktm[:, :])

        p.wait("pe", "dve")

        def dsacc(e, m=m):
            ins = None
            for s in range(4):
                for h in range(HCNT):
                    ins = e.matmul(ps_S[:, h * 64:(h + 1) * 64],
                                   ktm[:, s * 1024 + h * 64: s * 1024 + (h + 1) * 64],
                                   vt[:, s * 1024 + h * 64: s * 1024 + (h + 1) * 64],
                                   start=False, stop=(m == 3 and s == 3))
            return ins
        p.step("pe", dsacc)

        wd_flat = wdT.rearrange("a b -> (a b)").rearrange("(c p t) -> c p t",
                                                          p=128, t=512)
        for ci in range(16):
            g = m * 16 + ci
            emit_wmean_job(wd_flat[g, :, :], g)

    emit_wscale(0, INNER * E, 0, 1)

    p.wait("dve", "pe")

    def sa_ev(e):
        return e.tensor_copy(sin_f, ps_S[:, :])
    sa_v = p.step("dve", sa_ev)
    if "sat" in dbg:
        p.wait("sp", "dve", sa_v)
        p.dma("sp", "dsp", dbg["sat"][:, :], sin_f)
    p.wait("gp", "dve", sa_v)
    cc_in_sent = p.dma("gp", "dgp", cc_in, sin_f)
    p.wdma("gp", cc_in_sent)

    def coll(g):
        g.collective_compute(
            "AllGather", OP.bypass, replica_groups=[list(range(8))],
            ins=[cc_in.opt()], outs=[cc_out.opt()]).then_inc(p.sems["cc"], 1)
    p.plain("gp", coll)
    p.cnt["cc"] += 1
    p.wait("gp", "cc", 1)

    if "ccg" in dbg:
        p.dma("gp", "dgp", dbg["ccg"], cc_out)

    def mk_seldg(e):
        ins = None
        for rr in range(8):
            ins = e.tensor_scalar(seldg[:, rr * 64:(rr + 1) * 64],
                                  ident[0:64, 0:64], selc_sb[:, rr:rr + 1], None,
                                  OP.mult)
        return ins
    p.step("dve", mk_seldg)

    for rr in range(8):
        p.wait("gp", "dve")
        h_blk = p.dma("gp", "dgp", cc_sb, cc_out[rr * 64:(rr + 1) * 64, :])
        p.wdma("dve", h_blk)
        p.wait("dve", "pe")

        def blkcast(e):
            return e.tensor_copy(cc_bf[:, :], cc_sb)
        bc_v = p.step("dve", blkcast)
        p.wait("pe", "dve", bc_v)

        def sacc(e, rr=rr):
            e.matmul(ps_S[:, 0:512], seldg[:, rr * 64:(rr + 1) * 64],
                     cc_bf[:, 0:512], start=(rr == 0), stop=(rr == 7))
            return e.matmul(ps_S[:, 512:1024], seldg[:, rr * 64:(rr + 1) * 64],
                            cc_bf[:, 512:1024], start=(rr == 0), stop=(rr == 7))
        p.step("pe", sacc)

    if "sdg" in dbg:
        p.wait("gp", "dve")
        p.dma("gp", "dgp", dbg["sdg"][:, :], seldg[:, :])
        p.dma("gp", "dgp", dbg["ccbf"][:, :], cc_bf[:, :])
    if "sa" in dbg:
        p.wait("dve", "pe")

        def sadump(e):
            return e.tensor_copy(sin_f, ps_S[:, :])
        sd_v = p.step("dve", sadump)
        p.wait("pe", "dve", sd_v)
        p.wait("sp", "dve", sd_v)
        p.dma("sp", "dsp", dbg["sa"][:, :], sin_f)
        p.wdma_all("pe", "dsp")

    if STAGE < 2:
        for c in range(8):
            p.dma("sp", "dsp", y[c * 128:(c + 1) * 128, :], FA[:, 0:2048])
        p.wdma_all("sp", "dsp")
        p.wdma_all("gp", "dgp")
        _emit_all(nc, p)
        aps.close()
        abps.close()
        ab.close()
        octx.close()
        return nc

    aps.close()
    bps = ExitStack()
    ps_O1 = bps.enter_context(nc.psum_tensor("ps_O1", [64, 1024], F32))
    ps_O2 = bps.enter_context(nc.psum_tensor("ps_O2", [64, 1024], F32))

    p.wait("gp", "pe")
    h_wb = [p.dma("gp", "dgp", W1[:, :].rearrange("p (f o) -> p f o", f=8),
                  Wq.rearrange("(f p) o -> p f o", p=128)),
            p.dma("gp", "dgp", W2[:, :].rearrange("p (f o) -> p f o", f=8),
                  Wo.rearrange("(f p) o -> p f o", p=128))]


    wmean_jobs = []
    wg_flat = wgT.rearrange("a b -> (a b)").rearrange("(c p t) -> c p t",
                                                     p=128, t=512)
    for ci in range(64):
        wmean_jobs.append((wg_flat[ci, :, :], 64 + ci))
    wu_flat = wuSL.rearrange("a b c -> (a b c)").rearrange("(c p t) -> c p t",
                                                           p=128, t=512)
    for ci in range(64):
        wmean_jobs.append((wu_flat[ci, :, :], 128 + ci))
    wj = [0]
    tj = [0]

    for m in range(NMAC):
        mc = slice(m * MT, (m + 1) * MT)
        p.wait("gp", "pe")
        h_xt = p.dma("gp", "dgp", xt_bf[:, :].rearrange("p (f t) -> p f t", f=8),
                     xT.rearrange("(f p) t -> p f t", p=128)[:, :, mc])
        p.wait("sp", "dve")
        p.wait("sp", "pool")
        p.wait("sp", "pe")
        h_kv = [p.dma("sp", "dsp", kfmB[:, :], kfm_sp[:, m * 4096:(m + 1) * 4096]),
                p.dma("sp", "dsp", ktm[:, :], ktm_sp[:, m * 4096:(m + 1) * 4096])]
        p.wait("gp", "dve")
        p.wait("gp", "pe")
        h_kv.append(p.dma("gp", "dgp", vt[:, :], v_sp[:, m * 4096:(m + 1) * 4096]))
        h_tab = [p.dma("sp", "dsp", cos_m[:, :], cosd[:, mc]),
                 p.dma("sp", "dsp", sin_m[:, :], ssin[:, mc])]
        p.wait("dve", "pool")
        p.wait("pool", "dve")
        p.wait("act", "dve")
        p.wait("act", "pool")
        p.wdma("dve", *h_kv)

        p.wdma("pe", h_xt)
        if m == 0:
            p.wdma("pe", *h_wb)
        for oc in range(8):
            pst = ps_W1 if oc % 2 == 0 else ps_W2

            def qproj(e, oc=oc, pst=pst):
                ins = None
                for f in range(8):
                    ins = e.matmul(pst[:, :],
                                   W1[:, f * 1024 + oc * 128: f * 1024 + (oc + 1) * 128],
                                   xt_bf[:, f * 512:(f + 1) * 512],
                                   start=(f == 0), stop=(f == 7))
                return ins
            if oc >= 2:
                p.wait("pe", "act")
            p.step("pe", qproj)
            p.wait("act", "pe")

            def ev(e, oc=oc, pst=pst):
                return e.activation(FA[:, oc * 512:(oc + 1) * 512], pst[:, :], AF.Silu)
            p.step("act", ev)

        if BSTOP < 2:
            continue
        p.wait("dve", "act")
        p.wdma("dve", *h_tab)
        p.wait("pool", "act")
        p.wdma("pool", *h_tab)
        for c in range(8):
            cols = slice(c * 512, (c + 1) * 512)
            if c < 5:
                emit_rotary("dve", cols, FA, krot, swd, t1d, t2d)
            else:
                emit_rotary("pool", cols, FA, krot, swp, t1p, oscr_b)
        p.wait("dve", "pool")

        def oddcp(e):
            e.tensor_copy(q_oddM[0:64, :], krot[64:128, :])
            return e.tensor_copy(k_oddM[0:64, :], kfmB[64:128, :])
        ocv = p.step("dve", oddcp)
        if m == 0 and "qod" in dbg:
            p.wait("gp", "dve", ocv)
            p.dma("gp", "dgp", dbg["qod"], q_oddM[0:64, :])
            p.dma("gp", "dgp", dbg["kod"], k_oddM[0:64, :])
            if "qrt" in dbg:
                p.dma("gp", "dgp", dbg["qrt"], krot[:, :])
            if "qsl" in dbg:
                p.dma("gp", "dgp", dbg["qsl"], FA[:, 0:4096])

        if BSTOP < 3:
            continue
        def g1(e):
            ins = None
            for f in range(8):
                ins = e.matmul(ps_O1[:, 0:512], Wg1_sb[:, f * 64:(f + 1) * 64],
                               xt_bf[:, f * 512:(f + 1) * 512],
                               start=(f == 0), stop=(f == 7))
            return ins
        p.wait("pe", "dve")
        p.step("pe", g1)
        p.wait("dve", "pe")

        def g1e(e):
            return e.tensor_copy(xg1_bf[:, :], ps_O1[:, 0:512])
        g1v = p.step("dve", g1e)
        p.wait("pe", "dve", g1v)
        for oc in range(8):
            pst = ps_W1 if oc % 2 == 0 else ps_W2

            def g2(e, oc=oc, pst=pst):
                return e.matmul(pst[:, :], Wg2_sb[:, oc * 128:(oc + 1) * 128],
                                xg1_bf[:, :], start=True, stop=True)
            if oc >= 2:
                p.wait("pe", "dve")
            p.step("pe", g2)
            p.wait("act", "pe")
            p.wait("act", "pool")

            def gt(e, oc=oc, pst=pst):
                return e.activation(jA[:, 0:512], pst[:, :], AF.Tanh, scale=0.5)
            p.step("act", gt)
            p.wait("dve", "act")

            def gs(e, oc=oc):
                return e.tensor_scalar(FB[:, oc * 512:(oc + 1) * 512], jA[:, 0:512],
                                       0.5, 0.5, OP.mult, OP.add)
            v = p.step("dve", gs)
            p.wait("act", "dve", v)

        if BSTOP < 4:
            continue
        p.wdma("pe", *h_kv)
        for s in range(4):
            g = m * 4 + s
            p.wait("dve", "pe")

            def sbf(e):
                return e.tensor_copy(S_bf[:, :], ps_S[:, :])
            sv = p.step("dve", sbf)
            p.wait("pe", "dve", sv)

            if ASTOP < 2:
                continue
            for rnd in range(2):
                def emm(e, s=s, rnd=rnd):
                    ins = None
                    for hh in range(8):
                        h = rnd * 8 + hh
                        c = h // 2
                        pst = ps_W1 if hh < 4 else ps_W2
                        col = (hh % 4) * 128
                        cols = slice(c * 512 + s * 128, c * 512 + (s + 1) * 128)
                        kl = (kfmB[0:64, cols] if h % 2 == 0
                              else k_oddM[0:64, cols])
                        ql = (krot[0:64, cols] if h % 2 == 0
                              else q_oddM[0:64, cols])
                        ins = e.matmul(pst[:, col:col + 128], kl, ql,
                                       start=True, stop=True)
                    return ins
                p.step("pe", emm)
                p.wait("dve", "pe")

                def eev(e, rnd=rnd):
                    e.tensor_tensor(eT_sb[:, rnd * 1024: rnd * 1024 + 512],
                                    ps_W1[:, :], mask4[:, :], OP.mult)
                    return e.tensor_tensor(eT_sb[:, rnd * 1024 + 512: rnd * 1024 + 1024],
                                           ps_W2[:, :], mask4[:, :], OP.mult)
                ev_v = p.step("dve", eev)
                p.wait("pe", "dve", ev_v)

            if ASTOP < 3:
                continue
            def avq(e, s=s):
                ins = None
                for h in range(HCNT):
                    c = h // 2
                    pso = ps_O1 if h % 2 == 0 else ps_O2
                    ocol = c * 128
                    e.matmul(pso[:, ocol:ocol + 128],
                             vt[:, s * 1024 + h * 64: s * 1024 + (h + 1) * 64],
                             eT_sb[:, h * 128:(h + 1) * 128],
                             start=True, stop=False)
                    qsl = (krot[0:64, c * 512 + s * 128: c * 512 + (s + 1) * 128]
                           if h % 2 == 0 else
                           q_oddM[0:64, c * 512 + s * 128: c * 512 + (s + 1) * 128])
                    ins = e.matmul(pso[:, ocol:ocol + 128],
                                   S_bf[:, h * 64:(h + 1) * 64], qsl,
                                   start=False, stop=True)
                return ins
            p.step("pe", avq)

            if ASTOP < 4:
                continue
            def dsu(e, s=s, g=g):
                ins = None
                for h in range(HCNT):
                    ins = e.matmul(ps_S[:, h * 64:(h + 1) * 64],
                                   ktm[:, s * 1024 + h * 64: s * 1024 + (h + 1) * 64],
                                   vt[:, s * 1024 + h * 64: s * 1024 + (h + 1) * 64],
                                   start=False, stop=(g == 15))
                return ins
            p.step("pe", dsu)

            if ASTOP < 5:
                continue
            p.wait("dve", "pe")

            def oge(e, s=s):
                ins = None
                for par, pso in ((0, ps_O1), (1, ps_O2)):
                    ov = FC[par * 64:(par + 1) * 64, :].rearrange(
                        "p (c t) -> p c t", c=8)[:, :, s * 128:(s + 1) * 128]
                    iv = pso[:, :].rearrange("p (c t) -> p c t", c=8)
                    gv = FB[par * 64:(par + 1) * 64, :].rearrange(
                        "p (c t) -> p c t", c=8)[:, :, s * 128:(s + 1) * 128]
                    ins = e.tensor_tensor(ov, iv, gv, OP.mult)
                return ins
            ogv = p.step("dve", oge)
            p.wait("pe", "dve", ogv)

        if BSTOP >= 8:
            for _ in range(32):
                if wj[0] < len(wmean_jobs):
                    srcj, col = wmean_jobs[wj[0]]
                    emit_wmean_job(srcj, col)
                    wj[0] += 1
            for _ in range(16):
                if tj[0] < 64:
                    emit_wtern_job(tj[0])
                    tj[0] += 1

        if "og" in dbg:
            p.wait("sp", "dve")
            p.dma("sp", "dsp", dbg["og"][:, m * 4096:(m + 1) * 4096], FC[:, :])
        if "gate" in dbg:
            p.wait("sp", "dve")
            p.dma("sp", "dsp", dbg["gate"][:, m * 4096:(m + 1) * 4096], FB[:, :])
        if BSTOP < 5:
            continue
        emit_ln(FC, True, out_bf=xn_bf)

        if BSTOP < 6:
            continue
        p.wait("pe", "dve")
        p.wait("pe", "pool")
        for oc in range(8):
            pst = ps_W1 if oc % 2 == 0 else ps_W2
            xslot = (t1d if oc % 2 == 0 else t2d)[:, :]
            p.wait("sp", "dve")
            h_x = p.dma("sp", "dsp", xslot,
                        xT[oc * 128:(oc + 1) * 128, mc])

            def wop(e, oc=oc, pst=pst):
                ins = None
                for f in range(8):
                    ins = e.matmul(pst[:, :],
                                   W2[:, f * 1024 + oc * 128: f * 1024 + (oc + 1) * 128],
                                   xn_bf[:, f * 512:(f + 1) * 512],
                                   start=(f == 0), stop=(f == 7))
                return ins
            if oc >= 2:
                p.wait("pe", "dve")
            p.step("pe", wop)
            p.wait("dve", "pe")
            p.wdma("dve", h_x)

            def r1f(e, oc=oc, pst=pst, xslot=xslot):
                return e.tensor_tensor(FA[:, oc * 512:(oc + 1) * 512], pst[:, :],
                                       xslot, OP.add)
            p.step("dve", r1f)

        if BSTOP < 7:
            continue
        emit_ln(FA, False, out_f32=FB)

        p.wait("sp", "dve")
        p.wait("sp", "pool")
        p.dma("sp", "dsp", xmid_sp[:, m * 4096:(m + 1) * 4096], FB[:, :])
        if "xmid" in dbg:
            p.dma("sp", "dsp", dbg["xmid"][:, m * 4096:(m + 1) * 4096], FB[:, :])
        if "r1" in dbg:
            p.dma("sp", "dsp", dbg["r1"][:, m * 4096:(m + 1) * 4096], FA[:, :])
        if "xn" in dbg:
            p.wait("gp", "dve")
            p.wait("gp", "pool")
            p.dma("gp", "dgp", dbg["xn"][:, m * 4096:(m + 1) * 4096], xn_bf[:, :])

    if BSTOP >= 8:
        while wj[0] < len(wmean_jobs):
            srcj, col = wmean_jobs[wj[0]]
            emit_wmean_job(srcj, col)
            wj[0] += 1
        while tj[0] < 64:
            emit_wtern_job(tj[0])
            tj[0] += 1

        emit_wscale(64, INNER * E, 2, 4)
        emit_wscale(128, INNER * E, 3, 5)
        if "wsc" in dbg:
            p.wait("sp", "dve")
            p.dma("sp", "dsp", dbg["wsc"], wsc[:, 0:8])
            p.dma("sp", "dsp", dbg["tinyd"], tiny[0:1, :])
            p.dma("sp", "dsp", dbg["waccd"], wacc[:, :])
            if "spdbg" in dbg:
                p.dma("sp", "dsp", dbg["spdbg"], t2d[0:32, 0:16])
                p.wait("gp", "dve")
                p.dma("gp", "dgp", dbg["wabdbg"], wab_bf[0:32, 0:4])
                p.dma("sp", "dsp", dbg["t1dbg"], t1d[:, 0:64])

    if "sa" in dbg:
        p.wait("sp", "dve")
        p.dma("sp", "dsp", dbg["sa"][:, :], sin_f)

    if STAGE < 3:
        for c in range(8):
            p.dma("sp", "dsp", y[c * 128:(c + 1) * 128, :], FA[:, 0:2048])
        p.wdma_all("sp", "dsp")
        p.wdma_all("gp", "dgp")
        _emit_all(nc, p)
        bps.close()
        abps.close()
        ab.close()
        octx.close()
        return nc

    fin = {e: p.cnt[e] for e in ("pe", "act", "dve", "pool")}
    for e in ("pe", "act", "dve", "pool", "sp", "gp"):
        for s in ("pe", "act", "dve", "pool"):
            if s != e:
                p.wait(e, s, fin[s])
        p.wdma_all(e, "dsp")
        p.wdma_all(e, "dgp")

    bps.close()
    abps.close()
    ab.close()

    cs = ExitStack()

    def csb(name, shape, dt=F32):
        return cs.enter_context(nc.sbuf_tensor(name, shape, dt))

    wg_sb = csb("wg_sb", [128, 8 * INNER], BF16)
    junkc = csb("junkc", [128, 1024])
    junk2 = csb("junk2", [128, 1024])
    wstr = csb("wstr", [128, 2048])
    xmid_c = csb("xmid_c", [128, 2048])
    hbuf = csb("hbuf", [128, 8192])
    hq = csb("hq", [128, 8192], BF16)
    xq_deq = csb("xq_deq", [128, 2048], BF16)
    xq_tmp = csb("xq_tmp", [128, 256], BF16)
    gsil = csb("gsil", [128, 256])
    wd_buf = csb("wd_buf", [128, 2048], BF16)
    wu_buf = csb("wu_buf", [128, 2048], BF16)
    rmx = csb("rmx", [128, 1024])
    rmn = csb("rmn", [128, 1024])
    fold = csb("fold", [64, 512])
    fold_bf = csb("fold_bf", [32, 512], BF16)
    tmv = csb("tmv", [128, 16])
    tmv_bf = csb("tmv_bf", [128, 8], BF16)
    rowv_bf = csb("rowv_bf", [1, 1024], BF16)
    bs1 = csb("bs1", [128, 256])
    bsi1 = csb("bsi1", [128, 256])
    bs2 = csb("bs2", [128, 256])
    bsd = csb("bsd", [128, 256])
    ytile = csb("ytile", [128, 2048])
    lnr2 = csb("lnr2", [1, 1280])
    lnr2_bf = csb("lnr2_bf", [1, 1024], BF16)
    br2 = csb("br2", [128, 256])
    bm2 = csb("bm2", [128, 256])
    tscr = csb("tscr", [128, 512])
    oscr = csb("oscr", [128, 512])

    cps = ExitStack()
    ps_GU = cps.enter_context(nc.psum_tensor("ps_GU", [128, 512], F32))
    ps_GU2 = cps.enter_context(nc.psum_tensor("ps_GU2", [128, 512], F32))
    ps_DN = cps.enter_context(nc.psum_tensor("ps_DN", [128, 2048], F32))
    ps_TP = cps.enter_context(nc.psum_tensor("ps_TP", [128, 512], BF16))

    def spacer_c(e):
        return e.tensor_scalar(junkc[:, 256:512], junkc[:, 0:256], 1.0, None,
                               OP.mult)

    cln_bf = hq[:, 0:2048]
    sq3_bf = hq[:, 2048:4096]

    wg_flatC = wgT.rearrange("a b -> (a b)").rearrange("(c p t) -> c p t",
                                                       p=128, t=1024)
    ca_hist = []
    for ci in range(32):
        buf = wstr[:, (ci % 2) * 1024:(ci % 2) * 1024 + 1024]
        jb = junkc if ci % 2 == 0 else junk2
        q, dq = ("sp", "dsp") if ci % 2 == 0 else ("gp", "dgp")
        if ci >= 2:
            p.wait(q, "dve", ca_hist[ci - 2])
        else:
            p.wait(q, "dve")
            p.wait(q, "pool")
        hb = p.dma(q, dq, buf, wg_flatC[ci, :, :])
        p.wdma("dve", hb)
        p.wait("dve", "pool")

        def ca(e, buf=buf, jb=jb):
            e.scalar_tensor_tensor(jb[:, 0:512], buf[:, 0:512], wsc[:, 2:3],
                                   mrow[:, :], OP.mult, OP.add)
            return e.scalar_tensor_tensor(jb[:, 512:1024], buf[:, 512:1024],
                                          wsc[:, 2:3], mrow[:, :], OP.mult, OP.add)
        ca_hist.append(p.step("dve", ca))
        p.wait("pool", "dve")

        def cb(e, ci=ci, jb=jb):
            e.tensor_scalar(oscr[:, 0:512], jb[:, 0:512], MRND + 1.0, MRND - 1.0,
                            OP.min, OP.max)
            e.tensor_scalar(oscr[:, 0:512], jb[:, 512:1024], MRND + 1.0, MRND - 1.0,
                            OP.min, OP.max) if False else None
            e.tensor_scalar(
                wg_sb[:, :].rearrange("p (c t) -> p c t", c=32)[:, ci, 0:512],
                oscr[:, 0:512], -MRND, None, OP.add)
            e.tensor_scalar(oscr[:, 0:512], jb[:, 512:1024], MRND + 1.0, MRND - 1.0,
                            OP.min, OP.max)
            return e.tensor_scalar(
                wg_sb[:, :].rearrange("p (c t) -> p c t", c=32)[:, ci, 512:1024],
                oscr[:, 0:512], -MRND, None, OP.add)
        p.step("pool", cb)
    for oc in range(32):
        buf = wstr[:, (oc % 2) * 1024:(oc % 2) * 1024 + 1024]
        jb = (junkc if oc % 2 == 0 else junk2)[:, 0:1024]
        tb = wu_buf[:, (oc % 2) * 1024:(oc % 2) * 1024 + 1024]
        q, dq = ("sp", "dsp") if oc % 2 == 0 else ("gp", "dgp")
        if oc >= 2:
            p.wait(q, "dve", ca_hist[32 + oc - 2])
        else:
            p.wait(q, "dve")
            p.wait(q, "pool")
        hb = p.dma(q, dq, buf,
                   wuSL[oc, :, :].rearrange("(f p) o -> p f o", p=128))
        p.wdma("dve", hb)
        p.wait("dve", "pool")

        def cua(e, buf=buf, jb=jb):
            e.scalar_tensor_tensor(jb[:, 0:512], buf[:, 0:512], wsc[:, 3:4],
                                   mrow[:, :], OP.mult, OP.add)
            return e.scalar_tensor_tensor(jb[:, 512:1024], buf[:, 512:1024],
                                          wsc[:, 3:4], mrow[:, :], OP.mult, OP.add)
        ca_hist.append(p.step("dve", cua))
        p.wait("pool", "dve")

        def cub(e, jb=jb, tb=tb):
            e.tensor_scalar(oscr[:, 0:512], jb[:, 0:512], MRND + 1.0, MRND - 1.0,
                            OP.min, OP.max)
            e.tensor_scalar(tb[:, 0:512], oscr[:, 0:512], -MRND, None, OP.add)
            e.tensor_scalar(oscr[:, 0:512], jb[:, 512:1024], MRND + 1.0, MRND - 1.0,
                            OP.min, OP.max)
            return e.tensor_scalar(tb[:, 512:1024], oscr[:, 0:512], -MRND, None,
                                   OP.add)
        v = p.step("pool", cub)
        p.wait("sp", "pool", v)
        p.dma("sp", "dsp", wu_t[oc, :, :], tb)

    def emit_scales(chunks, s_dst, inv_dst):
        n = len(chunks)
        p.wait("dve", "pool")

        def mm0(e):
            e.tensor_copy(rmx[:, 0:256], chunks[0])
            return e.tensor_copy(rmn[:, 0:256], chunks[0])
        p.step("dve", mm0)
        cur = 0
        for i in range(1, n):
            nxt = (cur + 1) % 2

            def mm(e, i=i, cur=cur, nxt=nxt):
                e.tensor_tensor(rmx[:, nxt * 256:(nxt + 1) * 256],
                                rmx[:, cur * 256:(cur + 1) * 256],
                                chunks[i], OP.max)
                return e.tensor_tensor(rmn[:, nxt * 256:(nxt + 1) * 256],
                                       rmn[:, cur * 256:(cur + 1) * 256],
                                       chunks[i], OP.min)
            p.step("dve", mm)
            cur = nxt

        def comb(e, cur=cur):
            e.scalar_tensor_tensor(rmx[:, 512:768], rmn[:, cur * 256:(cur + 1) * 256],
                                   -1.0, rmx[:, cur * 256:(cur + 1) * 256],
                                   OP.mult, OP.max)
            e.tensor_copy(fold[0:64, 0:256], rmx[64:128, 512:768])
            e.tensor_tensor(fold[0:64, 256:512], rmx[0:64, 512:768],
                            fold[0:64, 0:256], OP.max)
            e.tensor_copy(rmx[0:32, 768:1024], fold[32:64, 256:512])
            e.tensor_tensor(fold[0:32, 0:256], fold[0:32, 256:512],
                            rmx[0:32, 768:1024], OP.max)
            e.tensor_copy(fold_bf[:, 0:256], fold[0:32, 0:256])
            e.tensor_tensor(fold[0:32, 256:512], fold[0:32, 0:256],
                            fold_bf[:, 0:256], OP.subtract)
            return e.tensor_copy(fold_bf[:, 256:512], fold[0:32, 256:512])
        p.step("dve", comb)
        p.wait("pe", "dve")

        def tp1(e):
            e.transpose(ps_TP[:, 0:32], fold_bf[:, 0:128], ident[0:32, 0:32])
            e.transpose(ps_TP[:, 64:96], fold_bf[:, 256:384], ident[0:32, 0:32])
            e.transpose(ps_TP[:, 32:64], fold_bf[:, 128:256], ident[0:32, 0:32])
            return e.transpose(ps_TP[:, 96:128], fold_bf[:, 384:512],
                               ident[0:32, 0:32])
        p.step("pe", tp1)
        p.wait("dve", "pe")

        def tp1e(e):
            e.tensor_copy(tscr[:, 0:32], ps_TP[:, 0:32])
            e.tensor_copy(tscr[:, 64:96], ps_TP[:, 64:96])
            e.tensor_copy(tscr[:, 32:64], ps_TP[:, 32:64])
            e.tensor_copy(tscr[:, 96:128], ps_TP[:, 96:128])
            spacer_c(e)
            e.tensor_tensor(tscr[:, 128:192], tscr[:, 0:64], tscr[:, 64:128],
                            OP.add)
            spacer_c(e)
            e.reduce_max(tmv[:, 0:1], tscr[:, 128:160], axis=AX)
            e.reduce_max(tmv[:, 1:2], tscr[:, 160:192], axis=AX)
            spacer_c(e)
            e.tensor_copy(tmv_bf[:, 0:2], tmv[:, 0:2])
            spacer_c(e)
            e.scalar_tensor_tensor(tmv[:, 2:4], tmv_bf[:, 0:2], -1.0,
                                   tmv[:, 0:2], OP.mult, OP.add)
            spacer_c(e)
            return e.tensor_copy(tmv_bf[:, 2:4], tmv[:, 2:4])
        p.step("dve", tp1e)
        p.wait("pe", "dve")

        def tpb(e):
            e.transpose(ps_TP[0:1, 0:128], tmv_bf[:, 0:1], ident[:, :])
            e.transpose(ps_TP[0:1, 128:256], tmv_bf[:, 1:2], ident[:, :])
            e.transpose(ps_TP[0:1, 256:384], tmv_bf[:, 2:3], ident[:, :])
            return e.transpose(ps_TP[0:1, 384:512], tmv_bf[:, 3:4], ident[:, :])
        p.step("pe", tpb)
        p.wait("dve", "pe")

        def chain(e):
            S = [lnr2[0:1, i * 256:(i + 1) * 256] for i in range(5)]
            e.tensor_copy(S[0], ps_TP[0:1, 256:512])
            e.tensor_tensor(S[1], ps_TP[0:1, 0:256], S[0], OP.add)
            e.tensor_scalar(S[2], S[1], 1e-5, None, OP.max)
            e.reciprocal(S[3], S[2])
            e.tensor_scalar(S[4], S[3], 127.0, None, OP.mult)
            e.tensor_scalar(S[1], S[2], 1.0 / 127.0, None, OP.mult)
            e.tensor_copy(rowv_bf[0:1, 0:256], S[4])
            e.tensor_copy(rowv_bf[0:1, 512:768], S[1])
            e.scalar_tensor_tensor(S[0], rowv_bf[0:1, 0:256], -1.0, S[4],
                                   OP.mult, OP.add)
            return e.tensor_copy(rowv_bf[0:1, 256:512], S[0])
        p.step("dve", chain)
        p.wait("pe", "dve")

        def bcst(e):
            e.matmul(ps_GU[:, 0:256], ones_row[:, :], rowv_bf[0:1, 0:256],
                     start=True, stop=False)
            e.matmul(ps_GU[:, 0:256], ones_row[:, :], rowv_bf[0:1, 256:512],
                     start=False, stop=True)
            return e.matmul(ps_GU2[:, 0:256], ones_row[:, :], rowv_bf[0:1, 512:768],
                            start=True, stop=True)
        p.step("pe", bcst)
        p.wait("dve", "pe")

        def bcev(e):
            e.tensor_copy(s_dst[:, :], ps_GU[:, 0:256])
            return e.tensor_copy(inv_dst[:, :], ps_GU2[:, 0:256])
        v = p.step("dve", bcev)
        p.wait("pe", "dve", v)

    for u in range(NSUB):
        m, hh = u // 2, u % 2
        p.wait("dve", "pool")
        p.wait("pool", "dve")
        p.wait("pool", "pe")
        p.wait("act", "dve")
        xq_, dq_ = ("sp", "dsp") if u % 2 == 0 else ("gp", "dgp")
        p.wait(xq_, "dve")
        p.wait(xq_, "pool")
        if u == 0:
            p.wdma_all("sp", "dsp")
        h_xm = p.dma(xq_, dq_, xmid_c[:, :].rearrange("p (c t) -> p c t", c=8),
                     xmid_sp[:, :].rearrange("p (mm c t) -> p mm c t", mm=4, c=8)
                     [:, m, :, hh * 256:(hh + 1) * 256])
        p.wdma("dve", h_xm)
        p.wdma("pool", h_xm)

        xch = [xmid_c[:, c * 256:(c + 1) * 256] for c in range(8)]
        emit_scales(xch, bs1, bsi1)
        if u == DUMP_U and "bs1" in dbg:
            p.wait("sp", "dve")
            p.dma("sp", "dsp", dbg["bs1"], bs1[:, :])
            p.dma("sp", "dsp", dbg["tmv"], tmv[:, :])
            p.dma("sp", "dsp", dbg["habs"], rmx[:, 512:768])
            p.wait("gp", "dve")
            p.dma("gp", "dgp", dbg["bsi1"], bsi1[:, :])
            if "rowv" in dbg:
                p.dma("gp", "dgp", dbg["rowv"], rowv_bf[:, :])
                p.dma("gp", "dgp", dbg["tmvbf"], tmv_bf[:, :])

        for c in range(8):
            def xq1(e, c=c):
                e.tensor_tensor(tscr[:, 0:256], xch[c], bs1[:, :], OP.mult)
                e.tensor_scalar(tscr[:, 256:512], tscr[:, 0:256], MRND, MRND + 127.0,
                                OP.add, OP.min)
                e.tensor_scalar(xq_tmp[:, :], tscr[:, 256:512], MRND - 128.0, -MRND,
                                OP.max, OP.add)
                return e.tensor_tensor(xq_deq[:, c * 256:(c + 1) * 256], xq_tmp[:, :],
                                       bsi1[:, :], OP.mult)
            p.step("dve", xq1)

        p.wait("pe", "dve")
        if u == 0:
            p.wait("pe", "pool")
            p.wdma_all("gp", "dsp")
        gu_hist = []
        for oc in range(32):
            slab = wu_buf[:, (oc % 2) * 1024:(oc % 2) * 1024 + 1024]
            q, dq = ("sp", "dsp") if oc % 2 == 0 else ("gp", "dgp")
            if oc >= 2:
                p.wait(q, "pe", gu_hist[oc - 2])
            else:
                p.wait(q, "pe")
            h_sl = p.dma(q, dq, slab, wu_t[oc, :, :])
            pst = ps_GU if oc % 2 == 0 else ps_GU2
            p.wdma("pe", h_sl)
            if oc >= 2:
                p.wait("pe", "dve")

            col = (oc % 2) * 256

            def gu(e, oc=oc, col=col, slab=slab):
                ins = None
                for f in range(8):
                    e.matmul(ps_GU[:, col:col + 256],
                             wg_sb[:, f * INNER + oc * 128: f * INNER + (oc + 1) * 128],
                             xq_deq[:, f * 256:(f + 1) * 256],
                             start=(f == 0), stop=(f == 7))
                    ins = e.matmul(ps_GU2[:, col:col + 256], slab[:, f * 128:(f + 1) * 128],
                                   xq_deq[:, f * 256:(f + 1) * 256],
                                   start=(f == 0), stop=(f == 7))
                return ins
            gu_hist.append(p.step("pe", gu))
            p.wait("act", "pe")
            p.wait("act", "dve")

            def sil(e, col=col):
                return e.activation(gsil[:, :], ps_GU[:, col:col + 256], AF.Silu,
                                    scale=wsc[:, 4:5])
            sv = p.step("act", sil)
            p.wait("dve", "act", sv)

            def hm(e, oc=oc, col=col):
                return e.scalar_tensor_tensor(hbuf[:, oc * 256:(oc + 1) * 256],
                                              ps_GU2[:, col:col + 256], wsc[:, 5:6],
                                              gsil[:, :], OP.mult, OP.mult)
            p.step("dve", hm)

        hch = [hbuf[:, c * 256:(c + 1) * 256] for c in range(32)]
        emit_scales(hch, bs2, bsd)

        p.wait("pool", "dve")
        for c in range(32):
            eng = "dve" if c % 2 == 0 else "pool"
            tb = tscr if eng == "dve" else oscr

            def hqf(e, c=c, tb=tb):
                e.tensor_tensor(tb[:, 0:256], hch[c], bs2[:, :], OP.mult)
                e.tensor_scalar(tb[:, 256:512], tb[:, 0:256], MRND, MRND + 127.0,
                                OP.add, OP.min)
                return e.tensor_scalar(hq[:, c * 256:(c + 1) * 256], tb[:, 256:512],
                                       MRND - 128.0, -MRND, OP.max, OP.add)
            p.step(eng, hqf)
        p.wait("pe", "dve")
        p.wait("pe", "pool")

        dn_hist = []
        for mi in range(32):
            slab = wd_buf[:, (mi % 2) * 1024:(mi % 2) * 1024 + 1024]
            q, dq = ("sp", "dsp") if mi % 2 == 0 else ("gp", "dgp")
            if mi >= 2:
                p.wait(q, "pe", dn_hist[mi - 2])
            else:
                p.wait(q, "pe")
            h_sl = p.dma(q, dq, slab, wd_t[mi * 128:(mi + 1) * 128, :])
            p.wdma("pe", h_sl)

            def dn(e, mi=mi, slab=slab):
                if mi == 0:
                    for b in range(4):
                        e.matmul(ps_DN[:, b * 512:(b + 1) * 512], ones_row[:, :],
                                 zrow_bf[0:1, 0:512], start=True, stop=True)
                ins = None
                for oc in range(8):
                    ins = e.matmul(ps_DN[:, oc * 256:(oc + 1) * 256],
                                   slab[:, oc * 128:(oc + 1) * 128],
                                   hq[:, mi * 256:(mi + 1) * 256],
                                   start=False, stop=(mi == 31))
                return ins
            dn_hist.append(p.step("pe", dn))

        p.wait("dve", "pe")
        for c in range(8):
            def r2f(e, c=c):
                e.scalar_tensor_tensor(tscr[:, 0:256], ps_DN[:, c * 256:(c + 1) * 256],
                                       wsc[:, 1:2], bsd[:, :], OP.mult, OP.mult)
                return e.tensor_tensor(ytile[:, c * 256:(c + 1) * 256],
                                       tscr[:, 0:256], xch[c], OP.add)
            p.step("dve", r2f)

        if u == DUMP_U:
            if "xq0" in dbg:
                p.wait("gp", "dve")
                p.dma("gp", "dgp", dbg["xq0"][:, :], xq_deq[:, :])
            if "h0" in dbg:
                p.wait("sp", "dve")
                p.dma("sp", "dsp", dbg["h0"][:, :], hbuf[:, :])
            if "hq0" in dbg:
                p.wait("gp", "dve")
                p.wait("gp", "pool")
                hqh = p.dma("gp", "dgp", dbg["hq0"][:, :], hq[:, :])
                p.wdma("dve", hqh)
            if "r20" in dbg:
                p.wait("sp", "dve")
                p.dma("sp", "dsp", dbg["r20"][:, :], ytile[:, :])
        for c in range(8):
            cl = slice(c * 256, (c + 1) * 256)

            def cp3(e, cl=cl):
                return e.tensor_copy(cln_bf[:, cl], ytile[:, cl])
            p.step("dve", cp3)
            p.wait("act", "dve")

            def sq3(e, cl=cl):
                return e.activation(sq3_bf[:, cl], ytile[:, cl], AF.Square)
            p.step("act", sq3)
        p.wait("pe", "dve")
        p.wait("pe", "act")

        def st3(e):
            ins = None
            for c in range(8):
                cl = slice(c * 256, (c + 1) * 256)
                e.matmul(ps_DN[0:1, 0:256], ones_col[:, :], cln_bf[:, cl],
                         start=(c == 0), stop=(c == 7))
                ins = e.matmul(ps_DN[0:1, 1024:1280], ones_col[:, :], sq3_bf[:, cl],
                               start=(c == 0), stop=(c == 7))
            return ins
        p.step("pe", st3)
        p.wait("dve", "pe")

        def chain3(e):
            L = lnr2
            mu, var, yy, tmp, mrs = (L[0:1, i * 256:(i + 1) * 256]
                                     for i in range(5))
            t2 = tmp
            e.tensor_scalar(mu, ps_DN[0:1, 0:256], 1.0 / E, None, OP.mult)
            e.tensor_scalar(tmp, ps_DN[0:1, 1024:1280], 1.0 / E, EPS, OP.mult, OP.add)
            e.tensor_tensor(var, mu, mu, OP.mult)
            e.tensor_tensor(var, tmp, var, OP.subtract)
            e.tensor_copy(lnr2_bf[0:1, 0:256], var)
            e.tensor_copy(tmp, lnr2_bf[0:1, 0:256].bitcast(U16))
            e.tensor_scalar(tmp, tmp, 0.5, MRND, OP.mult, OP.add)
            e.tensor_scalar(tmp, tmp, -MRND, None, OP.add)
            e.tensor_scalar(tmp, tmp, -1.0, 24375.0, OP.mult, OP.add)
            e.tensor_copy(lnr2_bf[0:1, 0:256].bitcast(U16), tmp)
            e.tensor_copy(yy, lnr2_bf[0:1, 0:256])
            for _ in range(3):
                e.tensor_tensor(tmp, yy, yy, OP.mult)
                e.tensor_tensor(tmp, tmp, var, OP.mult)
                e.tensor_scalar(tmp, tmp, -0.5, 1.5, OP.mult, OP.add)
                e.tensor_tensor(yy, yy, tmp, OP.mult)
            e.tensor_tensor(mrs, mu, yy, OP.mult)
            e.tensor_copy(lnr2_bf[0:1, 0:256], yy)
            e.tensor_tensor(t2, yy, lnr2_bf[0:1, 0:256], OP.subtract)
            e.tensor_copy(lnr2_bf[0:1, 256:512], t2)
            e.tensor_copy(lnr2_bf[0:1, 512:768], mrs)
            e.tensor_tensor(t2, mrs, lnr2_bf[0:1, 512:768], OP.subtract)
            return e.tensor_copy(lnr2_bf[0:1, 768:1024], t2)
        p.step("dve", chain3)
        p.wait("pe", "dve")

        def bc3(e):
            e.matmul(ps_DN[:, 256:512], ones_row[:, :], lnr2_bf[0:1, 0:256],
                     start=True, stop=False)
            e.matmul(ps_DN[:, 256:512], ones_row[:, :], lnr2_bf[0:1, 256:512],
                     start=False, stop=True)
            e.matmul(ps_DN[:, 1280:1536], ones_row[:, :], lnr2_bf[0:1, 512:768],
                     start=True, stop=False)
            return e.matmul(ps_DN[:, 1280:1536], ones_row[:, :], lnr2_bf[0:1, 768:1024],
                            start=False, stop=True)
        p.step("pe", bc3)
        p.wait("dve", "pe")

        def bev3(e):
            e.tensor_copy(br2[:, :], ps_DN[:, 256:512])
            return e.tensor_copy(bm2[:, :], ps_DN[:, 1280:1536])
        p.step("dve", bev3)
        p.wait("pool", "dve")

        for c in range(8):
            cl = slice(c * 256, (c + 1) * 256)
            eng = "dve" if c % 2 == 0 else "pool"
            tb = tscr if eng == "dve" else oscr

            def ap3(e, cl=cl, tb=tb):
                e.tensor_tensor(tb[:, 0:256], ytile[:, cl], br2[:, :], OP.mult)
                return e.tensor_tensor(tb[:, 256:512], tb[:, 0:256], bm2[:, :],
                                       OP.subtract)
            v = p.step(eng, ap3)
            p.wait("sp", eng, v)
            h_y = p.dma("sp", "dsp",
                        y[c * 128:(c + 1) * 128, u * 256:(u + 1) * 256], tb[:, 256:512])
            if c < 6:
                p.wdma(eng, h_y)

    p.wdma_all("sp", "dsp")
    p.wdma_all("gp", "dgp")

    _emit_all(nc, p)
    cps.close()
    cs.close()
    octx.close()
    return nc


def _emit_all(nc, p):
    with nc.Block() as block:
        @block.tensor
        def _(e):
            for _, fn in p.ops["pe"]:
                fn(e)

        @block.scalar
        def _(e):
            e.preload_act_table(AF.Silu)
            for _, fn in p.ops["act"]:
                fn(e)

        @block.vector
        def _(e):
            for _, fn in p.ops["dve"]:
                fn(e)

        @block.gpsimd
        def _(e):
            merged = sorted(p.ops["gp"] + p.ops["pool"], key=lambda t: t[0])
            for _, fn in merged:
                fn(e)

        @block.sync
        def _(e):
            for _, fn in p.ops["sp"]:
                fn(e)


def _host_inputs(full):
    x = np.asarray(full["L0_attn_x"], dtype=np.float32)
    BASE = 10000.0
    invf = BASE ** (-(np.arange(32, dtype=np.float64) * 2.0) / 64.0)
    freqs = invf[np.arange(128) % 32]
    sgn = np.where((np.arange(128) // 32) % 2 == 0, -1.0, 1.0)
    wu = np.asarray(full["L0_ffn_w_up"], np.float32)
    shared = {
        "Wq": np.ascontiguousarray(full["L0_attn_Wq"], dtype=np.float32),
        "Wk": np.ascontiguousarray(full["L0_attn_Wk"], dtype=np.float32),
        "Wv": np.ascontiguousarray(full["L0_attn_Wv"], dtype=np.float32),
        "Wo": np.ascontiguousarray(full["L0_attn_Wo"], dtype=np.float32),
        "Wg1": np.ascontiguousarray(full["L0_attn_Wg1"], dtype=np.float32),
        "Wg2": np.ascontiguousarray(full["L0_attn_Wg2"], dtype=np.float32),
        "wgT": np.ascontiguousarray(
            np.asarray(full["L0_ffn_w_gate"], np.float32).T
            .reshape(8, 128, INNER).transpose(0, 2, 1)
            .reshape(32, 1024, 128).transpose(0, 2, 1).reshape(E, INNER)),
        "wuSL": np.ascontiguousarray(
            wu.T.reshape(E, 32, 128).transpose(1, 0, 2)),
        "wdT": np.ascontiguousarray(np.asarray(full["L0_ffn_w_down"], np.float32).T),
        "gam": np.ascontiguousarray(
            np.asarray(full["L0_attn_gamma"], np.float32).reshape(8, 128).T),
        "bet": np.ascontiguousarray(
            np.asarray(full["L0_attn_beta"], np.float32).reshape(8, 128).T),
    }
    in_maps = []
    for c in range(8):
        b, j = c // 4, c % 4
        pos = np.arange(j * TOK, (j + 1) * TOK, dtype=np.float64)
        ang = freqs[:, None] * pos[None, :]
        im = dict(shared)
        im["xT"] = np.ascontiguousarray(x[b, j * TOK:(j + 1) * TOK, :].T)
        im["cosd"] = np.cos(ang).astype(np.float32)
        im["ssin"] = (sgn[:, None] * np.sin(ang)).astype(np.float32)
        sel = np.zeros((64, 8), np.float32)
        for r in range(8):
            if r // 4 == b and r % 4 < j:
                sel[:, r] = 1.0
        im["selc"] = sel
        in_maps.append(im)
    return in_maps


def kernel(**inputs):
    nc = build_program(DUMPS)
    in_maps = _host_inputs(inputs)
    res = run_bass_kernel_spmd(nc, in_maps, core_ids=list(range(8)))
    out = np.empty((2, 8192, 1024), np.float32)
    for c in range(8):
        b, j = c // 4, c % 4
        out[b, j * TOK:(j + 1) * TOK, :] = res.results[c]["y"].T
    return out


# version 12
# speedup vs baseline: 1.0104x; 1.0104x over previous
import numpy as np
import concourse.bass as bass
import concourse.mybir as mybir
from concourse.bass_utils import run_bass_kernel_spmd

F32 = mybir.dt.float32
BF16 = mybir.dt.bfloat16
I32 = mybir.dt.int32
U16 = mybir.dt.uint16
AF = mybir.ActivationFunctionType
OP = mybir.AluOpType
AX = mybir.AxisListType.X

MRND = 1.5 * 2.0 ** 23
E = 1024
HCNT = 16
D = 64
TOK = 2048
NMAC = 4
MT = 512
NSUB = 8
ST = 256
INNER = 4096
EPS = 1e-5

DUMPS = ()
STAGE = 3
BSTOP = 8
DUMP_U = 0
ASTOP = 5


class P:
    ENG = ("pe", "act", "dve", "pool", "sp", "gp")

    def __init__(self, nc):
        self.nc = nc
        self.ops = {e: [] for e in self.ENG}
        self.cnt = {}
        self.sems = {}
        self.seq = 0

    def _app(self, eng, fn):
        self.ops[eng].append((self.seq, fn))
        self.seq += 1

    def sem(self, name):
        h = self.nc.alloc_semaphore(name=name)
        self.sems[name] = h
        self.cnt[name] = 0

    def wait(self, eng, sname, val=None):
        v = self.cnt[sname] if val is None else val
        if v <= 0:
            return
        h = self.sems[sname]
        self._app(eng, lambda e: e.wait_ge(h, v))

    def step(self, eng, build_fn, sname=None):
        sname = sname or eng
        self.cnt[sname] += 1
        h = self.sems[sname]

        def fn(e):
            build_fn(e).then_inc(h, 1)
        self._app(eng, fn)
        return self.cnt[sname]

    def plain(self, eng, build_fn):
        self._app(eng, build_fn)

    NPOOL = 16

    def dma(self, eng, q, out_ap, in_ap):
        i = self.rot[q]
        self.rot[q] = (i + 1) % self.NPOOL
        s = f"{q}{i}"
        h = self.sems[s]
        prev = self.cnt[s]
        if prev > 0:
            self._app(eng, lambda e: e.wait_ge(h, prev))
        self.cnt[s] = prev + 16
        tgt = self.cnt[s]

        def fn(e):
            e.dma_start(out=out_ap, in_=in_ap).then_inc(h, 16)
        self._app(eng, fn)
        return (s, tgt)

    def wdma(self, eng, *handles):
        for s, tgt in handles:
            self.wait(eng, s, tgt)

    def wdma_all(self, eng, q):
        for i in range(self.NPOOL):
            self.wait(eng, f"{q}{i}")


def build_program(dumps=()):
    nc = bass.Bass("TRN2", target_bir_lowering=False)
    p = P(nc)

    xT = nc.dram_tensor("xT", [E, TOK], F32, kind="ExternalInput").ap()
    Wq = nc.dram_tensor("Wq", [E, E], F32, kind="ExternalInput").ap()
    Wk = nc.dram_tensor("Wk", [E, E], F32, kind="ExternalInput").ap()
    Wv = nc.dram_tensor("Wv", [E, E], F32, kind="ExternalInput").ap()
    Wo = nc.dram_tensor("Wo", [E, E], F32, kind="ExternalInput").ap()
    Wg1 = nc.dram_tensor("Wg1", [E, D], F32, kind="ExternalInput").ap()
    Wg2 = nc.dram_tensor("Wg2", [D, E], F32, kind="ExternalInput").ap()
    wgT = nc.dram_tensor("wgT", [E, INNER], F32, kind="ExternalInput").ap()
    wuSL = nc.dram_tensor("wuSL", [32, E, 128], F32, kind="ExternalInput").ap()
    wdT = nc.dram_tensor("wdT", [INNER, E], F32, kind="ExternalInput").ap()
    gam = nc.dram_tensor("gam", [128, 8], F32, kind="ExternalInput").ap()
    bet = nc.dram_tensor("bet", [128, 8], F32, kind="ExternalInput").ap()
    cosd = nc.dram_tensor("cosd", [128, TOK], F32, kind="ExternalInput").ap()
    ssin = nc.dram_tensor("ssin", [128, TOK], F32, kind="ExternalInput").ap()
    selc = nc.dram_tensor("selc", [64, 8], F32, kind="ExternalInput").ap()
    y = nc.dram_tensor("y", [E, TOK], F32, kind="ExternalOutput").ap()

    kfm_sp = nc.dram_tensor("kfm_sp", [128, 16384], BF16).ap()
    ktm_sp = nc.dram_tensor("ktm_sp", [128, 16384], BF16).ap()
    v_sp = nc.dram_tensor("v_sp", [128, 16384], BF16).ap()
    xmid_sp = nc.dram_tensor("xmid_sp", [128, 16384], F32).ap()
    wd_t = nc.dram_tensor("wd_t", [INNER, E], BF16).ap()
    wu_t = nc.dram_tensor("wu_t", [32, 128, E], BF16).ap()
    cc_in = nc.dram_tensor("cc_in_t", [64, 1024], F32).ap()
    cc_out = nc.dram_tensor("cc_out_t", [512, 1024], F32,
                            addr_space="Shared").ap()

    dbg = {}
    for name, shape in dumps:
        dbg[name] = nc.dram_tensor("dbg_" + name, list(shape), F32,
                                   kind="ExternalOutput").ap()

    for s in ("cc", "pe", "act", "dve", "pool", "gpc"):
        p.sem(s)
    p.rot = {"dsp": 0, "dgp": 0}
    for q in ("dsp", "dgp"):
        for i in range(P.NPOOL):
            p.sem(f"{q}{i}")

    from contextlib import ExitStack
    octx = ExitStack()

    def osb(name, shape, dt=F32):
        return octx.enter_context(nc.sbuf_tensor(name, shape, dt))

    ident = osb("ident", [128, 128], BF16)
    mask4 = osb("mask4", [128, 512], F32)
    ones_col = osb("ones_col", [128, 1], BF16)
    ones_row = osb("ones_row", [1, 128], BF16)
    gam_sb = osb("gam_sb", [128, 8])
    bet_sb = osb("bet_sb", [128, 8])
    mcol = osb("mcol", [128, 1])
    mrow = osb("mrow", [128, 512], BF16)
    wsc = osb("wsc", [128, 8])
    wacc = osb("wacc", [128, 192])
    wab_bf = osb("wab_bf", [128, 128], BF16)
    tiny = osb("tiny", [1, 16])
    tiny_bf = osb("tiny_bf", [1, 16], BF16)
    selc_sb = osb("selc_sb", [64, 8])
    zrow_bf = osb("zrow_bf", [1, 512], BF16)
    cc_bf = osb("cc_bf", [64, 1024], BF16)
    seldg = osb("seldg", [64, 512], BF16)

    def setup(g):
        g.memset(ident[:, :], 0.0)
        g.affine_select(out=ident[:, :], in_=ident[:, :], compare_op=OP.not_equal,
                        fill=1.0, base=0, pattern=[[-1, 128]], channel_multiplier=1)
        g.memset(mask4[:, :], 1.0)
        g.affine_select(out=mask4[:, :].rearrange("p (r n) -> p r n", r=4),
                        in_=mask4[:, :].rearrange("p (r n) -> p r n", r=4),
                        compare_op=OP.is_ge, fill=0.0, base=0,
                        pattern=[[0, 4], [1, 128]], channel_multiplier=-1)
        g.memset(ones_col[:, :], 1.0)
        g.memset(ones_row[:, :], 1.0)
        g.memset(wacc[:, :], 0.0)
        g.memset(zrow_bf[:, :], 0.0)
        g.memset(mrow[:, :], MRND)
        g.memset(mcol[:, :], MRND).then_inc(p.sems["gpc"], 1)
    p.plain("gp", setup)
    p.cnt["gpc"] += 1
    h_cst = [p.dma("gp", "dgp", gam_sb[:, :], gam),
             p.dma("gp", "dgp", bet_sb[:, :], bet),
             p.dma("gp", "dgp", selc_sb[:, :], selc)]
    p.wait("dve", "gpc", 1)
    p.wait("pool", "gpc", 1)
    p.wait("act", "gpc", 1)
    p.wdma("dve", *h_cst)
    p.wdma("pe", *h_cst)

    ab = ExitStack()

    def absb(name, shape, dt=F32):
        return ab.enter_context(nc.sbuf_tensor(name, shape, dt))

    cos_m = absb("cos_m", [128, 512])
    sin_m = absb("sin_m", [128, 512])
    W1 = absb("W1", [128, 8192], BF16)
    W2 = absb("W2", [128, 8192], BF16)
    Wg1_sb = absb("Wg1_sb", [128, 512], BF16)
    Wg2_sb = absb("Wg2_sb", [64, 1024], BF16)
    xt_bf = absb("xt_bf", [128, 4096], BF16)
    FA = absb("FA", [128, 4096])
    FB = absb("FB", [128, 4096])
    FC = absb("FC", [128, 4096])
    krot = absb("krot", [128, 4096], BF16)
    ktm = absb("ktm", [128, 4096], BF16)
    vt = absb("vt", [128, 4096], BF16)
    kfmB = absb("kfmB", [128, 4096], BF16)
    q_oddM = absb("q_oddM", [64, 4096], BF16)
    k_oddM = absb("k_oddM", [64, 4096], BF16)
    eT_sb = absb("eT_sb", [128, 2048], BF16)
    S_bf = absb("S_bf", [64, 1024], BF16)
    stat_bf = absb("stat_bf", [128, 2048], BF16)
    xn_bf = absb("xn_bf", [128, 4096], BF16)
    swd = absb("swd", [128, 512])
    t1d = absb("t1d", [128, 512])
    t2d = absb("t2d", [128, 512])
    swp = absb("swp", [128, 512])
    t1p = absb("t1p", [128, 512])
    t2p = absb("t2p", [128, 512])
    xg1_bf = absb("xg1_bf", [64, 512], BF16)
    lnr = absb("lnr", [1, 2048])
    cc_sb = FB[0:64, 0:1024]
    sin_f = FB[0:64, 1024:2048]
    lnr_bf = absb("lnr_bf", [1, 2048], BF16)
    brstd = absb("brstd", [128, 512])
    bmrs = absb("bmrs", [128, 512])
    wmb = absb("wmb", [128, 1024])
    jA = absb("jA", [128, 512])

    wtb = absb("wtb", [128, 512], BF16)
    wsb_bf = absb("wsb_bf", [128, 2], BF16)
    oscr_b = absb("oscr_b", [128, 512])

    abps = ExitStack()
    ps_S = abps.enter_context(nc.psum_tensor("ps_S", [64, 1024], F32))
    ps_W1 = abps.enter_context(nc.psum_tensor("ps_W1", [128, 512], F32))
    ps_W2 = abps.enter_context(nc.psum_tensor("ps_W2", [128, 512], F32))

    aps = ExitStack()
    ps_V = aps.enter_context(nc.psum_tensor("ps_V", [128, 1024], F32))
    ps_T = aps.enter_context(nc.psum_tensor("ps_T", [128, 128], BF16))

    h_w = [p.dma("gp", "dgp", W1[:, :].rearrange("p (f o) -> p f o", f=8),
                 Wk.rearrange("(f p) o -> p f o", p=128)),
           p.dma("gp", "dgp", W2[:, :].rearrange("p (f o) -> p f o", f=8),
                 Wv.rearrange("(f p) o -> p f o", p=128)),
           p.dma("gp", "dgp", Wg1_sb[:, :].rearrange("p (f o) -> p f o", f=8),
                 Wg1.rearrange("(f p) o -> p f o", p=128)),
           p.dma("gp", "dgp", Wg2_sb[:, :], Wg2)]

    def emit_rotary(eng, cols, src, dst_bf, sw, t1, t2):
        def fn(e):
            e.tensor_copy(sw[0:32, :], src[32:64, cols])
            e.tensor_copy(sw[32:64, :], src[0:32, cols])
            e.tensor_copy(sw[64:96, :], src[96:128, cols])
            e.tensor_copy(sw[96:128, :], src[64:96, cols])
            e.tensor_tensor(t1[:, :], src[:, cols], cos_m[:, :], OP.mult)
            e.tensor_tensor(t2[:, :], sw[:, :], sin_m[:, :], OP.mult)
            return e.tensor_tensor(dst_bf[:, cols], t1[:, :], t2[:, :], OP.add)
        p.step(eng, fn)

    def spacer(e):
        return e.tensor_scalar(jA[:, 256:512], jA[:, 0:256], 1.0, None, OP.mult)

    wj_hist = []

    def emit_wmean_job(src_ap, jcol):
        j = emit_wmean_job.i
        emit_wmean_job.i += 1
        buf = wmb[:, (j % 2) * 512:(j % 2) * 512 + 512]
        q, dq = ("sp", "dsp") if j % 2 == 0 else ("gp", "dgp")
        if j >= 2:
            p.wait(q, wj_hist[j - 2][0], wj_hist[j - 2][1])
        else:
            p.wait(q, "act")
            p.wait(q, "dve")
        hb = p.dma(q, dq, buf, src_ap)
        p.wdma("act", hb)
        p.wait("act", "pool")

        def f1(e, buf=buf, jcol=jcol):
            return e.activation(jA[:, 0:512], buf, AF.Abs,
                                accum_out=wacc[:, jcol:jcol + 1])
        wj_hist.append(("act", p.step("act", f1)))
    emit_wmean_job.i = 0

    def emit_wscale(jc0, n_elems, scol, icol):
        p.wait("dve", "act")

        def sp1(e, jc0=jc0):
            a, b = t1d, t2d
            e.tensor_tensor(a[:, 0:32], wacc[:, jc0:jc0 + 32],
                            wacc[:, jc0 + 32:jc0 + 64], OP.add)
            spacer(e)
            e.tensor_tensor(a[:, 32:48], a[:, 0:16], a[:, 16:32], OP.add)
            spacer(e)
            e.tensor_tensor(a[:, 48:56], a[:, 32:40], a[:, 40:48], OP.add)
            spacer(e)
            e.tensor_tensor(a[:, 56:60], a[:, 48:52], a[:, 52:56], OP.add)
            spacer(e)
            e.tensor_tensor(a[:, 60:62], a[:, 56:58], a[:, 58:60], OP.add)
            spacer(e)
            e.tensor_tensor(a[:, 62:63], a[:, 60:61], a[:, 61:62], OP.add)
            spacer(e)
            e.tensor_copy(b[0:64, 0:1], a[64:128, 62:63])
            spacer(e)
            e.tensor_tensor(b[0:64, 1:2], a[0:64, 62:63], b[0:64, 0:1], OP.add)
            spacer(e)
            e.tensor_copy(b[0:32, 2:3], b[32:64, 1:2])
            spacer(e)
            e.tensor_tensor(b[0:32, 3:4], b[0:32, 1:2], b[0:32, 2:3], OP.add)
            spacer(e)
            e.tensor_scalar(b[0:32, 4:5], b[0:32, 3:4], MRND, None, OP.add)
            spacer(e)
            e.tensor_scalar(b[0:32, 10:11], b[0:32, 4:5], -MRND, None, OP.add)
            spacer(e)
            e.tensor_tensor(b[0:32, 5:6], b[0:32, 3:4], b[0:32, 10:11], OP.subtract)
            spacer(e)
            e.tensor_scalar(b[0:32, 6:7], b[0:32, 10:11], 1.0 / 256.0, MRND,
                            OP.mult, OP.add)
            spacer(e)
            e.tensor_scalar(b[0:32, 7:8], b[0:32, 6:7], -MRND, 256.0,
                            OP.add, OP.mult)
            spacer(e)
            e.tensor_tensor(b[0:32, 8:9], b[0:32, 10:11], b[0:32, 7:8], OP.subtract)
            spacer(e)
            e.tensor_copy(wab_bf[0:32, 0:1], b[0:32, 7:8])
            e.tensor_copy(wab_bf[0:32, 1:2], b[0:32, 8:9])
            e.tensor_copy(wab_bf[0:32, 2:3], b[0:32, 5:6])
            spacer(e)
            e.scalar_tensor_tensor(b[0:32, 9:10], wab_bf[0:32, 2:3], -1.0,
                                   b[0:32, 5:6], OP.mult, OP.add)
            spacer(e)
            return e.tensor_copy(wab_bf[0:32, 3:4], b[0:32, 9:10])
        p.step("dve", sp1)
        p.wait("pe", "dve")

        def sm(e):
            return e.matmul(ps_W1[0:1, 0:4], ones_col[0:32, 0:1],
                            wab_bf[0:32, 0:4], start=True, stop=True)
        p.step("pe", sm)
        p.wait("act", "pe")

        def sacc_f(e):
            return e.activation(jA[0:1, 0:4], ps_W1[0:1, 0:4], AF.Identity,
                                accum_out=tiny[0:1, 0:1])
        p.step("act", sacc_f)
        p.wait("dve", "act")

        def sc(e, n_elems=n_elems):
            e.tensor_scalar(tiny[0:1, 2:3], tiny[0:1, 0:1], 1.0 / n_elems, 1e-5,
                            OP.mult, OP.max)
            spacer(e)
            e.reciprocal(tiny[0:1, 3:4], tiny[0:1, 2:3])
            spacer(e)
            e.tensor_tensor(tiny[0:1, 5:6], tiny[0:1, 2:3], tiny[0:1, 3:4], OP.mult)
            spacer(e)
            e.tensor_scalar(tiny[0:1, 6:7], tiny[0:1, 5:6], -1.0, 2.0,
                            OP.mult, OP.add)
            spacer(e)
            e.tensor_tensor(tiny[0:1, 7:8], tiny[0:1, 3:4], tiny[0:1, 6:7], OP.mult)
            spacer(e)
            e.tensor_copy(tiny_bf[0:1, 0:1], tiny[0:1, 7:8])
            spacer(e)
            e.scalar_tensor_tensor(tiny[0:1, 4:5], tiny_bf[0:1, 0:1], -1.0,
                                   tiny[0:1, 7:8], OP.mult, OP.add)
            spacer(e)
            e.tensor_copy(tiny_bf[0:1, 1:2], tiny[0:1, 4:5])
            e.tensor_copy(tiny_bf[0:1, 2:3], tiny[0:1, 2:3])
            spacer(e)
            e.scalar_tensor_tensor(tiny[0:1, 9:10], tiny_bf[0:1, 1:2], -1.0,
                                   tiny[0:1, 4:5], OP.mult, OP.add)
            e.scalar_tensor_tensor(tiny[0:1, 8:9], tiny_bf[0:1, 2:3], -1.0,
                                   tiny[0:1, 2:3], OP.mult, OP.add)
            spacer(e)
            e.tensor_copy(tiny_bf[0:1, 4:5], tiny[0:1, 9:10])
            e.tensor_copy(tiny_bf[0:1, 3:4], tiny[0:1, 8:9])
            spacer(e)
            e.scalar_tensor_tensor(tiny[0:1, 10:11], tiny_bf[0:1, 3:4], -1.0,
                                   tiny[0:1, 8:9], OP.mult, OP.add)
            spacer(e)
            return e.tensor_copy(tiny_bf[0:1, 5:6], tiny[0:1, 10:11])
        p.step("dve", sc)
        p.wait("pe", "dve")

        def bc(e):
            e.matmul(ps_W2[:, 0:1], ones_row[:, :], tiny_bf[0:1, 0:1], start=True, stop=False)
            e.matmul(ps_W2[:, 0:1], ones_row[:, :], tiny_bf[0:1, 1:2], start=False, stop=False)
            e.matmul(ps_W2[:, 0:1], ones_row[:, :], tiny_bf[0:1, 4:5], start=False, stop=True)
            e.matmul(ps_W2[:, 1:2], ones_row[:, :], tiny_bf[0:1, 2:3],
                     start=True, stop=False)
            e.matmul(ps_W2[:, 1:2], ones_row[:, :], tiny_bf[0:1, 3:4],
                     start=False, stop=False)
            return e.matmul(ps_W2[:, 1:2], ones_row[:, :], tiny_bf[0:1, 5:6],
                            start=False, stop=True)
        p.step("pe", bc)
        p.wait("dve", "pe")

        def cv(e, scol=scol, icol=icol):
            e.tensor_copy(wsc[:, scol:scol + 1], ps_W2[:, 0:1])
            return e.tensor_copy(wsc[:, icol:icol + 1], ps_W2[:, 1:2])
        p.step("dve", cv)

    wd_flatI = wdT.rearrange("a b -> (a b)").rearrange("(c p t) -> c p t",
                                                        p=128, t=512)
    wd_flatO = wd_t.rearrange("a b -> (a b)").rearrange("(c p t) -> c p t",
                                                        p=128, t=512)

    def emit_wtern_job(j):
        ri = emit_wmean_job.i
        emit_wmean_job.i += 1
        buf = wmb[:, (ri % 2) * 512:(ri % 2) * 512 + 512]
        q, dq = ("sp", "dsp") if ri % 2 == 0 else ("gp", "dgp")
        if ri >= 2:
            p.wait(q, wj_hist[ri - 2][0], wj_hist[ri - 2][1])
        else:
            p.wait(q, "act")
            p.wait(q, "dve")
            p.wait(q, "pool")
        hb = p.dma(q, dq, buf, wd_flatI[j, :, :])
        p.wdma("dve", hb)
        p.wait("dve", "pool")

        def f1(e, buf=buf):
            return e.scalar_tensor_tensor(swd[:, :], buf, wsc[:, 0:1],
                                          mrow[:, :], OP.mult, OP.add)
        wj_hist.append(("dve", p.step("dve", f1)))
        p.wait("pool", "dve")

        def f2(e):
            e.tensor_scalar(oscr_b[:, :], swd[:, :], MRND + 1.0, MRND - 1.0,
                            OP.min, OP.max)
            return e.tensor_scalar(wtb[:, :], oscr_b[:, :], -MRND, None, OP.add)
        p.step("pool", f2)
        p.wait("sp", "pool")
        p.dma("sp", "dsp", wd_flatO[j, :, :], wtb[:, :])

    def emit_ln(src, gamma, out_bf=None, out_f32=None):
        for c in range(8):
            cl = slice(c * 512, (c + 1) * 512)
            r = (c % 2) * 512

            def cp(e, cl=cl, r=r):
                return e.tensor_copy(stat_bf[:, r:r + 512], src[:, cl])
            p.step("dve", cp)
            p.wait("act", "dve")

            def sq(e, cl=cl, r=r):
                return e.activation(stat_bf[:, 1024 + r:1024 + r + 512], src[:, cl],
                                    AF.Square)
            p.step("act", sq)
            p.wait("pe", "dve")
            p.wait("pe", "act")

            def st(e, c=c, r=r):
                e.matmul(ps_W1[0:1, :], ones_col[:, :], stat_bf[:, r:r + 512],
                         start=(c == 0), stop=(c == 7))
                return e.matmul(ps_W2[0:1, :], ones_col[:, :],
                                stat_bf[:, 1024 + r:1024 + r + 512],
                                start=(c == 0), stop=(c == 7))
            v = p.step("pe", st)
            if c < 7:
                p.wait("dve", "pe", v)
                p.wait("act", "pe", v)
        p.wait("dve", "pe")

        def chain(e):
            L = lnr
            mu, var, yy, tmp = (L[0:1, i * 512:(i + 1) * 512] for i in range(4))
            mrs = tmp
            t2 = var
            e.tensor_scalar(mu, ps_W1[0:1, :], 1.0 / E, None, OP.mult)
            e.tensor_scalar(tmp, ps_W2[0:1, :], 1.0 / E, EPS, OP.mult, OP.add)
            e.tensor_tensor(var, mu, mu, OP.mult)
            e.tensor_tensor(var, tmp, var, OP.subtract)
            e.tensor_copy(lnr_bf[0:1, 0:512], var)
            e.tensor_copy(tmp, lnr_bf[0:1, 0:512].bitcast(U16))
            e.tensor_scalar(tmp, tmp, 0.5, MRND, OP.mult, OP.add)
            e.tensor_scalar(tmp, tmp, -MRND, None, OP.add)
            e.tensor_scalar(tmp, tmp, -1.0, 24375.0, OP.mult, OP.add)
            e.tensor_copy(lnr_bf[0:1, 0:512].bitcast(U16), tmp)
            e.tensor_copy(yy, lnr_bf[0:1, 0:512])
            for _ in range(3):
                e.tensor_tensor(tmp, yy, yy, OP.mult)
                e.tensor_tensor(tmp, tmp, var, OP.mult)
                e.tensor_scalar(tmp, tmp, -0.5, 1.5, OP.mult, OP.add)
                e.tensor_tensor(yy, yy, tmp, OP.mult)
            e.tensor_tensor(mrs, mu, yy, OP.mult)
            e.tensor_copy(lnr_bf[0:1, 0:512], yy)
            e.tensor_tensor(t2, yy, lnr_bf[0:1, 0:512], OP.subtract)
            e.tensor_copy(lnr_bf[0:1, 512:1024], t2)
            e.tensor_copy(lnr_bf[0:1, 1024:1536], mrs)
            e.tensor_tensor(t2, mrs, lnr_bf[0:1, 1024:1536], OP.subtract)
            return e.tensor_copy(lnr_bf[0:1, 1536:2048], t2)
        p.step("dve", chain)
        p.wait("pe", "dve")

        def bc(e):
            e.matmul(ps_W1[:, :], ones_row[:, :], lnr_bf[0:1, 0:512], start=True, stop=False)
            e.matmul(ps_W1[:, :], ones_row[:, :], lnr_bf[0:1, 512:1024], start=False, stop=True)
            e.matmul(ps_W2[:, :], ones_row[:, :], lnr_bf[0:1, 1024:1536], start=True, stop=False)
            return e.matmul(ps_W2[:, :], ones_row[:, :], lnr_bf[0:1, 1536:2048],
                            start=False, stop=True)
        p.step("pe", bc)
        p.wait("dve", "pe")

        def bev(e):
            e.tensor_copy(brstd[:, :], ps_W1[:, :])
            return e.tensor_copy(bmrs[:, :], ps_W2[:, :])
        p.step("dve", bev)
        p.wait("pool", "dve")
        for c in range(8):
            cl = slice(c * 512, (c + 1) * 512)
            eng = "dve" if c % 2 == 0 else "pool"
            tb = t1d if eng == "dve" else t1p

            def ap_fn(e, cl=cl, c=c, tb=tb):
                e.tensor_tensor(tb[:, :], src[:, cl], brstd[:, :], OP.mult)
                if gamma:
                    e.tensor_tensor(tb[:, :], tb[:, :], bmrs[:, :], OP.subtract)
                    return e.tensor_scalar(out_bf[:, cl], tb[:, :],
                                           gam_sb[:, c:c + 1], bet_sb[:, c:c + 1],
                                           OP.mult, OP.add)
                dst = out_f32 if out_f32 is not None else out_bf
                return e.tensor_tensor(dst[:, cl], tb[:, :], bmrs[:, :], OP.subtract)
            p.step(eng, ap_fn)

    p.wait("pe", "gpc", 1)

    def s_dummy(e):
        e.matmul(ps_S[:, 0:512], ones_row[0:1, 0:64], zrow_bf[0:1, 0:512],
                 start=True, stop=True)
        return e.matmul(ps_S[:, 512:1024], ones_row[0:1, 0:64], zrow_bf[0:1, 0:512],
                        start=True, stop=True)
    p.step("pe", s_dummy)

    for m in range(NMAC):
        mc = slice(m * MT, (m + 1) * MT)
        p.wait("gp", "pe")
        h_xt = p.dma("gp", "dgp", xt_bf[:, :].rearrange("p (f t) -> p f t", f=8),
                     xT.rearrange("(f p) t -> p f t", p=128)[:, :, mc])
        p.wait("sp", "dve")
        p.wait("sp", "pool")
        h_tab = [p.dma("sp", "dsp", cos_m[:, :], cosd[:, mc]),
                 p.dma("sp", "dsp", sin_m[:, :], ssin[:, mc])]

        p.wdma("pe", h_xt)
        if m == 0:
            p.wdma("pe", *h_w)
            p.wait("pe", "gpc", 1)
        kev_hist = []
        for oc in range(8):
            pst = ps_W1 if oc % 2 == 0 else ps_W2

            def kproj(e, oc=oc, pst=pst):
                ins = None
                for f in range(8):
                    ins = e.matmul(pst[:, :],
                                   W1[:, f * 1024 + oc * 128: f * 1024 + (oc + 1) * 128],
                                   xt_bf[:, f * 512:(f + 1) * 512],
                                   start=(f == 0), stop=(f == 7))
                return ins
            if oc >= 2:
                p.wait("pe", "act", kev_hist[oc - 2])
            p.step("pe", kproj)
            p.wait("act", "pe")

            def ev(e, oc=oc, pst=pst):
                return e.activation(FA[:, oc * 512:(oc + 1) * 512], pst[:, :], AF.Silu)
            kev_hist.append(p.step("act", ev))

        p.wait("dve", "act")
        p.wdma("dve", *h_tab)
        p.wait("pool", "act")
        p.wdma("pool", *h_tab)
        if m > 0:
            p.wdma("dve", *h_spill)
            p.wdma("pool", *h_spill)
        for c in range(8):
            cols = slice(c * 512, (c + 1) * 512)
            if c < 5:
                emit_rotary("dve", cols, FA, krot, swd, t1d, t2d)
            else:
                emit_rotary("pool", cols, FA, krot, swp, t1p, oscr_b)

        p.wait("pe", "dve")
        p.wait("pe", "pool")
        for s in range(4):
            for c in range(8):
                def tp(e, s=s, c=c):
                    return e.transpose(ps_T[:, :],
                                       krot[:, c * 512 + s * 128: c * 512 + (s + 1) * 128],
                                       ident[:, :])
                p.step("pe", tp)
                p.wait("dve", "pe")

                def tpe(e, s=s, c=c):
                    return e.tensor_copy(
                        ktm[:, s * 1024 + c * 128: s * 1024 + (c + 1) * 128], ps_T[:, :])
                v = p.step("dve", tpe)
                p.wait("pe", "dve", v)

        p.wait("sp", "dve")
        p.wait("sp", "pool")
        h_spill = [p.dma("sp", "dsp", kfm_sp[:, m * 4096:(m + 1) * 4096], krot[:, :])]
        p.wait("gp", "dve")
        p.wait("gp", "pe")
        p.wait("gp", "pool")
        h_spill.append(p.dma("gp", "dgp", ktm_sp[:, m * 4096:(m + 1) * 4096],
                             ktm[:, :]))

        for s in range(4):
            def vproj(e, s=s):
                ins = None
                for f in range(8):
                    for hf in range(2):
                        ins = e.matmul(ps_V[:, hf * 512:(hf + 1) * 512],
                                       xt_bf[:, f * 512 + s * 128: f * 512 + (s + 1) * 128],
                                       W2[:, f * 1024 + hf * 512: f * 1024 + (hf + 1) * 512],
                                       start=(f == 0), stop=(f == 7))
                return ins
            if s >= 1 or m >= 1:
                p.wait("pe", "dve")
            p.step("pe", vproj)
            p.wait("dve", "pe")

            def vev(e, s=s):
                return e.tensor_copy(vt[:, s * 1024:(s + 1) * 1024], ps_V[:, :])
            p.step("dve", vev)

        p.wait("sp", "dve")
        h_spill.append(p.dma("sp", "dsp", v_sp[:, m * 4096:(m + 1) * 4096], vt[:, :]))
        if "vsp" in dbg:
            p.wait("gp", "dve")
            p.dma("gp", "dgp", dbg["vsp"][:, m * 4096:(m + 1) * 4096], vt[:, :])
        if "kfm" in dbg:
            p.wait("gp", "dve")
            p.wait("gp", "pool")
            p.dma("gp", "dgp", dbg["kfm"][:, m * 4096:(m + 1) * 4096], krot[:, :])
        if "ktm" in dbg:
            p.wait("gp", "dve")
            p.dma("gp", "dgp", dbg["ktm"][:, m * 4096:(m + 1) * 4096], ktm[:, :])

        p.wait("pe", "dve")

        def dsacc(e, m=m):
            ins = None
            for s in range(4):
                for h in range(HCNT):
                    ins = e.matmul(ps_S[:, h * 64:(h + 1) * 64],
                                   ktm[:, s * 1024 + h * 64: s * 1024 + (h + 1) * 64],
                                   vt[:, s * 1024 + h * 64: s * 1024 + (h + 1) * 64],
                                   start=False, stop=(m == 3 and s == 3))
            return ins
        p.step("pe", dsacc)

        wd_flat = wdT.rearrange("a b -> (a b)").rearrange("(c p t) -> c p t",
                                                          p=128, t=512)
        for ci in range(16):
            g = m * 16 + ci
            emit_wmean_job(wd_flat[g, :, :], g)

    emit_wscale(0, INNER * E, 0, 1)

    p.wait("dve", "pe")

    def sa_ev(e):
        return e.tensor_copy(sin_f, ps_S[:, :])
    sa_v = p.step("dve", sa_ev)
    if "sat" in dbg:
        p.wait("sp", "dve", sa_v)
        p.dma("sp", "dsp", dbg["sat"][:, :], sin_f)
    p.wait("gp", "dve", sa_v)
    cc_in_sent = p.dma("gp", "dgp", cc_in, sin_f)
    p.wdma("gp", cc_in_sent)

    def coll(g):
        g.collective_compute(
            "AllGather", OP.bypass, replica_groups=[list(range(8))],
            ins=[cc_in.opt()], outs=[cc_out.opt()]).then_inc(p.sems["cc"], 1)
    p.plain("gp", coll)
    p.cnt["cc"] += 1
    p.wait("gp", "cc", 1)

    if "ccg" in dbg:
        p.dma("gp", "dgp", dbg["ccg"], cc_out)

    def mk_seldg(e):
        ins = None
        for rr in range(8):
            ins = e.tensor_scalar(seldg[:, rr * 64:(rr + 1) * 64],
                                  ident[0:64, 0:64], selc_sb[:, rr:rr + 1], None,
                                  OP.mult)
        return ins
    p.step("dve", mk_seldg)

    for rr in range(8):
        p.wait("gp", "dve")
        h_blk = p.dma("gp", "dgp", cc_sb, cc_out[rr * 64:(rr + 1) * 64, :])
        p.wdma("dve", h_blk)
        p.wait("dve", "pe")

        def blkcast(e):
            return e.tensor_copy(cc_bf[:, :], cc_sb)
        bc_v = p.step("dve", blkcast)
        p.wait("pe", "dve", bc_v)

        def sacc(e, rr=rr):
            e.matmul(ps_S[:, 0:512], seldg[:, rr * 64:(rr + 1) * 64],
                     cc_bf[:, 0:512], start=(rr == 0), stop=(rr == 7))
            return e.matmul(ps_S[:, 512:1024], seldg[:, rr * 64:(rr + 1) * 64],
                            cc_bf[:, 512:1024], start=(rr == 0), stop=(rr == 7))
        p.step("pe", sacc)

    if "sdg" in dbg:
        p.wait("gp", "dve")
        p.dma("gp", "dgp", dbg["sdg"][:, :], seldg[:, :])
        p.dma("gp", "dgp", dbg["ccbf"][:, :], cc_bf[:, :])
    if "sa" in dbg:
        p.wait("dve", "pe")

        def sadump(e):
            return e.tensor_copy(sin_f, ps_S[:, :])
        sd_v = p.step("dve", sadump)
        p.wait("pe", "dve", sd_v)
        p.wait("sp", "dve", sd_v)
        p.dma("sp", "dsp", dbg["sa"][:, :], sin_f)
        p.wdma_all("pe", "dsp")

    if STAGE < 2:
        for c in range(8):
            p.dma("sp", "dsp", y[c * 128:(c + 1) * 128, :], FA[:, 0:2048])
        p.wdma_all("sp", "dsp")
        p.wdma_all("gp", "dgp")
        _emit_all(nc, p)
        aps.close()
        abps.close()
        ab.close()
        octx.close()
        return nc

    aps.close()
    bps = ExitStack()
    ps_O1 = bps.enter_context(nc.psum_tensor("ps_O1", [64, 1024], F32))
    ps_O2 = bps.enter_context(nc.psum_tensor("ps_O2", [64, 1024], F32))

    p.wait("gp", "pe")
    h_wb = [p.dma("gp", "dgp", W1[:, :].rearrange("p (f o) -> p f o", f=8),
                  Wq.rearrange("(f p) o -> p f o", p=128)),
            p.dma("gp", "dgp", W2[:, :].rearrange("p (f o) -> p f o", f=8),
                  Wo.rearrange("(f p) o -> p f o", p=128))]


    wmean_jobs = []
    wg_flat = wgT.rearrange("a b -> (a b)").rearrange("(c p t) -> c p t",
                                                     p=128, t=512)
    for ci in range(64):
        wmean_jobs.append((wg_flat[ci, :, :], 64 + ci))
    wu_flat = wuSL.rearrange("a b c -> (a b c)").rearrange("(c p t) -> c p t",
                                                           p=128, t=512)
    for ci in range(64):
        wmean_jobs.append((wu_flat[ci, :, :], 128 + ci))
    wj = [0]
    tj = [0]

    for m in range(NMAC):
        mc = slice(m * MT, (m + 1) * MT)
        p.wait("gp", "pe")
        h_xt = p.dma("gp", "dgp", xt_bf[:, :].rearrange("p (f t) -> p f t", f=8),
                     xT.rearrange("(f p) t -> p f t", p=128)[:, :, mc])
        p.wait("sp", "dve")
        p.wait("sp", "pool")
        p.wait("sp", "pe")
        h_kv = [p.dma("sp", "dsp", kfmB[:, :], kfm_sp[:, m * 4096:(m + 1) * 4096]),
                p.dma("sp", "dsp", ktm[:, :], ktm_sp[:, m * 4096:(m + 1) * 4096])]
        p.wait("gp", "dve")
        p.wait("gp", "pe")
        h_kv.append(p.dma("gp", "dgp", vt[:, :], v_sp[:, m * 4096:(m + 1) * 4096]))
        h_tab = [p.dma("sp", "dsp", cos_m[:, :], cosd[:, mc]),
                 p.dma("sp", "dsp", sin_m[:, :], ssin[:, mc])]
        p.wait("dve", "pool")
        p.wait("pool", "dve")
        p.wait("act", "dve")
        p.wait("act", "pool")
        p.wdma("dve", *h_kv)

        qev_hist = []
        p.wdma("pe", h_xt)
        if m == 0:
            p.wdma("pe", *h_wb)
        for oc in range(8):
            pst = ps_W1 if oc % 2 == 0 else ps_W2

            def qproj(e, oc=oc, pst=pst):
                ins = None
                for f in range(8):
                    ins = e.matmul(pst[:, :],
                                   W1[:, f * 1024 + oc * 128: f * 1024 + (oc + 1) * 128],
                                   xt_bf[:, f * 512:(f + 1) * 512],
                                   start=(f == 0), stop=(f == 7))
                return ins
            if oc >= 2:
                p.wait("pe", "act", qev_hist[oc - 2])
            p.step("pe", qproj)
            p.wait("act", "pe")

            def ev(e, oc=oc, pst=pst):
                return e.activation(FA[:, oc * 512:(oc + 1) * 512], pst[:, :], AF.Silu)
            qev_hist.append(p.step("act", ev))

        if BSTOP < 2:
            continue
        p.wait("dve", "act")
        p.wdma("dve", *h_tab)
        p.wait("pool", "act")
        p.wdma("pool", *h_tab)
        for c in range(8):
            cols = slice(c * 512, (c + 1) * 512)
            if c < 5:
                emit_rotary("dve", cols, FA, krot, swd, t1d, t2d)
            else:
                emit_rotary("pool", cols, FA, krot, swp, t1p, oscr_b)
        p.wait("dve", "pool")

        def oddcp(e):
            e.tensor_copy(q_oddM[0:64, :], krot[64:128, :])
            return e.tensor_copy(k_oddM[0:64, :], kfmB[64:128, :])
        ocv = p.step("dve", oddcp)
        if m == 0 and "qod" in dbg:
            p.wait("gp", "dve", ocv)
            p.dma("gp", "dgp", dbg["qod"], q_oddM[0:64, :])
            p.dma("gp", "dgp", dbg["kod"], k_oddM[0:64, :])
            if "qrt" in dbg:
                p.dma("gp", "dgp", dbg["qrt"], krot[:, :])
            if "qsl" in dbg:
                p.dma("gp", "dgp", dbg["qsl"], FA[:, 0:4096])

        if BSTOP < 3:
            continue
        def g1(e):
            ins = None
            for f in range(8):
                ins = e.matmul(ps_O1[:, 0:512], Wg1_sb[:, f * 64:(f + 1) * 64],
                               xt_bf[:, f * 512:(f + 1) * 512],
                               start=(f == 0), stop=(f == 7))
            return ins
        p.wait("pe", "dve")
        p.step("pe", g1)
        p.wait("dve", "pe")

        def g1e(e):
            return e.tensor_copy(xg1_bf[:, :], ps_O1[:, 0:512])
        g1v = p.step("dve", g1e)
        p.wait("pe", "dve", g1v)
        for oc in range(8):
            pst = ps_W1 if oc % 2 == 0 else ps_W2

            def g2(e, oc=oc, pst=pst):
                return e.matmul(pst[:, :], Wg2_sb[:, oc * 128:(oc + 1) * 128],
                                xg1_bf[:, :], start=True, stop=True)
            if oc >= 2:
                p.wait("pe", "dve")
            p.step("pe", g2)
            p.wait("act", "pe")
            p.wait("act", "pool")

            def gt(e, oc=oc, pst=pst):
                return e.activation(jA[:, 0:512], pst[:, :], AF.Tanh, scale=0.5)
            p.step("act", gt)
            p.wait("dve", "act")

            def gs(e, oc=oc):
                return e.tensor_scalar(FB[:, oc * 512:(oc + 1) * 512], jA[:, 0:512],
                                       0.5, 0.5, OP.mult, OP.add)
            v = p.step("dve", gs)
            p.wait("act", "dve", v)

        if BSTOP < 4:
            continue
        p.wdma("pe", *h_kv)
        for s in range(4):
            g = m * 4 + s
            p.wait("dve", "pe")

            def sbf(e):
                return e.tensor_copy(S_bf[:, :], ps_S[:, :])
            sv = p.step("dve", sbf)
            p.wait("pe", "dve", sv)

            if ASTOP < 2:
                continue
            for rnd in range(2):
                def emm(e, s=s, rnd=rnd):
                    ins = None
                    for hh in range(8):
                        h = rnd * 8 + hh
                        c = h // 2
                        pst = ps_W1 if hh < 4 else ps_W2
                        col = (hh % 4) * 128
                        cols = slice(c * 512 + s * 128, c * 512 + (s + 1) * 128)
                        kl = (kfmB[0:64, cols] if h % 2 == 0
                              else k_oddM[0:64, cols])
                        ql = (krot[0:64, cols] if h % 2 == 0
                              else q_oddM[0:64, cols])
                        ins = e.matmul(pst[:, col:col + 128], kl, ql,
                                       start=True, stop=True)
                    return ins
                p.step("pe", emm)
                p.wait("dve", "pe")

                def eev(e, rnd=rnd):
                    e.tensor_tensor(eT_sb[:, rnd * 1024: rnd * 1024 + 512],
                                    ps_W1[:, :], mask4[:, :], OP.mult)
                    return e.tensor_tensor(eT_sb[:, rnd * 1024 + 512: rnd * 1024 + 1024],
                                           ps_W2[:, :], mask4[:, :], OP.mult)
                ev_v = p.step("dve", eev)
                p.wait("pe", "dve", ev_v)

            if ASTOP < 3:
                continue
            def avq(e, s=s):
                ins = None
                for h in range(HCNT):
                    c = h // 2
                    pso = ps_O1 if h % 2 == 0 else ps_O2
                    ocol = c * 128
                    e.matmul(pso[:, ocol:ocol + 128],
                             vt[:, s * 1024 + h * 64: s * 1024 + (h + 1) * 64],
                             eT_sb[:, h * 128:(h + 1) * 128],
                             start=True, stop=False)
                    qsl = (krot[0:64, c * 512 + s * 128: c * 512 + (s + 1) * 128]
                           if h % 2 == 0 else
                           q_oddM[0:64, c * 512 + s * 128: c * 512 + (s + 1) * 128])
                    ins = e.matmul(pso[:, ocol:ocol + 128],
                                   S_bf[:, h * 64:(h + 1) * 64], qsl,
                                   start=False, stop=True)
                return ins
            p.step("pe", avq)

            if ASTOP < 4:
                continue
            def dsu(e, s=s, g=g):
                ins = None
                for h in range(HCNT):
                    ins = e.matmul(ps_S[:, h * 64:(h + 1) * 64],
                                   ktm[:, s * 1024 + h * 64: s * 1024 + (h + 1) * 64],
                                   vt[:, s * 1024 + h * 64: s * 1024 + (h + 1) * 64],
                                   start=False, stop=(g == 15))
                return ins
            p.step("pe", dsu)

            if ASTOP < 5:
                continue
            p.wait("dve", "pe")

            def oge(e, s=s):
                ins = None
                for par, pso in ((0, ps_O1), (1, ps_O2)):
                    ov = FC[par * 64:(par + 1) * 64, :].rearrange(
                        "p (c t) -> p c t", c=8)[:, :, s * 128:(s + 1) * 128]
                    iv = pso[:, :].rearrange("p (c t) -> p c t", c=8)
                    gv = FB[par * 64:(par + 1) * 64, :].rearrange(
                        "p (c t) -> p c t", c=8)[:, :, s * 128:(s + 1) * 128]
                    ins = e.tensor_tensor(ov, iv, gv, OP.mult)
                return ins
            ogv = p.step("dve", oge)
            p.wait("pe", "dve", ogv)

        if BSTOP >= 8:
            for _ in range(32):
                if wj[0] < len(wmean_jobs):
                    srcj, col = wmean_jobs[wj[0]]
                    emit_wmean_job(srcj, col)
                    wj[0] += 1
            for _ in range(16):
                if tj[0] < 64:
                    emit_wtern_job(tj[0])
                    tj[0] += 1

        if "og" in dbg:
            p.wait("sp", "dve")
            p.dma("sp", "dsp", dbg["og"][:, m * 4096:(m + 1) * 4096], FC[:, :])
        if "gate" in dbg:
            p.wait("sp", "dve")
            p.dma("sp", "dsp", dbg["gate"][:, m * 4096:(m + 1) * 4096], FB[:, :])
        if BSTOP < 5:
            continue
        emit_ln(FC, True, out_bf=xn_bf)

        if BSTOP < 6:
            continue
        p.wait("pe", "dve")
        p.wait("pe", "pool")
        r1_hist = []
        for oc in range(8):
            pst = ps_W1 if oc % 2 == 0 else ps_W2
            xslot = (t1d if oc % 2 == 0 else t2d)[:, :]
            p.wait("sp", "dve")
            h_x = p.dma("sp", "dsp", xslot,
                        xT[oc * 128:(oc + 1) * 128, mc])

            def wop(e, oc=oc, pst=pst):
                ins = None
                for f in range(8):
                    ins = e.matmul(pst[:, :],
                                   W2[:, f * 1024 + oc * 128: f * 1024 + (oc + 1) * 128],
                                   xn_bf[:, f * 512:(f + 1) * 512],
                                   start=(f == 0), stop=(f == 7))
                return ins
            if oc >= 2:
                p.wait("pe", "dve", r1_hist[oc - 2])
            p.step("pe", wop)
            p.wait("dve", "pe")
            p.wdma("dve", h_x)

            def r1f(e, oc=oc, pst=pst, xslot=xslot):
                return e.tensor_tensor(FA[:, oc * 512:(oc + 1) * 512], pst[:, :],
                                       xslot, OP.add)
            r1_hist.append(p.step("dve", r1f))

        if BSTOP < 7:
            continue
        emit_ln(FA, False, out_f32=FB)

        p.wait("sp", "dve")
        p.wait("sp", "pool")
        p.dma("sp", "dsp", xmid_sp[:, m * 4096:(m + 1) * 4096], FB[:, :])
        if "xmid" in dbg:
            p.dma("sp", "dsp", dbg["xmid"][:, m * 4096:(m + 1) * 4096], FB[:, :])
        if "r1" in dbg:
            p.dma("sp", "dsp", dbg["r1"][:, m * 4096:(m + 1) * 4096], FA[:, :])
        if "xn" in dbg:
            p.wait("gp", "dve")
            p.wait("gp", "pool")
            p.dma("gp", "dgp", dbg["xn"][:, m * 4096:(m + 1) * 4096], xn_bf[:, :])

    if BSTOP >= 8:
        while wj[0] < len(wmean_jobs):
            srcj, col = wmean_jobs[wj[0]]
            emit_wmean_job(srcj, col)
            wj[0] += 1
        while tj[0] < 64:
            emit_wtern_job(tj[0])
            tj[0] += 1

        emit_wscale(64, INNER * E, 2, 4)
        emit_wscale(128, INNER * E, 3, 5)
        if "wsc" in dbg:
            p.wait("sp", "dve")
            p.dma("sp", "dsp", dbg["wsc"], wsc[:, 0:8])
            p.dma("sp", "dsp", dbg["tinyd"], tiny[0:1, :])
            p.dma("sp", "dsp", dbg["waccd"], wacc[:, :])
            if "spdbg" in dbg:
                p.dma("sp", "dsp", dbg["spdbg"], t2d[0:32, 0:16])
                p.wait("gp", "dve")
                p.dma("gp", "dgp", dbg["wabdbg"], wab_bf[0:32, 0:4])
                p.dma("sp", "dsp", dbg["t1dbg"], t1d[:, 0:64])

    if "sa" in dbg:
        p.wait("sp", "dve")
        p.dma("sp", "dsp", dbg["sa"][:, :], sin_f)

    if STAGE < 3:
        for c in range(8):
            p.dma("sp", "dsp", y[c * 128:(c + 1) * 128, :], FA[:, 0:2048])
        p.wdma_all("sp", "dsp")
        p.wdma_all("gp", "dgp")
        _emit_all(nc, p)
        bps.close()
        abps.close()
        ab.close()
        octx.close()
        return nc

    fin = {e: p.cnt[e] for e in ("pe", "act", "dve", "pool")}
    for e in ("pe", "act", "dve", "pool", "sp", "gp"):
        for s in ("pe", "act", "dve", "pool"):
            if s != e:
                p.wait(e, s, fin[s])
        p.wdma_all(e, "dsp")
        p.wdma_all(e, "dgp")

    bps.close()
    abps.close()
    ab.close()

    cs = ExitStack()

    def csb(name, shape, dt=F32):
        return cs.enter_context(nc.sbuf_tensor(name, shape, dt))

    wg_sb = csb("wg_sb", [128, 8 * INNER], BF16)
    junkc = csb("junkc", [128, 1024])
    junk2 = csb("junk2", [128, 1024])
    wstr = csb("wstr", [128, 2048])
    xmid_c = csb("xmid_c", [128, 2048])
    hbuf = csb("hbuf", [128, 8192])
    hq = csb("hq", [128, 8192], BF16)
    xq_deq = csb("xq_deq", [128, 2048], BF16)
    xq_tmp = csb("xq_tmp", [128, 256], BF16)
    gsil = csb("gsil", [128, 256])
    wd_buf = csb("wd_buf", [128, 2048], BF16)
    wu_buf = csb("wu_buf", [128, 2048], BF16)
    rmx = csb("rmx", [128, 1024])
    rmn = csb("rmn", [128, 1024])
    fold = csb("fold", [64, 512])
    fold_bf = csb("fold_bf", [32, 512], BF16)
    tmv = csb("tmv", [128, 16])
    tmv_bf = csb("tmv_bf", [128, 8], BF16)
    rowv_bf = csb("rowv_bf", [1, 1024], BF16)
    bs1 = csb("bs1", [128, 256])
    bsi1 = csb("bsi1", [128, 256])
    bs2 = csb("bs2", [128, 256])
    bsd = csb("bsd", [128, 256])
    ytile = csb("ytile", [128, 2048])
    lnr2 = csb("lnr2", [1, 1280])
    lnr2_bf = csb("lnr2_bf", [1, 1024], BF16)
    br2 = csb("br2", [128, 256])
    bm2 = csb("bm2", [128, 256])
    tscr = csb("tscr", [128, 512])
    oscr = csb("oscr", [128, 512])

    cps = ExitStack()
    ps_GU = cps.enter_context(nc.psum_tensor("ps_GU", [128, 512], F32))
    ps_GU2 = cps.enter_context(nc.psum_tensor("ps_GU2", [128, 512], F32))
    ps_DN = cps.enter_context(nc.psum_tensor("ps_DN", [128, 2048], F32))
    ps_TP = cps.enter_context(nc.psum_tensor("ps_TP", [128, 512], BF16))

    def spacer_c(e):
        return e.tensor_scalar(junkc[:, 256:512], junkc[:, 0:256], 1.0, None,
                               OP.mult)

    cln_bf = hq[:, 0:2048]
    sq3_bf = hq[:, 2048:4096]

    wg_flatC = wgT.rearrange("a b -> (a b)").rearrange("(c p t) -> c p t",
                                                       p=128, t=1024)
    ca_hist = []
    for ci in range(32):
        buf = wstr[:, (ci % 2) * 1024:(ci % 2) * 1024 + 1024]
        jb = junkc if ci % 2 == 0 else junk2
        q, dq = ("sp", "dsp") if ci % 2 == 0 else ("gp", "dgp")
        if ci >= 2:
            p.wait(q, "dve", ca_hist[ci - 2])
        else:
            p.wait(q, "dve")
            p.wait(q, "pool")
        hb = p.dma(q, dq, buf, wg_flatC[ci, :, :])
        p.wdma("dve", hb)
        p.wait("dve", "pool")

        def ca(e, buf=buf, jb=jb):
            e.scalar_tensor_tensor(jb[:, 0:512], buf[:, 0:512], wsc[:, 2:3],
                                   mrow[:, :], OP.mult, OP.add)
            return e.scalar_tensor_tensor(jb[:, 512:1024], buf[:, 512:1024],
                                          wsc[:, 2:3], mrow[:, :], OP.mult, OP.add)
        ca_hist.append(p.step("dve", ca))
        p.wait("pool", "dve")

        def cb(e, ci=ci, jb=jb):
            e.tensor_scalar(oscr[:, 0:512], jb[:, 0:512], MRND + 1.0, MRND - 1.0,
                            OP.min, OP.max)
            e.tensor_scalar(oscr[:, 0:512], jb[:, 512:1024], MRND + 1.0, MRND - 1.0,
                            OP.min, OP.max) if False else None
            e.tensor_scalar(
                wg_sb[:, :].rearrange("p (c t) -> p c t", c=32)[:, ci, 0:512],
                oscr[:, 0:512], -MRND, None, OP.add)
            e.tensor_scalar(oscr[:, 0:512], jb[:, 512:1024], MRND + 1.0, MRND - 1.0,
                            OP.min, OP.max)
            return e.tensor_scalar(
                wg_sb[:, :].rearrange("p (c t) -> p c t", c=32)[:, ci, 512:1024],
                oscr[:, 0:512], -MRND, None, OP.add)
        p.step("pool", cb)
    for oc in range(32):
        buf = wstr[:, (oc % 2) * 1024:(oc % 2) * 1024 + 1024]
        jb = (junkc if oc % 2 == 0 else junk2)[:, 0:1024]
        tb = wu_buf[:, (oc % 2) * 1024:(oc % 2) * 1024 + 1024]
        q, dq = ("sp", "dsp") if oc % 2 == 0 else ("gp", "dgp")
        if oc >= 2:
            p.wait(q, "dve", ca_hist[32 + oc - 2])
        else:
            p.wait(q, "dve")
            p.wait(q, "pool")
        hb = p.dma(q, dq, buf,
                   wuSL[oc, :, :].rearrange("(f p) o -> p f o", p=128))
        p.wdma("dve", hb)
        p.wait("dve", "pool")

        def cua(e, buf=buf, jb=jb):
            e.scalar_tensor_tensor(jb[:, 0:512], buf[:, 0:512], wsc[:, 3:4],
                                   mrow[:, :], OP.mult, OP.add)
            return e.scalar_tensor_tensor(jb[:, 512:1024], buf[:, 512:1024],
                                          wsc[:, 3:4], mrow[:, :], OP.mult, OP.add)
        ca_hist.append(p.step("dve", cua))
        p.wait("pool", "dve")

        def cub(e, jb=jb, tb=tb):
            e.tensor_scalar(oscr[:, 0:512], jb[:, 0:512], MRND + 1.0, MRND - 1.0,
                            OP.min, OP.max)
            e.tensor_scalar(tb[:, 0:512], oscr[:, 0:512], -MRND, None, OP.add)
            e.tensor_scalar(oscr[:, 0:512], jb[:, 512:1024], MRND + 1.0, MRND - 1.0,
                            OP.min, OP.max)
            return e.tensor_scalar(tb[:, 512:1024], oscr[:, 0:512], -MRND, None,
                                   OP.add)
        v = p.step("pool", cub)
        p.wait("sp", "pool", v)
        p.dma("sp", "dsp", wu_t[oc, :, :], tb)

    def emit_scales(chunks, s_dst, inv_dst):
        n = len(chunks)
        p.wait("dve", "pool")

        def mm0(e):
            e.tensor_copy(rmx[:, 0:256], chunks[0])
            return e.tensor_copy(rmn[:, 0:256], chunks[0])
        p.step("dve", mm0)
        cur = 0
        for i in range(1, n):
            nxt = (cur + 1) % 2

            def mm(e, i=i, cur=cur, nxt=nxt):
                e.tensor_tensor(rmx[:, nxt * 256:(nxt + 1) * 256],
                                rmx[:, cur * 256:(cur + 1) * 256],
                                chunks[i], OP.max)
                return e.tensor_tensor(rmn[:, nxt * 256:(nxt + 1) * 256],
                                       rmn[:, cur * 256:(cur + 1) * 256],
                                       chunks[i], OP.min)
            p.step("dve", mm)
            cur = nxt

        def comb(e, cur=cur):
            e.scalar_tensor_tensor(rmx[:, 512:768], rmn[:, cur * 256:(cur + 1) * 256],
                                   -1.0, rmx[:, cur * 256:(cur + 1) * 256],
                                   OP.mult, OP.max)
            e.tensor_copy(fold[0:64, 0:256], rmx[64:128, 512:768])
            e.tensor_tensor(fold[0:64, 256:512], rmx[0:64, 512:768],
                            fold[0:64, 0:256], OP.max)
            e.tensor_copy(rmx[0:32, 768:1024], fold[32:64, 256:512])
            e.tensor_tensor(fold[0:32, 0:256], fold[0:32, 256:512],
                            rmx[0:32, 768:1024], OP.max)
            e.tensor_copy(fold_bf[:, 0:256], fold[0:32, 0:256])
            e.tensor_tensor(fold[0:32, 256:512], fold[0:32, 0:256],
                            fold_bf[:, 0:256], OP.subtract)
            return e.tensor_copy(fold_bf[:, 256:512], fold[0:32, 256:512])
        p.step("dve", comb)
        p.wait("pe", "dve")

        def tp1(e):
            e.transpose(ps_TP[:, 0:32], fold_bf[:, 0:128], ident[0:32, 0:32])
            e.transpose(ps_TP[:, 64:96], fold_bf[:, 256:384], ident[0:32, 0:32])
            e.transpose(ps_TP[:, 32:64], fold_bf[:, 128:256], ident[0:32, 0:32])
            return e.transpose(ps_TP[:, 96:128], fold_bf[:, 384:512],
                               ident[0:32, 0:32])
        p.step("pe", tp1)
        p.wait("dve", "pe")

        def tp1e(e):
            e.tensor_copy(tscr[:, 0:32], ps_TP[:, 0:32])
            e.tensor_copy(tscr[:, 64:96], ps_TP[:, 64:96])
            e.tensor_copy(tscr[:, 32:64], ps_TP[:, 32:64])
            e.tensor_copy(tscr[:, 96:128], ps_TP[:, 96:128])
            spacer_c(e)
            e.tensor_tensor(tscr[:, 128:192], tscr[:, 0:64], tscr[:, 64:128],
                            OP.add)
            spacer_c(e)
            e.reduce_max(tmv[:, 0:1], tscr[:, 128:160], axis=AX)
            e.reduce_max(tmv[:, 1:2], tscr[:, 160:192], axis=AX)
            spacer_c(e)
            e.tensor_copy(tmv_bf[:, 0:2], tmv[:, 0:2])
            spacer_c(e)
            e.scalar_tensor_tensor(tmv[:, 2:4], tmv_bf[:, 0:2], -1.0,
                                   tmv[:, 0:2], OP.mult, OP.add)
            spacer_c(e)
            return e.tensor_copy(tmv_bf[:, 2:4], tmv[:, 2:4])
        p.step("dve", tp1e)
        p.wait("pe", "dve")

        def tpb(e):
            e.transpose(ps_TP[0:1, 0:128], tmv_bf[:, 0:1], ident[:, :])
            e.transpose(ps_TP[0:1, 128:256], tmv_bf[:, 1:2], ident[:, :])
            e.transpose(ps_TP[0:1, 256:384], tmv_bf[:, 2:3], ident[:, :])
            return e.transpose(ps_TP[0:1, 384:512], tmv_bf[:, 3:4], ident[:, :])
        p.step("pe", tpb)
        p.wait("dve", "pe")

        def chain(e):
            S = [lnr2[0:1, i * 256:(i + 1) * 256] for i in range(5)]
            e.tensor_copy(S[0], ps_TP[0:1, 256:512])
            e.tensor_tensor(S[1], ps_TP[0:1, 0:256], S[0], OP.add)
            e.tensor_scalar(S[2], S[1], 1e-5, None, OP.max)
            e.reciprocal(S[3], S[2])
            e.tensor_scalar(S[4], S[3], 127.0, None, OP.mult)
            e.tensor_scalar(S[1], S[2], 1.0 / 127.0, None, OP.mult)
            e.tensor_copy(rowv_bf[0:1, 0:256], S[4])
            e.tensor_copy(rowv_bf[0:1, 512:768], S[1])
            e.scalar_tensor_tensor(S[0], rowv_bf[0:1, 0:256], -1.0, S[4],
                                   OP.mult, OP.add)
            return e.tensor_copy(rowv_bf[0:1, 256:512], S[0])
        p.step("dve", chain)
        p.wait("pe", "dve")

        def bcst(e):
            e.matmul(ps_GU[:, 0:256], ones_row[:, :], rowv_bf[0:1, 0:256],
                     start=True, stop=False)
            e.matmul(ps_GU[:, 0:256], ones_row[:, :], rowv_bf[0:1, 256:512],
                     start=False, stop=True)
            return e.matmul(ps_GU2[:, 0:256], ones_row[:, :], rowv_bf[0:1, 512:768],
                            start=True, stop=True)
        p.step("pe", bcst)
        p.wait("dve", "pe")

        def bcev(e):
            e.tensor_copy(s_dst[:, :], ps_GU[:, 0:256])
            return e.tensor_copy(inv_dst[:, :], ps_GU2[:, 0:256])
        v = p.step("dve", bcev)
        p.wait("pe", "dve", v)

    for u in range(NSUB):
        m, hh = u // 2, u % 2
        p.wait("dve", "pool")
        p.wait("pool", "dve")
        p.wait("pool", "pe")
        p.wait("act", "dve")
        xq_, dq_ = ("sp", "dsp") if u % 2 == 0 else ("gp", "dgp")
        p.wait(xq_, "dve")
        p.wait(xq_, "pool")
        if u == 0:
            p.wdma_all("sp", "dsp")
        h_xm = p.dma(xq_, dq_, xmid_c[:, :].rearrange("p (c t) -> p c t", c=8),
                     xmid_sp[:, :].rearrange("p (mm c t) -> p mm c t", mm=4, c=8)
                     [:, m, :, hh * 256:(hh + 1) * 256])
        p.wdma("dve", h_xm)
        p.wdma("pool", h_xm)

        xch = [xmid_c[:, c * 256:(c + 1) * 256] for c in range(8)]
        emit_scales(xch, bs1, bsi1)
        if u == DUMP_U and "bs1" in dbg:
            p.wait("sp", "dve")
            p.dma("sp", "dsp", dbg["bs1"], bs1[:, :])
            p.dma("sp", "dsp", dbg["tmv"], tmv[:, :])
            p.dma("sp", "dsp", dbg["habs"], rmx[:, 512:768])
            p.wait("gp", "dve")
            p.dma("gp", "dgp", dbg["bsi1"], bsi1[:, :])
            if "rowv" in dbg:
                p.dma("gp", "dgp", dbg["rowv"], rowv_bf[:, :])
                p.dma("gp", "dgp", dbg["tmvbf"], tmv_bf[:, :])

        for c in range(8):
            def xq1(e, c=c):
                e.tensor_tensor(tscr[:, 0:256], xch[c], bs1[:, :], OP.mult)
                e.tensor_scalar(tscr[:, 256:512], tscr[:, 0:256], MRND, MRND + 127.0,
                                OP.add, OP.min)
                e.tensor_scalar(xq_tmp[:, :], tscr[:, 256:512], MRND - 128.0, -MRND,
                                OP.max, OP.add)
                return e.tensor_tensor(xq_deq[:, c * 256:(c + 1) * 256], xq_tmp[:, :],
                                       bsi1[:, :], OP.mult)
            p.step("dve", xq1)

        p.wait("pe", "dve")
        if u == 0:
            p.wait("pe", "pool")
            p.wdma_all("gp", "dsp")
        gu_hist = []
        for oc in range(32):
            slab = wu_buf[:, (oc % 2) * 1024:(oc % 2) * 1024 + 1024]
            q, dq = ("sp", "dsp") if oc % 2 == 0 else ("gp", "dgp")
            if oc >= 2:
                p.wait(q, "pe", gu_hist[oc - 2])
            else:
                p.wait(q, "pe")
            h_sl = p.dma(q, dq, slab, wu_t[oc, :, :])
            pst = ps_GU if oc % 2 == 0 else ps_GU2
            p.wdma("pe", h_sl)
            if oc >= 2:
                p.wait("pe", "dve")

            col = (oc % 2) * 256

            def gu(e, oc=oc, col=col, slab=slab):
                ins = None
                for f in range(8):
                    e.matmul(ps_GU[:, col:col + 256],
                             wg_sb[:, f * INNER + oc * 128: f * INNER + (oc + 1) * 128],
                             xq_deq[:, f * 256:(f + 1) * 256],
                             start=(f == 0), stop=(f == 7))
                    ins = e.matmul(ps_GU2[:, col:col + 256], slab[:, f * 128:(f + 1) * 128],
                                   xq_deq[:, f * 256:(f + 1) * 256],
                                   start=(f == 0), stop=(f == 7))
                return ins
            gu_hist.append(p.step("pe", gu))
            p.wait("act", "pe")
            p.wait("act", "dve")

            def sil(e, col=col):
                return e.activation(gsil[:, :], ps_GU[:, col:col + 256], AF.Silu,
                                    scale=wsc[:, 4:5])
            sv = p.step("act", sil)
            p.wait("dve", "act", sv)

            def hm(e, oc=oc, col=col):
                return e.scalar_tensor_tensor(hbuf[:, oc * 256:(oc + 1) * 256],
                                              ps_GU2[:, col:col + 256], wsc[:, 5:6],
                                              gsil[:, :], OP.mult, OP.mult)
            p.step("dve", hm)

        hch = [hbuf[:, c * 256:(c + 1) * 256] for c in range(32)]
        emit_scales(hch, bs2, bsd)

        p.wait("pool", "dve")
        for c in range(32):
            eng = "dve" if c % 2 == 0 else "pool"
            tb = tscr if eng == "dve" else oscr

            def hqf(e, c=c, tb=tb):
                e.tensor_tensor(tb[:, 0:256], hch[c], bs2[:, :], OP.mult)
                e.tensor_scalar(tb[:, 256:512], tb[:, 0:256], MRND, MRND + 127.0,
                                OP.add, OP.min)
                return e.tensor_scalar(hq[:, c * 256:(c + 1) * 256], tb[:, 256:512],
                                       MRND - 128.0, -MRND, OP.max, OP.add)
            p.step(eng, hqf)
        p.wait("pe", "dve")
        p.wait("pe", "pool")

        dn_hist = []
        for mi in range(32):
            slab = wd_buf[:, (mi % 2) * 1024:(mi % 2) * 1024 + 1024]
            q, dq = ("sp", "dsp") if mi % 2 == 0 else ("gp", "dgp")
            if mi >= 2:
                p.wait(q, "pe", dn_hist[mi - 2])
            else:
                p.wait(q, "pe")
            h_sl = p.dma(q, dq, slab, wd_t[mi * 128:(mi + 1) * 128, :])
            p.wdma("pe", h_sl)

            def dn(e, mi=mi, slab=slab):
                if mi == 0:
                    for b in range(4):
                        e.matmul(ps_DN[:, b * 512:(b + 1) * 512], ones_row[:, :],
                                 zrow_bf[0:1, 0:512], start=True, stop=True)
                ins = None
                for oc in range(8):
                    ins = e.matmul(ps_DN[:, oc * 256:(oc + 1) * 256],
                                   slab[:, oc * 128:(oc + 1) * 128],
                                   hq[:, mi * 256:(mi + 1) * 256],
                                   start=False, stop=(mi == 31))
                return ins
            dn_hist.append(p.step("pe", dn))

        p.wait("dve", "pe")
        for c in range(8):
            def r2f(e, c=c):
                e.scalar_tensor_tensor(tscr[:, 0:256], ps_DN[:, c * 256:(c + 1) * 256],
                                       wsc[:, 1:2], bsd[:, :], OP.mult, OP.mult)
                return e.tensor_tensor(ytile[:, c * 256:(c + 1) * 256],
                                       tscr[:, 0:256], xch[c], OP.add)
            p.step("dve", r2f)

        if u == DUMP_U:
            if "xq0" in dbg:
                p.wait("gp", "dve")
                p.dma("gp", "dgp", dbg["xq0"][:, :], xq_deq[:, :])
            if "h0" in dbg:
                p.wait("sp", "dve")
                p.dma("sp", "dsp", dbg["h0"][:, :], hbuf[:, :])
            if "hq0" in dbg:
                p.wait("gp", "dve")
                p.wait("gp", "pool")
                hqh = p.dma("gp", "dgp", dbg["hq0"][:, :], hq[:, :])
                p.wdma("dve", hqh)
            if "r20" in dbg:
                p.wait("sp", "dve")
                p.dma("sp", "dsp", dbg["r20"][:, :], ytile[:, :])
        for c in range(8):
            cl = slice(c * 256, (c + 1) * 256)

            def cp3(e, cl=cl):
                return e.tensor_copy(cln_bf[:, cl], ytile[:, cl])
            p.step("dve", cp3)
            p.wait("act", "dve")

            def sq3(e, cl=cl):
                return e.activation(sq3_bf[:, cl], ytile[:, cl], AF.Square)
            p.step("act", sq3)
        p.wait("pe", "dve")
        p.wait("pe", "act")

        def st3(e):
            ins = None
            for c in range(8):
                cl = slice(c * 256, (c + 1) * 256)
                e.matmul(ps_DN[0:1, 0:256], ones_col[:, :], cln_bf[:, cl],
                         start=(c == 0), stop=(c == 7))
                ins = e.matmul(ps_DN[0:1, 1024:1280], ones_col[:, :], sq3_bf[:, cl],
                               start=(c == 0), stop=(c == 7))
            return ins
        p.step("pe", st3)
        p.wait("dve", "pe")

        def chain3(e):
            L = lnr2
            mu, var, yy, tmp, mrs = (L[0:1, i * 256:(i + 1) * 256]
                                     for i in range(5))
            t2 = tmp
            e.tensor_scalar(mu, ps_DN[0:1, 0:256], 1.0 / E, None, OP.mult)
            e.tensor_scalar(tmp, ps_DN[0:1, 1024:1280], 1.0 / E, EPS, OP.mult, OP.add)
            e.tensor_tensor(var, mu, mu, OP.mult)
            e.tensor_tensor(var, tmp, var, OP.subtract)
            e.tensor_copy(lnr2_bf[0:1, 0:256], var)
            e.tensor_copy(tmp, lnr2_bf[0:1, 0:256].bitcast(U16))
            e.tensor_scalar(tmp, tmp, 0.5, MRND, OP.mult, OP.add)
            e.tensor_scalar(tmp, tmp, -MRND, None, OP.add)
            e.tensor_scalar(tmp, tmp, -1.0, 24375.0, OP.mult, OP.add)
            e.tensor_copy(lnr2_bf[0:1, 0:256].bitcast(U16), tmp)
            e.tensor_copy(yy, lnr2_bf[0:1, 0:256])
            for _ in range(3):
                e.tensor_tensor(tmp, yy, yy, OP.mult)
                e.tensor_tensor(tmp, tmp, var, OP.mult)
                e.tensor_scalar(tmp, tmp, -0.5, 1.5, OP.mult, OP.add)
                e.tensor_tensor(yy, yy, tmp, OP.mult)
            e.tensor_tensor(mrs, mu, yy, OP.mult)
            e.tensor_copy(lnr2_bf[0:1, 0:256], yy)
            e.tensor_tensor(t2, yy, lnr2_bf[0:1, 0:256], OP.subtract)
            e.tensor_copy(lnr2_bf[0:1, 256:512], t2)
            e.tensor_copy(lnr2_bf[0:1, 512:768], mrs)
            e.tensor_tensor(t2, mrs, lnr2_bf[0:1, 512:768], OP.subtract)
            return e.tensor_copy(lnr2_bf[0:1, 768:1024], t2)
        p.step("dve", chain3)
        p.wait("pe", "dve")

        def bc3(e):
            e.matmul(ps_DN[:, 256:512], ones_row[:, :], lnr2_bf[0:1, 0:256],
                     start=True, stop=False)
            e.matmul(ps_DN[:, 256:512], ones_row[:, :], lnr2_bf[0:1, 256:512],
                     start=False, stop=True)
            e.matmul(ps_DN[:, 1280:1536], ones_row[:, :], lnr2_bf[0:1, 512:768],
                     start=True, stop=False)
            return e.matmul(ps_DN[:, 1280:1536], ones_row[:, :], lnr2_bf[0:1, 768:1024],
                            start=False, stop=True)
        p.step("pe", bc3)
        p.wait("dve", "pe")

        def bev3(e):
            e.tensor_copy(br2[:, :], ps_DN[:, 256:512])
            return e.tensor_copy(bm2[:, :], ps_DN[:, 1280:1536])
        p.step("dve", bev3)
        p.wait("pool", "dve")

        for c in range(8):
            cl = slice(c * 256, (c + 1) * 256)
            eng = "dve" if c % 2 == 0 else "pool"
            tb = tscr if eng == "dve" else oscr

            def ap3(e, cl=cl, tb=tb):
                e.tensor_tensor(tb[:, 0:256], ytile[:, cl], br2[:, :], OP.mult)
                return e.tensor_tensor(tb[:, 256:512], tb[:, 0:256], bm2[:, :],
                                       OP.subtract)
            v = p.step(eng, ap3)
            p.wait("sp", eng, v)
            h_y = p.dma("sp", "dsp",
                        y[c * 128:(c + 1) * 128, u * 256:(u + 1) * 256], tb[:, 256:512])
            if c < 6:
                p.wdma(eng, h_y)

    p.wdma_all("sp", "dsp")
    p.wdma_all("gp", "dgp")

    _emit_all(nc, p)
    cps.close()
    cs.close()
    octx.close()
    return nc


def _emit_all(nc, p):
    with nc.Block() as block:
        @block.tensor
        def _(e):
            for _, fn in p.ops["pe"]:
                fn(e)

        @block.scalar
        def _(e):
            e.preload_act_table(AF.Silu)
            for _, fn in p.ops["act"]:
                fn(e)

        @block.vector
        def _(e):
            for _, fn in p.ops["dve"]:
                fn(e)

        @block.gpsimd
        def _(e):
            merged = sorted(p.ops["gp"] + p.ops["pool"], key=lambda t: t[0])
            for _, fn in merged:
                fn(e)

        @block.sync
        def _(e):
            for _, fn in p.ops["sp"]:
                fn(e)


def _host_inputs(full):
    x = np.asarray(full["L0_attn_x"], dtype=np.float32)
    BASE = 10000.0
    invf = BASE ** (-(np.arange(32, dtype=np.float64) * 2.0) / 64.0)
    freqs = invf[np.arange(128) % 32]
    sgn = np.where((np.arange(128) // 32) % 2 == 0, -1.0, 1.0)
    wu = np.asarray(full["L0_ffn_w_up"], np.float32)
    shared = {
        "Wq": np.ascontiguousarray(full["L0_attn_Wq"], dtype=np.float32),
        "Wk": np.ascontiguousarray(full["L0_attn_Wk"], dtype=np.float32),
        "Wv": np.ascontiguousarray(full["L0_attn_Wv"], dtype=np.float32),
        "Wo": np.ascontiguousarray(full["L0_attn_Wo"], dtype=np.float32),
        "Wg1": np.ascontiguousarray(full["L0_attn_Wg1"], dtype=np.float32),
        "Wg2": np.ascontiguousarray(full["L0_attn_Wg2"], dtype=np.float32),
        "wgT": np.ascontiguousarray(
            np.asarray(full["L0_ffn_w_gate"], np.float32).T
            .reshape(8, 128, INNER).transpose(0, 2, 1)
            .reshape(32, 1024, 128).transpose(0, 2, 1).reshape(E, INNER)),
        "wuSL": np.ascontiguousarray(
            wu.T.reshape(E, 32, 128).transpose(1, 0, 2)),
        "wdT": np.ascontiguousarray(np.asarray(full["L0_ffn_w_down"], np.float32).T),
        "gam": np.ascontiguousarray(
            np.asarray(full["L0_attn_gamma"], np.float32).reshape(8, 128).T),
        "bet": np.ascontiguousarray(
            np.asarray(full["L0_attn_beta"], np.float32).reshape(8, 128).T),
    }
    in_maps = []
    for c in range(8):
        b, j = c // 4, c % 4
        pos = np.arange(j * TOK, (j + 1) * TOK, dtype=np.float64)
        ang = freqs[:, None] * pos[None, :]
        im = dict(shared)
        im["xT"] = np.ascontiguousarray(x[b, j * TOK:(j + 1) * TOK, :].T)
        im["cosd"] = np.cos(ang).astype(np.float32)
        im["ssin"] = (sgn[:, None] * np.sin(ang)).astype(np.float32)
        sel = np.zeros((64, 8), np.float32)
        for r in range(8):
            if r // 4 == b and r % 4 < j:
                sel[:, r] = 1.0
        im["selc"] = sel
        in_maps.append(im)
    return in_maps


def kernel(**inputs):
    nc = build_program(DUMPS)
    in_maps = _host_inputs(inputs)
    res = run_bass_kernel_spmd(nc, in_maps, core_ids=list(range(8)))
    out = np.empty((2, 8192, 1024), np.float32)
    for c in range(8):
        b, j = c // 4, c % 4
        out[b, j * TOK:(j + 1) * TOK, :] = res.results[c]["y"].T
    return out


# version 13
# speedup vs baseline: 1.0114x; 1.0010x over previous
import numpy as np
import concourse.bass as bass
import concourse.mybir as mybir
from concourse.bass_utils import run_bass_kernel_spmd

F32 = mybir.dt.float32
BF16 = mybir.dt.bfloat16
I32 = mybir.dt.int32
U16 = mybir.dt.uint16
AF = mybir.ActivationFunctionType
OP = mybir.AluOpType
AX = mybir.AxisListType.X

MRND = 1.5 * 2.0 ** 23
E = 1024
HCNT = 16
D = 64
TOK = 2048
NMAC = 4
MT = 512
NSUB = 8
ST = 256
INNER = 4096
EPS = 1e-5

DUMPS = ()
STAGE = 3
BSTOP = 8
DUMP_U = 0
ASTOP = 5


class P:
    ENG = ("pe", "act", "dve", "pool", "sp", "gp")

    def __init__(self, nc):
        self.nc = nc
        self.ops = {e: [] for e in self.ENG}
        self.cnt = {}
        self.sems = {}
        self.seq = 0

    def _app(self, eng, fn):
        self.ops[eng].append((self.seq, fn))
        self.seq += 1

    def sem(self, name):
        h = self.nc.alloc_semaphore(name=name)
        self.sems[name] = h
        self.cnt[name] = 0

    def wait(self, eng, sname, val=None):
        v = self.cnt[sname] if val is None else val
        if v <= 0:
            return
        h = self.sems[sname]
        self._app(eng, lambda e: e.wait_ge(h, v))

    def step(self, eng, build_fn, sname=None):
        sname = sname or eng
        self.cnt[sname] += 1
        h = self.sems[sname]

        def fn(e):
            build_fn(e).then_inc(h, 1)
        self._app(eng, fn)
        return self.cnt[sname]

    def plain(self, eng, build_fn):
        self._app(eng, build_fn)

    NPOOL = 16

    def dma(self, eng, q, out_ap, in_ap):
        i = self.rot[q]
        self.rot[q] = (i + 1) % self.NPOOL
        s = f"{q}{i}"
        h = self.sems[s]
        prev = self.cnt[s]
        if prev > 0:
            self._app(eng, lambda e: e.wait_ge(h, prev))
        self.cnt[s] = prev + 16
        tgt = self.cnt[s]

        def fn(e):
            e.dma_start(out=out_ap, in_=in_ap).then_inc(h, 16)
        self._app(eng, fn)
        return (s, tgt)

    def wdma(self, eng, *handles):
        for s, tgt in handles:
            self.wait(eng, s, tgt)

    def wdma_all(self, eng, q):
        for i in range(self.NPOOL):
            self.wait(eng, f"{q}{i}")


def build_program(dumps=()):
    nc = bass.Bass("TRN2", target_bir_lowering=False)
    p = P(nc)

    xT = nc.dram_tensor("xT", [E, TOK], F32, kind="ExternalInput").ap()
    Wq = nc.dram_tensor("Wq", [E, E], F32, kind="ExternalInput").ap()
    Wk = nc.dram_tensor("Wk", [E, E], F32, kind="ExternalInput").ap()
    Wv = nc.dram_tensor("Wv", [E, E], F32, kind="ExternalInput").ap()
    Wo = nc.dram_tensor("Wo", [E, E], F32, kind="ExternalInput").ap()
    Wg1 = nc.dram_tensor("Wg1", [E, D], F32, kind="ExternalInput").ap()
    Wg2 = nc.dram_tensor("Wg2", [D, E], F32, kind="ExternalInput").ap()
    wgT = nc.dram_tensor("wgT", [E, INNER], F32, kind="ExternalInput").ap()
    wuSL = nc.dram_tensor("wuSL", [32, E, 128], F32, kind="ExternalInput").ap()
    wdT = nc.dram_tensor("wdT", [INNER, E], F32, kind="ExternalInput").ap()
    gam = nc.dram_tensor("gam", [128, 8], F32, kind="ExternalInput").ap()
    bet = nc.dram_tensor("bet", [128, 8], F32, kind="ExternalInput").ap()
    cosd = nc.dram_tensor("cosd", [128, TOK], F32, kind="ExternalInput").ap()
    ssin = nc.dram_tensor("ssin", [128, TOK], F32, kind="ExternalInput").ap()
    selc = nc.dram_tensor("selc", [64, 8], F32, kind="ExternalInput").ap()
    y = nc.dram_tensor("y", [E, TOK], F32, kind="ExternalOutput").ap()

    kfm_sp = nc.dram_tensor("kfm_sp", [128, 16384], BF16).ap()
    ktm_sp = nc.dram_tensor("ktm_sp", [128, 16384], BF16).ap()
    v_sp = nc.dram_tensor("v_sp", [128, 16384], BF16).ap()
    xmid_sp = nc.dram_tensor("xmid_sp", [128, 16384], F32).ap()
    wd_t = nc.dram_tensor("wd_t", [INNER, E], BF16).ap()
    wu_t = nc.dram_tensor("wu_t", [32, 128, E], BF16).ap()
    cc_in = nc.dram_tensor("cc_in_t", [64, 1024], F32).ap()
    cc_out = nc.dram_tensor("cc_out_t", [512, 1024], F32,
                            addr_space="Shared").ap()

    dbg = {}
    for name, shape in dumps:
        dbg[name] = nc.dram_tensor("dbg_" + name, list(shape), F32,
                                   kind="ExternalOutput").ap()

    for s in ("cc", "pe", "act", "dve", "pool", "gpc"):
        p.sem(s)
    p.rot = {"dsp": 0, "dgp": 0}
    for q in ("dsp", "dgp"):
        for i in range(P.NPOOL):
            p.sem(f"{q}{i}")

    from contextlib import ExitStack
    octx = ExitStack()

    def osb(name, shape, dt=F32):
        return octx.enter_context(nc.sbuf_tensor(name, shape, dt))

    ident = osb("ident", [128, 128], BF16)
    mask4 = osb("mask4", [128, 512], F32)
    ones_col = osb("ones_col", [128, 1], BF16)
    ones_row = osb("ones_row", [1, 128], BF16)
    gam_sb = osb("gam_sb", [128, 8])
    bet_sb = osb("bet_sb", [128, 8])
    mcol = osb("mcol", [128, 1])
    mrow = osb("mrow", [128, 512], BF16)
    wsc = osb("wsc", [128, 8])
    wacc = osb("wacc", [128, 192])
    wab_bf = osb("wab_bf", [128, 128], BF16)
    tiny = osb("tiny", [1, 16])
    tiny_bf = osb("tiny_bf", [1, 16], BF16)
    selc_sb = osb("selc_sb", [64, 8])
    zrow_bf = osb("zrow_bf", [1, 512], BF16)
    cc_bf = osb("cc_bf", [64, 1024], BF16)
    seldg = osb("seldg", [64, 512], BF16)

    def setup(g):
        g.memset(ident[:, :], 0.0)
        g.affine_select(out=ident[:, :], in_=ident[:, :], compare_op=OP.not_equal,
                        fill=1.0, base=0, pattern=[[-1, 128]], channel_multiplier=1)
        g.memset(mask4[:, :], 1.0)
        g.affine_select(out=mask4[:, :].rearrange("p (r n) -> p r n", r=4),
                        in_=mask4[:, :].rearrange("p (r n) -> p r n", r=4),
                        compare_op=OP.is_ge, fill=0.0, base=0,
                        pattern=[[0, 4], [1, 128]], channel_multiplier=-1)
        g.memset(ones_col[:, :], 1.0)
        g.memset(ones_row[:, :], 1.0)
        g.memset(wacc[:, :], 0.0)
        g.memset(zrow_bf[:, :], 0.0)
        g.memset(mrow[:, :], MRND)
        g.memset(mcol[:, :], MRND).then_inc(p.sems["gpc"], 1)
    p.plain("gp", setup)
    p.cnt["gpc"] += 1
    h_cst = [p.dma("gp", "dgp", gam_sb[:, :], gam),
             p.dma("gp", "dgp", bet_sb[:, :], bet),
             p.dma("gp", "dgp", selc_sb[:, :], selc)]
    p.wait("dve", "gpc", 1)
    p.wait("pool", "gpc", 1)
    p.wait("act", "gpc", 1)
    p.wdma("dve", *h_cst)
    p.wdma("pe", *h_cst)

    ab = ExitStack()

    def absb(name, shape, dt=F32):
        return ab.enter_context(nc.sbuf_tensor(name, shape, dt))

    cos_m = absb("cos_m", [128, 512])
    sin_m = absb("sin_m", [128, 512])
    W1 = absb("W1", [128, 8192], BF16)
    W2 = absb("W2", [128, 8192], BF16)
    Wg1_sb = absb("Wg1_sb", [128, 512], BF16)
    Wg2_sb = absb("Wg2_sb", [64, 1024], BF16)
    xt_bf = absb("xt_bf", [128, 4096], BF16)
    FA = absb("FA", [128, 4096])
    FB = absb("FB", [128, 4096])
    FC = absb("FC", [128, 4096])
    krot = absb("krot", [128, 4096], BF16)
    ktm = absb("ktm", [128, 4096], BF16)
    vt = absb("vt", [128, 4096], BF16)
    kfmB = absb("kfmB", [128, 4096], BF16)
    q_oddM = absb("q_oddM", [64, 4096], BF16)
    k_oddM = absb("k_oddM", [64, 4096], BF16)
    eT_sb = absb("eT_sb", [128, 2048], BF16)
    S_bf = absb("S_bf", [64, 1024], BF16)
    stat_bf = absb("stat_bf", [128, 2048], BF16)
    xn_bf = absb("xn_bf", [128, 4096], BF16)
    swd = absb("swd", [128, 512])
    t1d = absb("t1d", [128, 512])
    t2d = absb("t2d", [128, 512])
    swp = absb("swp", [128, 512])
    t1p = absb("t1p", [128, 512])
    t2p = absb("t2p", [128, 512])
    xg1_bf = absb("xg1_bf", [64, 512], BF16)
    lnr = absb("lnr", [1, 2048])
    cc_sb = FB[0:64, 0:1024]
    sin_f = FB[0:64, 1024:2048]
    lnr_bf = absb("lnr_bf", [1, 2048], BF16)
    brstd = absb("brstd", [128, 512])
    bmrs = absb("bmrs", [128, 512])
    wmb = absb("wmb", [128, 1024])
    jA = absb("jA", [128, 512])

    wtb = absb("wtb", [128, 512], BF16)
    wsb_bf = absb("wsb_bf", [128, 2], BF16)
    oscr_b = absb("oscr_b", [128, 512])

    abps = ExitStack()
    ps_S = abps.enter_context(nc.psum_tensor("ps_S", [64, 1024], F32))
    ps_W1 = abps.enter_context(nc.psum_tensor("ps_W1", [128, 512], F32))
    ps_W2 = abps.enter_context(nc.psum_tensor("ps_W2", [128, 512], F32))

    aps = ExitStack()
    ps_V = aps.enter_context(nc.psum_tensor("ps_V", [128, 1024], F32))
    ps_T = aps.enter_context(nc.psum_tensor("ps_T", [128, 128], BF16))

    h_w = [p.dma("gp", "dgp", W1[:, :].rearrange("p (f o) -> p f o", f=8),
                 Wk.rearrange("(f p) o -> p f o", p=128)),
           p.dma("gp", "dgp", W2[:, :].rearrange("p (f o) -> p f o", f=8),
                 Wv.rearrange("(f p) o -> p f o", p=128)),
           p.dma("gp", "dgp", Wg1_sb[:, :].rearrange("p (f o) -> p f o", f=8),
                 Wg1.rearrange("(f p) o -> p f o", p=128)),
           p.dma("gp", "dgp", Wg2_sb[:, :], Wg2)]

    def emit_rotary(eng, cols, src, dst_bf, sw, t1, t2):
        def fn(e):
            e.tensor_copy(sw[0:32, :], src[32:64, cols])
            e.tensor_copy(sw[32:64, :], src[0:32, cols])
            e.tensor_copy(sw[64:96, :], src[96:128, cols])
            e.tensor_copy(sw[96:128, :], src[64:96, cols])
            e.tensor_tensor(t1[:, :], src[:, cols], cos_m[:, :], OP.mult)
            e.tensor_tensor(t2[:, :], sw[:, :], sin_m[:, :], OP.mult)
            return e.tensor_tensor(dst_bf[:, cols], t1[:, :], t2[:, :], OP.add)
        p.step(eng, fn)

    def spacer(e):
        return e.tensor_scalar(jA[:, 256:512], jA[:, 0:256], 1.0, None, OP.mult)

    wj_hist = []

    def emit_wmean_job(src_ap, jcol):
        j = emit_wmean_job.i
        emit_wmean_job.i += 1
        buf = wmb[:, (j % 2) * 512:(j % 2) * 512 + 512]
        q, dq = ("sp", "dsp") if j % 2 == 0 else ("gp", "dgp")
        if j >= 2:
            p.wait(q, wj_hist[j - 2][0], wj_hist[j - 2][1])
        else:
            p.wait(q, "act")
            p.wait(q, "dve")
        hb = p.dma(q, dq, buf, src_ap)
        p.wdma("act", hb)
        p.wait("act", "pool")

        def f1(e, buf=buf, jcol=jcol):
            return e.activation(jA[:, 0:512], buf, AF.Abs,
                                accum_out=wacc[:, jcol:jcol + 1])
        wj_hist.append(("act", p.step("act", f1)))
    emit_wmean_job.i = 0

    def emit_wscale(jc0, n_elems, scol, icol):
        p.wait("dve", "act")

        def sp1(e, jc0=jc0):
            a, b = t1d, t2d
            e.tensor_tensor(a[:, 0:32], wacc[:, jc0:jc0 + 32],
                            wacc[:, jc0 + 32:jc0 + 64], OP.add)
            spacer(e)
            e.tensor_tensor(a[:, 32:48], a[:, 0:16], a[:, 16:32], OP.add)
            spacer(e)
            e.tensor_tensor(a[:, 48:56], a[:, 32:40], a[:, 40:48], OP.add)
            spacer(e)
            e.tensor_tensor(a[:, 56:60], a[:, 48:52], a[:, 52:56], OP.add)
            spacer(e)
            e.tensor_tensor(a[:, 60:62], a[:, 56:58], a[:, 58:60], OP.add)
            spacer(e)
            e.tensor_tensor(a[:, 62:63], a[:, 60:61], a[:, 61:62], OP.add)
            spacer(e)
            e.tensor_copy(b[0:64, 0:1], a[64:128, 62:63])
            spacer(e)
            e.tensor_tensor(b[0:64, 1:2], a[0:64, 62:63], b[0:64, 0:1], OP.add)
            spacer(e)
            e.tensor_copy(b[0:32, 2:3], b[32:64, 1:2])
            spacer(e)
            e.tensor_tensor(b[0:32, 3:4], b[0:32, 1:2], b[0:32, 2:3], OP.add)
            spacer(e)
            e.tensor_scalar(b[0:32, 4:5], b[0:32, 3:4], MRND, None, OP.add)
            spacer(e)
            e.tensor_scalar(b[0:32, 10:11], b[0:32, 4:5], -MRND, None, OP.add)
            spacer(e)
            e.tensor_tensor(b[0:32, 5:6], b[0:32, 3:4], b[0:32, 10:11], OP.subtract)
            spacer(e)
            e.tensor_scalar(b[0:32, 6:7], b[0:32, 10:11], 1.0 / 256.0, MRND,
                            OP.mult, OP.add)
            spacer(e)
            e.tensor_scalar(b[0:32, 7:8], b[0:32, 6:7], -MRND, 256.0,
                            OP.add, OP.mult)
            spacer(e)
            e.tensor_tensor(b[0:32, 8:9], b[0:32, 10:11], b[0:32, 7:8], OP.subtract)
            spacer(e)
            e.tensor_copy(wab_bf[0:32, 0:1], b[0:32, 7:8])
            e.tensor_copy(wab_bf[0:32, 1:2], b[0:32, 8:9])
            e.tensor_copy(wab_bf[0:32, 2:3], b[0:32, 5:6])
            spacer(e)
            e.scalar_tensor_tensor(b[0:32, 9:10], wab_bf[0:32, 2:3], -1.0,
                                   b[0:32, 5:6], OP.mult, OP.add)
            spacer(e)
            return e.tensor_copy(wab_bf[0:32, 3:4], b[0:32, 9:10])
        p.step("dve", sp1)
        p.wait("pe", "dve")

        def sm(e):
            return e.matmul(ps_W1[0:1, 0:4], ones_col[0:32, 0:1],
                            wab_bf[0:32, 0:4], start=True, stop=True)
        p.step("pe", sm)
        p.wait("act", "pe")

        def sacc_f(e):
            return e.activation(jA[0:1, 0:4], ps_W1[0:1, 0:4], AF.Identity,
                                accum_out=tiny[0:1, 0:1])
        p.step("act", sacc_f)
        p.wait("dve", "act")

        def sc(e, n_elems=n_elems):
            e.tensor_scalar(tiny[0:1, 2:3], tiny[0:1, 0:1], 1.0 / n_elems, 1e-5,
                            OP.mult, OP.max)
            spacer(e)
            e.reciprocal(tiny[0:1, 3:4], tiny[0:1, 2:3])
            spacer(e)
            e.tensor_tensor(tiny[0:1, 5:6], tiny[0:1, 2:3], tiny[0:1, 3:4], OP.mult)
            spacer(e)
            e.tensor_scalar(tiny[0:1, 6:7], tiny[0:1, 5:6], -1.0, 2.0,
                            OP.mult, OP.add)
            spacer(e)
            e.tensor_tensor(tiny[0:1, 7:8], tiny[0:1, 3:4], tiny[0:1, 6:7], OP.mult)
            spacer(e)
            e.tensor_copy(tiny_bf[0:1, 0:1], tiny[0:1, 7:8])
            spacer(e)
            e.scalar_tensor_tensor(tiny[0:1, 4:5], tiny_bf[0:1, 0:1], -1.0,
                                   tiny[0:1, 7:8], OP.mult, OP.add)
            spacer(e)
            e.tensor_copy(tiny_bf[0:1, 1:2], tiny[0:1, 4:5])
            e.tensor_copy(tiny_bf[0:1, 2:3], tiny[0:1, 2:3])
            spacer(e)
            e.scalar_tensor_tensor(tiny[0:1, 9:10], tiny_bf[0:1, 1:2], -1.0,
                                   tiny[0:1, 4:5], OP.mult, OP.add)
            e.scalar_tensor_tensor(tiny[0:1, 8:9], tiny_bf[0:1, 2:3], -1.0,
                                   tiny[0:1, 2:3], OP.mult, OP.add)
            spacer(e)
            e.tensor_copy(tiny_bf[0:1, 4:5], tiny[0:1, 9:10])
            e.tensor_copy(tiny_bf[0:1, 3:4], tiny[0:1, 8:9])
            spacer(e)
            e.scalar_tensor_tensor(tiny[0:1, 10:11], tiny_bf[0:1, 3:4], -1.0,
                                   tiny[0:1, 8:9], OP.mult, OP.add)
            spacer(e)
            return e.tensor_copy(tiny_bf[0:1, 5:6], tiny[0:1, 10:11])
        p.step("dve", sc)
        p.wait("pe", "dve")

        def bc(e):
            e.matmul(ps_W2[:, 0:1], ones_row[:, :], tiny_bf[0:1, 0:1], start=True, stop=False)
            e.matmul(ps_W2[:, 0:1], ones_row[:, :], tiny_bf[0:1, 1:2], start=False, stop=False)
            e.matmul(ps_W2[:, 0:1], ones_row[:, :], tiny_bf[0:1, 4:5], start=False, stop=True)
            e.matmul(ps_W2[:, 1:2], ones_row[:, :], tiny_bf[0:1, 2:3],
                     start=True, stop=False)
            e.matmul(ps_W2[:, 1:2], ones_row[:, :], tiny_bf[0:1, 3:4],
                     start=False, stop=False)
            return e.matmul(ps_W2[:, 1:2], ones_row[:, :], tiny_bf[0:1, 5:6],
                            start=False, stop=True)
        p.step("pe", bc)
        p.wait("dve", "pe")

        def cv(e, scol=scol, icol=icol):
            e.tensor_copy(wsc[:, scol:scol + 1], ps_W2[:, 0:1])
            return e.tensor_copy(wsc[:, icol:icol + 1], ps_W2[:, 1:2])
        p.step("dve", cv)

    wd_flatI = wdT.rearrange("a b -> (a b)").rearrange("(c p t) -> c p t",
                                                        p=128, t=512)
    wd_flatO = wd_t.rearrange("a b -> (a b)").rearrange("(c p t) -> c p t",
                                                        p=128, t=512)

    def emit_wtern_job(j):
        ri = emit_wmean_job.i
        emit_wmean_job.i += 1
        buf = wmb[:, (ri % 2) * 512:(ri % 2) * 512 + 512]
        q, dq = ("sp", "dsp") if ri % 2 == 0 else ("gp", "dgp")
        if ri >= 2:
            p.wait(q, wj_hist[ri - 2][0], wj_hist[ri - 2][1])
        else:
            p.wait(q, "act")
            p.wait(q, "dve")
            p.wait(q, "pool")
        hb = p.dma(q, dq, buf, wd_flatI[j, :, :])
        p.wdma("dve", hb)
        p.wait("dve", "pool")

        def f1(e, buf=buf):
            return e.scalar_tensor_tensor(swd[:, :], buf, wsc[:, 0:1],
                                          mrow[:, :], OP.mult, OP.add)
        wj_hist.append(("dve", p.step("dve", f1)))
        p.wait("pool", "dve")

        def f2(e):
            e.tensor_scalar(oscr_b[:, :], swd[:, :], MRND + 1.0, MRND - 1.0,
                            OP.min, OP.max)
            return e.tensor_scalar(wtb[:, :], oscr_b[:, :], -MRND, None, OP.add)
        p.step("pool", f2)
        p.wait("sp", "pool")
        p.dma("sp", "dsp", wd_flatO[j, :, :], wtb[:, :])

    def emit_ln(src, gamma, out_bf=None, out_f32=None):
        for c in range(8):
            cl = slice(c * 512, (c + 1) * 512)
            r = (c % 2) * 512

            def cp(e, cl=cl, r=r):
                return e.tensor_copy(stat_bf[:, r:r + 512], src[:, cl])
            p.step("dve", cp)
            p.wait("act", "dve")

            def sq(e, cl=cl, r=r):
                return e.activation(stat_bf[:, 1024 + r:1024 + r + 512], src[:, cl],
                                    AF.Square)
            p.step("act", sq)
            p.wait("pe", "dve")
            p.wait("pe", "act")

            def st(e, c=c, r=r):
                e.matmul(ps_W1[0:1, :], ones_col[:, :], stat_bf[:, r:r + 512],
                         start=(c == 0), stop=(c == 7))
                return e.matmul(ps_W2[0:1, :], ones_col[:, :],
                                stat_bf[:, 1024 + r:1024 + r + 512],
                                start=(c == 0), stop=(c == 7))
            v = p.step("pe", st)
            if c < 7:
                p.wait("dve", "pe", v)
                p.wait("act", "pe", v)
        p.wait("dve", "pe")

        def chain(e):
            L = lnr
            mu, var, yy, tmp = (L[0:1, i * 512:(i + 1) * 512] for i in range(4))
            mrs = tmp
            t2 = var
            e.tensor_scalar(mu, ps_W1[0:1, :], 1.0 / E, None, OP.mult)
            e.tensor_scalar(tmp, ps_W2[0:1, :], 1.0 / E, EPS, OP.mult, OP.add)
            e.tensor_tensor(var, mu, mu, OP.mult)
            e.tensor_tensor(var, tmp, var, OP.subtract)
            e.tensor_copy(lnr_bf[0:1, 0:512], var)
            e.tensor_copy(tmp, lnr_bf[0:1, 0:512].bitcast(U16))
            e.tensor_scalar(tmp, tmp, 0.5, MRND, OP.mult, OP.add)
            e.tensor_scalar(tmp, tmp, -MRND, None, OP.add)
            e.tensor_scalar(tmp, tmp, -1.0, 24375.0, OP.mult, OP.add)
            e.tensor_copy(lnr_bf[0:1, 0:512].bitcast(U16), tmp)
            e.tensor_copy(yy, lnr_bf[0:1, 0:512])
            for _ in range(3):
                e.tensor_tensor(tmp, yy, yy, OP.mult)
                e.tensor_tensor(tmp, tmp, var, OP.mult)
                e.tensor_scalar(tmp, tmp, -0.5, 1.5, OP.mult, OP.add)
                e.tensor_tensor(yy, yy, tmp, OP.mult)
            e.tensor_tensor(mrs, mu, yy, OP.mult)
            e.tensor_copy(lnr_bf[0:1, 0:512], yy)
            e.tensor_tensor(t2, yy, lnr_bf[0:1, 0:512], OP.subtract)
            e.tensor_copy(lnr_bf[0:1, 512:1024], t2)
            e.tensor_copy(lnr_bf[0:1, 1024:1536], mrs)
            e.tensor_tensor(t2, mrs, lnr_bf[0:1, 1024:1536], OP.subtract)
            return e.tensor_copy(lnr_bf[0:1, 1536:2048], t2)
        p.step("dve", chain)
        p.wait("pe", "dve")

        def bc(e):
            e.matmul(ps_W1[:, :], ones_row[:, :], lnr_bf[0:1, 0:512], start=True, stop=False)
            e.matmul(ps_W1[:, :], ones_row[:, :], lnr_bf[0:1, 512:1024], start=False, stop=True)
            e.matmul(ps_W2[:, :], ones_row[:, :], lnr_bf[0:1, 1024:1536], start=True, stop=False)
            return e.matmul(ps_W2[:, :], ones_row[:, :], lnr_bf[0:1, 1536:2048],
                            start=False, stop=True)
        p.step("pe", bc)
        p.wait("dve", "pe")

        def bev(e):
            e.tensor_copy(brstd[:, :], ps_W1[:, :])
            return e.tensor_copy(bmrs[:, :], ps_W2[:, :])
        p.step("dve", bev)
        p.wait("pool", "dve")
        for c in range(8):
            cl = slice(c * 512, (c + 1) * 512)
            eng = "dve" if c % 2 == 0 else "pool"
            tb = t1d if eng == "dve" else t1p

            def ap_fn(e, cl=cl, c=c, tb=tb):
                e.tensor_tensor(tb[:, :], src[:, cl], brstd[:, :], OP.mult)
                if gamma:
                    e.tensor_tensor(tb[:, :], tb[:, :], bmrs[:, :], OP.subtract)
                    return e.tensor_scalar(out_bf[:, cl], tb[:, :],
                                           gam_sb[:, c:c + 1], bet_sb[:, c:c + 1],
                                           OP.mult, OP.add)
                dst = out_f32 if out_f32 is not None else out_bf
                return e.tensor_tensor(dst[:, cl], tb[:, :], bmrs[:, :], OP.subtract)
            p.step(eng, ap_fn)

    p.wait("pe", "gpc", 1)

    def s_dummy(e):
        e.matmul(ps_S[:, 0:512], ones_row[0:1, 0:64], zrow_bf[0:1, 0:512],
                 start=True, stop=True)
        return e.matmul(ps_S[:, 512:1024], ones_row[0:1, 0:64], zrow_bf[0:1, 0:512],
                        start=True, stop=True)
    p.step("pe", s_dummy)

    for m in range(NMAC):
        mc = slice(m * MT, (m + 1) * MT)
        p.wait("gp", "pe")
        h_xt = p.dma("gp", "dgp", xt_bf[:, :].rearrange("p (f t) -> p f t", f=8),
                     xT.rearrange("(f p) t -> p f t", p=128)[:, :, mc])
        p.wait("sp", "dve")
        p.wait("sp", "pool")
        h_tab = [p.dma("sp", "dsp", cos_m[:, :], cosd[:, mc]),
                 p.dma("sp", "dsp", sin_m[:, :], ssin[:, mc])]

        p.wdma("pe", h_xt)
        if m == 0:
            p.wdma("pe", *h_w)
            p.wait("pe", "gpc", 1)
        kev_hist = []
        for oc in range(8):
            pst = ps_W1 if oc % 2 == 0 else ps_W2

            def kproj(e, oc=oc, pst=pst):
                ins = None
                for f in range(8):
                    ins = e.matmul(pst[:, :],
                                   W1[:, f * 1024 + oc * 128: f * 1024 + (oc + 1) * 128],
                                   xt_bf[:, f * 512:(f + 1) * 512],
                                   start=(f == 0), stop=(f == 7))
                return ins
            if oc >= 2:
                p.wait("pe", "act", kev_hist[oc - 2])
            p.step("pe", kproj)
            p.wait("act", "pe")

            def ev(e, oc=oc, pst=pst):
                return e.activation(FA[:, oc * 512:(oc + 1) * 512], pst[:, :], AF.Silu)
            kev_hist.append(p.step("act", ev))

        p.wait("dve", "act")
        p.wdma("dve", *h_tab)
        p.wait("pool", "act")
        p.wdma("pool", *h_tab)
        if m > 0:
            p.wdma("dve", *h_spill)
            p.wdma("pool", *h_spill)
        for c in range(8):
            cols = slice(c * 512, (c + 1) * 512)
            if c < 5:
                emit_rotary("dve", cols, FA, krot, swd, t1d, t2d)
            else:
                emit_rotary("pool", cols, FA, krot, swp, t1p, oscr_b)

        p.wait("pe", "dve")
        p.wait("pe", "pool")
        for s in range(4):
            for c in range(8):
                def tp(e, s=s, c=c):
                    return e.transpose(ps_T[:, :],
                                       krot[:, c * 512 + s * 128: c * 512 + (s + 1) * 128],
                                       ident[:, :])
                p.step("pe", tp)
                p.wait("dve", "pe")

                def tpe(e, s=s, c=c):
                    return e.tensor_copy(
                        ktm[:, s * 1024 + c * 128: s * 1024 + (c + 1) * 128], ps_T[:, :])
                v = p.step("dve", tpe)
                p.wait("pe", "dve", v)

        p.wait("sp", "dve")
        p.wait("sp", "pool")
        h_spill = [p.dma("sp", "dsp", kfm_sp[:, m * 4096:(m + 1) * 4096], krot[:, :])]
        p.wait("gp", "dve")
        p.wait("gp", "pe")
        p.wait("gp", "pool")
        h_spill.append(p.dma("gp", "dgp", ktm_sp[:, m * 4096:(m + 1) * 4096],
                             ktm[:, :]))

        for s in range(4):
            def vproj(e, s=s):
                ins = None
                for f in range(8):
                    for hf in range(2):
                        ins = e.matmul(ps_V[:, hf * 512:(hf + 1) * 512],
                                       xt_bf[:, f * 512 + s * 128: f * 512 + (s + 1) * 128],
                                       W2[:, f * 1024 + hf * 512: f * 1024 + (hf + 1) * 512],
                                       start=(f == 0), stop=(f == 7))
                return ins
            if s >= 1 or m >= 1:
                p.wait("pe", "dve")
            p.step("pe", vproj)
            p.wait("dve", "pe")

            def vev(e, s=s):
                return e.tensor_copy(vt[:, s * 1024:(s + 1) * 1024], ps_V[:, :])
            p.step("dve", vev)

        p.wait("sp", "dve")
        h_spill.append(p.dma("sp", "dsp", v_sp[:, m * 4096:(m + 1) * 4096], vt[:, :]))
        if "vsp" in dbg:
            p.wait("gp", "dve")
            p.dma("gp", "dgp", dbg["vsp"][:, m * 4096:(m + 1) * 4096], vt[:, :])
        if "kfm" in dbg:
            p.wait("gp", "dve")
            p.wait("gp", "pool")
            p.dma("gp", "dgp", dbg["kfm"][:, m * 4096:(m + 1) * 4096], krot[:, :])
        if "ktm" in dbg:
            p.wait("gp", "dve")
            p.dma("gp", "dgp", dbg["ktm"][:, m * 4096:(m + 1) * 4096], ktm[:, :])

        p.wait("pe", "dve")

        def dsacc(e, m=m):
            ins = None
            for s in range(4):
                for h in range(HCNT):
                    ins = e.matmul(ps_S[:, h * 64:(h + 1) * 64],
                                   ktm[:, s * 1024 + h * 64: s * 1024 + (h + 1) * 64],
                                   vt[:, s * 1024 + h * 64: s * 1024 + (h + 1) * 64],
                                   start=False, stop=(m == 3 and s == 3))
            return ins
        p.step("pe", dsacc)

        wd_flat = wdT.rearrange("a b -> (a b)").rearrange("(c p t) -> c p t",
                                                          p=128, t=512)
        for ci in range(16):
            g = m * 16 + ci
            emit_wmean_job(wd_flat[g, :, :], g)

    emit_wscale(0, INNER * E, 0, 1)

    p.wait("dve", "pe")

    def sa_ev(e):
        return e.tensor_copy(sin_f, ps_S[:, :])
    sa_v = p.step("dve", sa_ev)
    if "sat" in dbg:
        p.wait("sp", "dve", sa_v)
        p.dma("sp", "dsp", dbg["sat"][:, :], sin_f)
    p.wait("gp", "dve", sa_v)
    cc_in_sent = p.dma("gp", "dgp", cc_in, sin_f)
    p.wdma("gp", cc_in_sent)

    def coll(g):
        g.collective_compute(
            "AllGather", OP.bypass, replica_groups=[list(range(8))],
            ins=[cc_in.opt()], outs=[cc_out.opt()]).then_inc(p.sems["cc"], 1)
    p.plain("gp", coll)
    p.cnt["cc"] += 1
    p.wait("gp", "cc", 1)

    if "ccg" in dbg:
        p.dma("gp", "dgp", dbg["ccg"], cc_out)

    def mk_seldg(e):
        ins = None
        for rr in range(8):
            ins = e.tensor_scalar(seldg[:, rr * 64:(rr + 1) * 64],
                                  ident[0:64, 0:64], selc_sb[:, rr:rr + 1], None,
                                  OP.mult)
        return ins
    p.step("dve", mk_seldg)

    for rr in range(8):
        p.wait("gp", "dve")
        h_blk = p.dma("gp", "dgp", cc_sb, cc_out[rr * 64:(rr + 1) * 64, :])
        p.wdma("dve", h_blk)
        p.wait("dve", "pe")

        def blkcast(e):
            return e.tensor_copy(cc_bf[:, :], cc_sb)
        bc_v = p.step("dve", blkcast)
        p.wait("pe", "dve", bc_v)

        def sacc(e, rr=rr):
            e.matmul(ps_S[:, 0:512], seldg[:, rr * 64:(rr + 1) * 64],
                     cc_bf[:, 0:512], start=(rr == 0), stop=(rr == 7))
            return e.matmul(ps_S[:, 512:1024], seldg[:, rr * 64:(rr + 1) * 64],
                            cc_bf[:, 512:1024], start=(rr == 0), stop=(rr == 7))
        p.step("pe", sacc)

    if "sdg" in dbg:
        p.wait("gp", "dve")
        p.dma("gp", "dgp", dbg["sdg"][:, :], seldg[:, :])
        p.dma("gp", "dgp", dbg["ccbf"][:, :], cc_bf[:, :])
    if "sa" in dbg:
        p.wait("dve", "pe")

        def sadump(e):
            return e.tensor_copy(sin_f, ps_S[:, :])
        sd_v = p.step("dve", sadump)
        p.wait("pe", "dve", sd_v)
        p.wait("sp", "dve", sd_v)
        p.dma("sp", "dsp", dbg["sa"][:, :], sin_f)
        p.wdma_all("pe", "dsp")

    if STAGE < 2:
        for c in range(8):
            p.dma("sp", "dsp", y[c * 128:(c + 1) * 128, :], FA[:, 0:2048])
        p.wdma_all("sp", "dsp")
        p.wdma_all("gp", "dgp")
        _emit_all(nc, p)
        aps.close()
        abps.close()
        ab.close()
        octx.close()
        return nc

    aps.close()
    bps = ExitStack()
    ps_O1 = bps.enter_context(nc.psum_tensor("ps_O1", [64, 1024], F32))
    ps_O2 = bps.enter_context(nc.psum_tensor("ps_O2", [64, 1024], F32))

    p.wait("gp", "pe")
    h_wb = [p.dma("gp", "dgp", W1[:, :].rearrange("p (f o) -> p f o", f=8),
                  Wq.rearrange("(f p) o -> p f o", p=128)),
            p.dma("gp", "dgp", W2[:, :].rearrange("p (f o) -> p f o", f=8),
                  Wo.rearrange("(f p) o -> p f o", p=128))]


    wmean_jobs = []
    wg_flat = wgT.rearrange("a b -> (a b)").rearrange("(c p t) -> c p t",
                                                     p=128, t=512)
    for ci in range(64):
        wmean_jobs.append((wg_flat[ci, :, :], 64 + ci))
    wu_flat = wuSL.rearrange("a b c -> (a b c)").rearrange("(c p t) -> c p t",
                                                           p=128, t=512)
    for ci in range(64):
        wmean_jobs.append((wu_flat[ci, :, :], 128 + ci))
    wj = [0]
    tj = [0]

    g1_hist = []
    for m in range(NMAC):
        mc = slice(m * MT, (m + 1) * MT)
        if m >= 1:
            p.wait("gp", "pe", g1_hist[m - 1])
        else:
            p.wait("gp", "pe")
        h_xt = p.dma("gp", "dgp", xt_bf[:, :].rearrange("p (f t) -> p f t", f=8),
                     xT.rearrange("(f p) t -> p f t", p=128)[:, :, mc])
        p.wait("sp", "dve")
        p.wait("sp", "pool")
        p.wait("sp", "pe")
        h_kv = [p.dma("sp", "dsp", kfmB[:, :], kfm_sp[:, m * 4096:(m + 1) * 4096]),
                p.dma("sp", "dsp", ktm[:, :], ktm_sp[:, m * 4096:(m + 1) * 4096])]
        p.wait("gp", "dve")
        p.wait("gp", "pe")
        h_kv.append(p.dma("gp", "dgp", vt[:, :], v_sp[:, m * 4096:(m + 1) * 4096]))
        h_tab = [p.dma("sp", "dsp", cos_m[:, :], cosd[:, mc]),
                 p.dma("sp", "dsp", sin_m[:, :], ssin[:, mc])]
        p.wait("dve", "pool")
        p.wait("pool", "dve")
        p.wait("act", "dve")
        p.wait("act", "pool")
        p.wdma("dve", *h_kv)

        qev_hist = []
        p.wdma("pe", h_xt)
        if m == 0:
            p.wdma("pe", *h_wb)
        for oc in range(8):
            pst = ps_W1 if oc % 2 == 0 else ps_W2

            def qproj(e, oc=oc, pst=pst):
                ins = None
                for f in range(8):
                    ins = e.matmul(pst[:, :],
                                   W1[:, f * 1024 + oc * 128: f * 1024 + (oc + 1) * 128],
                                   xt_bf[:, f * 512:(f + 1) * 512],
                                   start=(f == 0), stop=(f == 7))
                return ins
            if oc >= 2:
                p.wait("pe", "act", qev_hist[oc - 2])
            p.step("pe", qproj)
            p.wait("act", "pe")

            def ev(e, oc=oc, pst=pst):
                return e.activation(FA[:, oc * 512:(oc + 1) * 512], pst[:, :], AF.Silu)
            qev_hist.append(p.step("act", ev))

        if BSTOP < 2:
            continue
        p.wait("dve", "act")
        p.wdma("dve", *h_tab)
        p.wait("pool", "act")
        p.wdma("pool", *h_tab)
        for c in range(8):
            cols = slice(c * 512, (c + 1) * 512)
            if c < 5:
                emit_rotary("dve", cols, FA, krot, swd, t1d, t2d)
            else:
                emit_rotary("pool", cols, FA, krot, swp, t1p, oscr_b)
        p.wait("dve", "pool")

        def oddcp(e):
            e.tensor_copy(q_oddM[0:64, :], krot[64:128, :])
            return e.tensor_copy(k_oddM[0:64, :], kfmB[64:128, :])
        ocv = p.step("dve", oddcp)
        if m == 0 and "qod" in dbg:
            p.wait("gp", "dve", ocv)
            p.dma("gp", "dgp", dbg["qod"], q_oddM[0:64, :])
            p.dma("gp", "dgp", dbg["kod"], k_oddM[0:64, :])
            if "qrt" in dbg:
                p.dma("gp", "dgp", dbg["qrt"], krot[:, :])
            if "qsl" in dbg:
                p.dma("gp", "dgp", dbg["qsl"], FA[:, 0:4096])

        if BSTOP < 3:
            continue
        def g1(e):
            ins = None
            for f in range(8):
                ins = e.matmul(ps_O1[:, 0:512], Wg1_sb[:, f * 64:(f + 1) * 64],
                               xt_bf[:, f * 512:(f + 1) * 512],
                               start=(f == 0), stop=(f == 7))
            return ins
        p.wait("pe", "dve")
        g1_hist.append(p.step("pe", g1))
        p.wait("dve", "pe")

        def g1e(e):
            return e.tensor_copy(xg1_bf[:, :], ps_O1[:, 0:512])
        g1v = p.step("dve", g1e)
        p.wait("pe", "dve", g1v)
        for oc in range(8):
            pst = ps_W1 if oc % 2 == 0 else ps_W2

            def g2(e, oc=oc, pst=pst):
                return e.matmul(pst[:, :], Wg2_sb[:, oc * 128:(oc + 1) * 128],
                                xg1_bf[:, :], start=True, stop=True)
            if oc >= 2:
                p.wait("pe", "dve")
            p.step("pe", g2)
            p.wait("act", "pe")
            p.wait("act", "pool")

            def gt(e, oc=oc, pst=pst):
                return e.activation(jA[:, 0:512], pst[:, :], AF.Tanh, scale=0.5)
            p.step("act", gt)
            p.wait("dve", "act")

            def gs(e, oc=oc):
                return e.tensor_scalar(FB[:, oc * 512:(oc + 1) * 512], jA[:, 0:512],
                                       0.5, 0.5, OP.mult, OP.add)
            v = p.step("dve", gs)
            p.wait("act", "dve", v)

        if BSTOP < 4:
            continue
        p.wdma("pe", *h_kv)
        for s in range(4):
            g = m * 4 + s
            p.wait("dve", "pe")

            def sbf(e):
                return e.tensor_copy(S_bf[:, :], ps_S[:, :])
            sv = p.step("dve", sbf)
            p.wait("pe", "dve", sv)

            if ASTOP < 2:
                continue
            for rnd in range(2):
                def emm(e, s=s, rnd=rnd):
                    ins = None
                    for hh in range(8):
                        h = rnd * 8 + hh
                        c = h // 2
                        pst = ps_W1 if hh < 4 else ps_W2
                        col = (hh % 4) * 128
                        cols = slice(c * 512 + s * 128, c * 512 + (s + 1) * 128)
                        kl = (kfmB[0:64, cols] if h % 2 == 0
                              else k_oddM[0:64, cols])
                        ql = (krot[0:64, cols] if h % 2 == 0
                              else q_oddM[0:64, cols])
                        ins = e.matmul(pst[:, col:col + 128], kl, ql,
                                       start=True, stop=True)
                    return ins
                p.step("pe", emm)
                p.wait("dve", "pe")

                def eev(e, rnd=rnd):
                    e.tensor_tensor(eT_sb[:, rnd * 1024: rnd * 1024 + 512],
                                    ps_W1[:, :], mask4[:, :], OP.mult)
                    return e.tensor_tensor(eT_sb[:, rnd * 1024 + 512: rnd * 1024 + 1024],
                                           ps_W2[:, :], mask4[:, :], OP.mult)
                ev_v = p.step("dve", eev)
                p.wait("pe", "dve", ev_v)

            if ASTOP < 3:
                continue
            def avq(e, s=s):
                ins = None
                for h in range(HCNT):
                    c = h // 2
                    pso = ps_O1 if h % 2 == 0 else ps_O2
                    ocol = c * 128
                    e.matmul(pso[:, ocol:ocol + 128],
                             vt[:, s * 1024 + h * 64: s * 1024 + (h + 1) * 64],
                             eT_sb[:, h * 128:(h + 1) * 128],
                             start=True, stop=False)
                    qsl = (krot[0:64, c * 512 + s * 128: c * 512 + (s + 1) * 128]
                           if h % 2 == 0 else
                           q_oddM[0:64, c * 512 + s * 128: c * 512 + (s + 1) * 128])
                    ins = e.matmul(pso[:, ocol:ocol + 128],
                                   S_bf[:, h * 64:(h + 1) * 64], qsl,
                                   start=False, stop=True)
                return ins
            p.step("pe", avq)

            if ASTOP < 4:
                continue
            def dsu(e, s=s, g=g):
                ins = None
                for h in range(HCNT):
                    ins = e.matmul(ps_S[:, h * 64:(h + 1) * 64],
                                   ktm[:, s * 1024 + h * 64: s * 1024 + (h + 1) * 64],
                                   vt[:, s * 1024 + h * 64: s * 1024 + (h + 1) * 64],
                                   start=False, stop=(g == 15))
                return ins
            p.step("pe", dsu)

            if ASTOP < 5:
                continue
            p.wait("dve", "pe")

            def oge(e, s=s):
                ins = None
                for par, pso in ((0, ps_O1), (1, ps_O2)):
                    ov = FC[par * 64:(par + 1) * 64, :].rearrange(
                        "p (c t) -> p c t", c=8)[:, :, s * 128:(s + 1) * 128]
                    iv = pso[:, :].rearrange("p (c t) -> p c t", c=8)
                    gv = FB[par * 64:(par + 1) * 64, :].rearrange(
                        "p (c t) -> p c t", c=8)[:, :, s * 128:(s + 1) * 128]
                    ins = e.tensor_tensor(ov, iv, gv, OP.mult)
                return ins
            ogv = p.step("dve", oge)
            p.wait("pe", "dve", ogv)

        if BSTOP >= 8:
            for _ in range(32):
                if wj[0] < len(wmean_jobs):
                    srcj, col = wmean_jobs[wj[0]]
                    emit_wmean_job(srcj, col)
                    wj[0] += 1
            for _ in range(16):
                if tj[0] < 64:
                    emit_wtern_job(tj[0])
                    tj[0] += 1

        if "og" in dbg:
            p.wait("sp", "dve")
            p.dma("sp", "dsp", dbg["og"][:, m * 4096:(m + 1) * 4096], FC[:, :])
        if "gate" in dbg:
            p.wait("sp", "dve")
            p.dma("sp", "dsp", dbg["gate"][:, m * 4096:(m + 1) * 4096], FB[:, :])
        if BSTOP < 5:
            continue
        emit_ln(FC, True, out_bf=xn_bf)

        if BSTOP < 6:
            continue
        p.wait("pe", "dve")
        p.wait("pe", "pool")
        r1_hist = []
        for oc in range(8):
            pst = ps_W1 if oc % 2 == 0 else ps_W2
            xslot = (t1d if oc % 2 == 0 else t2d)[:, :]
            p.wait("sp", "dve")
            h_x = p.dma("sp", "dsp", xslot,
                        xT[oc * 128:(oc + 1) * 128, mc])

            def wop(e, oc=oc, pst=pst):
                ins = None
                for f in range(8):
                    ins = e.matmul(pst[:, :],
                                   W2[:, f * 1024 + oc * 128: f * 1024 + (oc + 1) * 128],
                                   xn_bf[:, f * 512:(f + 1) * 512],
                                   start=(f == 0), stop=(f == 7))
                return ins
            if oc >= 2:
                p.wait("pe", "dve", r1_hist[oc - 2])
            p.step("pe", wop)
            p.wait("dve", "pe")
            p.wdma("dve", h_x)

            def r1f(e, oc=oc, pst=pst, xslot=xslot):
                return e.tensor_tensor(FA[:, oc * 512:(oc + 1) * 512], pst[:, :],
                                       xslot, OP.add)
            r1_hist.append(p.step("dve", r1f))

        if BSTOP < 7:
            continue
        emit_ln(FA, False, out_f32=FB)

        p.wait("sp", "dve")
        p.wait("sp", "pool")
        p.dma("sp", "dsp", xmid_sp[:, m * 4096:(m + 1) * 4096], FB[:, :])
        if "xmid" in dbg:
            p.dma("sp", "dsp", dbg["xmid"][:, m * 4096:(m + 1) * 4096], FB[:, :])
        if "r1" in dbg:
            p.dma("sp", "dsp", dbg["r1"][:, m * 4096:(m + 1) * 4096], FA[:, :])
        if "xn" in dbg:
            p.wait("gp", "dve")
            p.wait("gp", "pool")
            p.dma("gp", "dgp", dbg["xn"][:, m * 4096:(m + 1) * 4096], xn_bf[:, :])

    if BSTOP >= 8:
        while wj[0] < len(wmean_jobs):
            srcj, col = wmean_jobs[wj[0]]
            emit_wmean_job(srcj, col)
            wj[0] += 1
        while tj[0] < 64:
            emit_wtern_job(tj[0])
            tj[0] += 1

        emit_wscale(64, INNER * E, 2, 4)
        emit_wscale(128, INNER * E, 3, 5)
        if "wsc" in dbg:
            p.wait("sp", "dve")
            p.dma("sp", "dsp", dbg["wsc"], wsc[:, 0:8])
            p.dma("sp", "dsp", dbg["tinyd"], tiny[0:1, :])
            p.dma("sp", "dsp", dbg["waccd"], wacc[:, :])
            if "spdbg" in dbg:
                p.dma("sp", "dsp", dbg["spdbg"], t2d[0:32, 0:16])
                p.wait("gp", "dve")
                p.dma("gp", "dgp", dbg["wabdbg"], wab_bf[0:32, 0:4])
                p.dma("sp", "dsp", dbg["t1dbg"], t1d[:, 0:64])

    if "sa" in dbg:
        p.wait("sp", "dve")
        p.dma("sp", "dsp", dbg["sa"][:, :], sin_f)

    if STAGE < 3:
        for c in range(8):
            p.dma("sp", "dsp", y[c * 128:(c + 1) * 128, :], FA[:, 0:2048])
        p.wdma_all("sp", "dsp")
        p.wdma_all("gp", "dgp")
        _emit_all(nc, p)
        bps.close()
        abps.close()
        ab.close()
        octx.close()
        return nc

    fin = {e: p.cnt[e] for e in ("pe", "act", "dve", "pool")}
    for e in ("pe", "act", "dve", "pool", "sp", "gp"):
        for s in ("pe", "act", "dve", "pool"):
            if s != e:
                p.wait(e, s, fin[s])
        p.wdma_all(e, "dsp")
        p.wdma_all(e, "dgp")

    bps.close()
    abps.close()
    ab.close()

    cs = ExitStack()

    def csb(name, shape, dt=F32):
        return cs.enter_context(nc.sbuf_tensor(name, shape, dt))

    wg_sb = csb("wg_sb", [128, 8 * INNER], BF16)
    junkc = csb("junkc", [128, 1024])
    junk2 = csb("junk2", [128, 1024])
    wstr = csb("wstr", [128, 2048])
    xmid_c = csb("xmid_c", [128, 2048])
    hbuf = csb("hbuf", [128, 8192])
    hq = csb("hq", [128, 8192], BF16)
    xq_deq = csb("xq_deq", [128, 2048], BF16)
    xq_tmp = csb("xq_tmp", [128, 256], BF16)
    gsil = csb("gsil", [128, 256])
    wd_buf = csb("wd_buf", [128, 2048], BF16)
    wu_buf = csb("wu_buf", [128, 2048], BF16)
    rmx = csb("rmx", [128, 1024])
    rmn = csb("rmn", [128, 1024])
    fold = csb("fold", [64, 512])
    fold_bf = csb("fold_bf", [32, 512], BF16)
    tmv = csb("tmv", [128, 16])
    tmv_bf = csb("tmv_bf", [128, 8], BF16)
    rowv_bf = csb("rowv_bf", [1, 1024], BF16)
    bs1 = csb("bs1", [128, 256])
    bsi1 = csb("bsi1", [128, 256])
    bs2 = csb("bs2", [128, 256])
    bsd = csb("bsd", [128, 256])
    ytile = csb("ytile", [128, 2048])
    lnr2 = csb("lnr2", [1, 1280])
    lnr2_bf = csb("lnr2_bf", [1, 1024], BF16)
    br2 = csb("br2", [128, 256])
    bm2 = csb("bm2", [128, 256])
    tscr = csb("tscr", [128, 512])
    oscr = csb("oscr", [128, 512])

    cps = ExitStack()
    ps_GU = cps.enter_context(nc.psum_tensor("ps_GU", [128, 512], F32))
    ps_GU2 = cps.enter_context(nc.psum_tensor("ps_GU2", [128, 512], F32))
    ps_DN = cps.enter_context(nc.psum_tensor("ps_DN", [128, 2048], F32))
    ps_TP = cps.enter_context(nc.psum_tensor("ps_TP", [128, 512], BF16))

    def spacer_c(e):
        return e.tensor_scalar(junkc[:, 256:512], junkc[:, 0:256], 1.0, None,
                               OP.mult)

    cln_bf = hq[:, 0:2048]
    sq3_bf = hq[:, 2048:4096]

    wg_flatC = wgT.rearrange("a b -> (a b)").rearrange("(c p t) -> c p t",
                                                       p=128, t=1024)
    ca_hist = []
    for ci in range(32):
        buf = wstr[:, (ci % 2) * 1024:(ci % 2) * 1024 + 1024]
        jb = junkc if ci % 2 == 0 else junk2
        q, dq = ("sp", "dsp") if ci % 2 == 0 else ("gp", "dgp")
        if ci >= 2:
            p.wait(q, "dve", ca_hist[ci - 2])
        else:
            p.wait(q, "dve")
            p.wait(q, "pool")
        hb = p.dma(q, dq, buf, wg_flatC[ci, :, :])
        p.wdma("dve", hb)
        p.wait("dve", "pool")

        def ca(e, buf=buf, jb=jb):
            e.scalar_tensor_tensor(jb[:, 0:512], buf[:, 0:512], wsc[:, 2:3],
                                   mrow[:, :], OP.mult, OP.add)
            return e.scalar_tensor_tensor(jb[:, 512:1024], buf[:, 512:1024],
                                          wsc[:, 2:3], mrow[:, :], OP.mult, OP.add)
        ca_hist.append(p.step("dve", ca))
        p.wait("pool", "dve")

        def cb(e, ci=ci, jb=jb):
            e.tensor_scalar(oscr[:, 0:512], jb[:, 0:512], MRND + 1.0, MRND - 1.0,
                            OP.min, OP.max)
            e.tensor_scalar(oscr[:, 0:512], jb[:, 512:1024], MRND + 1.0, MRND - 1.0,
                            OP.min, OP.max) if False else None
            e.tensor_scalar(
                wg_sb[:, :].rearrange("p (c t) -> p c t", c=32)[:, ci, 0:512],
                oscr[:, 0:512], -MRND, None, OP.add)
            e.tensor_scalar(oscr[:, 0:512], jb[:, 512:1024], MRND + 1.0, MRND - 1.0,
                            OP.min, OP.max)
            return e.tensor_scalar(
                wg_sb[:, :].rearrange("p (c t) -> p c t", c=32)[:, ci, 512:1024],
                oscr[:, 0:512], -MRND, None, OP.add)
        p.step("pool", cb)
    for oc in range(32):
        buf = wstr[:, (oc % 2) * 1024:(oc % 2) * 1024 + 1024]
        jb = (junkc if oc % 2 == 0 else junk2)[:, 0:1024]
        tb = wu_buf[:, (oc % 2) * 1024:(oc % 2) * 1024 + 1024]
        q, dq = ("sp", "dsp") if oc % 2 == 0 else ("gp", "dgp")
        if oc >= 2:
            p.wait(q, "dve", ca_hist[32 + oc - 2])
        else:
            p.wait(q, "dve")
            p.wait(q, "pool")
        hb = p.dma(q, dq, buf,
                   wuSL[oc, :, :].rearrange("(f p) o -> p f o", p=128))
        p.wdma("dve", hb)
        p.wait("dve", "pool")

        def cua(e, buf=buf, jb=jb):
            e.scalar_tensor_tensor(jb[:, 0:512], buf[:, 0:512], wsc[:, 3:4],
                                   mrow[:, :], OP.mult, OP.add)
            return e.scalar_tensor_tensor(jb[:, 512:1024], buf[:, 512:1024],
                                          wsc[:, 3:4], mrow[:, :], OP.mult, OP.add)
        ca_hist.append(p.step("dve", cua))
        p.wait("pool", "dve")

        def cub(e, jb=jb, tb=tb):
            e.tensor_scalar(oscr[:, 0:512], jb[:, 0:512], MRND + 1.0, MRND - 1.0,
                            OP.min, OP.max)
            e.tensor_scalar(tb[:, 0:512], oscr[:, 0:512], -MRND, None, OP.add)
            e.tensor_scalar(oscr[:, 0:512], jb[:, 512:1024], MRND + 1.0, MRND - 1.0,
                            OP.min, OP.max)
            return e.tensor_scalar(tb[:, 512:1024], oscr[:, 0:512], -MRND, None,
                                   OP.add)
        v = p.step("pool", cub)
        p.wait("sp", "pool", v)
        p.dma("sp", "dsp", wu_t[oc, :, :], tb)

    def emit_scales(chunks, s_dst, inv_dst):
        n = len(chunks)
        p.wait("dve", "pool")

        def mm0(e):
            e.tensor_copy(rmx[:, 0:256], chunks[0])
            return e.tensor_copy(rmn[:, 0:256], chunks[0])
        p.step("dve", mm0)
        cur = 0
        for i in range(1, n):
            nxt = (cur + 1) % 2

            def mm(e, i=i, cur=cur, nxt=nxt):
                e.tensor_tensor(rmx[:, nxt * 256:(nxt + 1) * 256],
                                rmx[:, cur * 256:(cur + 1) * 256],
                                chunks[i], OP.max)
                return e.tensor_tensor(rmn[:, nxt * 256:(nxt + 1) * 256],
                                       rmn[:, cur * 256:(cur + 1) * 256],
                                       chunks[i], OP.min)
            p.step("dve", mm)
            cur = nxt

        def comb(e, cur=cur):
            e.scalar_tensor_tensor(rmx[:, 512:768], rmn[:, cur * 256:(cur + 1) * 256],
                                   -1.0, rmx[:, cur * 256:(cur + 1) * 256],
                                   OP.mult, OP.max)
            e.tensor_copy(fold[0:64, 0:256], rmx[64:128, 512:768])
            e.tensor_tensor(fold[0:64, 256:512], rmx[0:64, 512:768],
                            fold[0:64, 0:256], OP.max)
            e.tensor_copy(rmx[0:32, 768:1024], fold[32:64, 256:512])
            e.tensor_tensor(fold[0:32, 0:256], fold[0:32, 256:512],
                            rmx[0:32, 768:1024], OP.max)
            e.tensor_copy(fold_bf[:, 0:256], fold[0:32, 0:256])
            e.tensor_tensor(fold[0:32, 256:512], fold[0:32, 0:256],
                            fold_bf[:, 0:256], OP.subtract)
            return e.tensor_copy(fold_bf[:, 256:512], fold[0:32, 256:512])
        p.step("dve", comb)
        p.wait("pe", "dve")

        def tp1(e):
            e.transpose(ps_TP[:, 0:32], fold_bf[:, 0:128], ident[0:32, 0:32])
            e.transpose(ps_TP[:, 64:96], fold_bf[:, 256:384], ident[0:32, 0:32])
            e.transpose(ps_TP[:, 32:64], fold_bf[:, 128:256], ident[0:32, 0:32])
            return e.transpose(ps_TP[:, 96:128], fold_bf[:, 384:512],
                               ident[0:32, 0:32])
        p.step("pe", tp1)
        p.wait("dve", "pe")

        def tp1e(e):
            e.tensor_copy(tscr[:, 0:32], ps_TP[:, 0:32])
            e.tensor_copy(tscr[:, 64:96], ps_TP[:, 64:96])
            e.tensor_copy(tscr[:, 32:64], ps_TP[:, 32:64])
            e.tensor_copy(tscr[:, 96:128], ps_TP[:, 96:128])
            spacer_c(e)
            e.tensor_tensor(tscr[:, 128:192], tscr[:, 0:64], tscr[:, 64:128],
                            OP.add)
            spacer_c(e)
            e.reduce_max(tmv[:, 0:1], tscr[:, 128:160], axis=AX)
            e.reduce_max(tmv[:, 1:2], tscr[:, 160:192], axis=AX)
            spacer_c(e)
            e.tensor_copy(tmv_bf[:, 0:2], tmv[:, 0:2])
            spacer_c(e)
            e.scalar_tensor_tensor(tmv[:, 2:4], tmv_bf[:, 0:2], -1.0,
                                   tmv[:, 0:2], OP.mult, OP.add)
            spacer_c(e)
            return e.tensor_copy(tmv_bf[:, 2:4], tmv[:, 2:4])
        p.step("dve", tp1e)
        p.wait("pe", "dve")

        def tpb(e):
            e.transpose(ps_TP[0:1, 0:128], tmv_bf[:, 0:1], ident[:, :])
            e.transpose(ps_TP[0:1, 128:256], tmv_bf[:, 1:2], ident[:, :])
            e.transpose(ps_TP[0:1, 256:384], tmv_bf[:, 2:3], ident[:, :])
            return e.transpose(ps_TP[0:1, 384:512], tmv_bf[:, 3:4], ident[:, :])
        p.step("pe", tpb)
        p.wait("dve", "pe")

        def chain(e):
            S = [lnr2[0:1, i * 256:(i + 1) * 256] for i in range(5)]
            e.tensor_copy(S[0], ps_TP[0:1, 256:512])
            e.tensor_tensor(S[1], ps_TP[0:1, 0:256], S[0], OP.add)
            e.tensor_scalar(S[2], S[1], 1e-5, None, OP.max)
            e.reciprocal(S[3], S[2])
            e.tensor_scalar(S[4], S[3], 127.0, None, OP.mult)
            e.tensor_scalar(S[1], S[2], 1.0 / 127.0, None, OP.mult)
            e.tensor_copy(rowv_bf[0:1, 0:256], S[4])
            e.tensor_copy(rowv_bf[0:1, 512:768], S[1])
            e.scalar_tensor_tensor(S[0], rowv_bf[0:1, 0:256], -1.0, S[4],
                                   OP.mult, OP.add)
            return e.tensor_copy(rowv_bf[0:1, 256:512], S[0])
        p.step("dve", chain)
        p.wait("pe", "dve")

        def bcst(e):
            e.matmul(ps_GU[:, 0:256], ones_row[:, :], rowv_bf[0:1, 0:256],
                     start=True, stop=False)
            e.matmul(ps_GU[:, 0:256], ones_row[:, :], rowv_bf[0:1, 256:512],
                     start=False, stop=True)
            return e.matmul(ps_GU2[:, 0:256], ones_row[:, :], rowv_bf[0:1, 512:768],
                            start=True, stop=True)
        p.step("pe", bcst)
        p.wait("dve", "pe")

        def bcev(e):
            e.tensor_copy(s_dst[:, :], ps_GU[:, 0:256])
            return e.tensor_copy(inv_dst[:, :], ps_GU2[:, 0:256])
        v = p.step("dve", bcev)
        p.wait("pe", "dve", v)

    for u in range(NSUB):
        m, hh = u // 2, u % 2
        p.wait("dve", "pool")
        p.wait("pool", "dve")
        p.wait("pool", "pe")
        p.wait("act", "dve")
        xq_, dq_ = ("sp", "dsp") if u % 2 == 0 else ("gp", "dgp")
        p.wait(xq_, "dve")
        p.wait(xq_, "pool")
        if u == 0:
            p.wdma_all("sp", "dsp")
        h_xm = p.dma(xq_, dq_, xmid_c[:, :].rearrange("p (c t) -> p c t", c=8),
                     xmid_sp[:, :].rearrange("p (mm c t) -> p mm c t", mm=4, c=8)
                     [:, m, :, hh * 256:(hh + 1) * 256])
        p.wdma("dve", h_xm)
        p.wdma("pool", h_xm)

        xch = [xmid_c[:, c * 256:(c + 1) * 256] for c in range(8)]
        emit_scales(xch, bs1, bsi1)
        if u == DUMP_U and "bs1" in dbg:
            p.wait("sp", "dve")
            p.dma("sp", "dsp", dbg["bs1"], bs1[:, :])
            p.dma("sp", "dsp", dbg["tmv"], tmv[:, :])
            p.dma("sp", "dsp", dbg["habs"], rmx[:, 512:768])
            p.wait("gp", "dve")
            p.dma("gp", "dgp", dbg["bsi1"], bsi1[:, :])
            if "rowv" in dbg:
                p.dma("gp", "dgp", dbg["rowv"], rowv_bf[:, :])
                p.dma("gp", "dgp", dbg["tmvbf"], tmv_bf[:, :])

        for c in range(8):
            def xq1(e, c=c):
                e.tensor_tensor(tscr[:, 0:256], xch[c], bs1[:, :], OP.mult)
                e.tensor_scalar(tscr[:, 256:512], tscr[:, 0:256], MRND, MRND + 127.0,
                                OP.add, OP.min)
                e.tensor_scalar(xq_tmp[:, :], tscr[:, 256:512], MRND - 128.0, -MRND,
                                OP.max, OP.add)
                return e.tensor_tensor(xq_deq[:, c * 256:(c + 1) * 256], xq_tmp[:, :],
                                       bsi1[:, :], OP.mult)
            p.step("dve", xq1)

        p.wait("pe", "dve")
        if u == 0:
            p.wait("pe", "pool")
            p.wdma_all("gp", "dsp")
        gu_hist = []
        for oc in range(32):
            slab = wu_buf[:, (oc % 2) * 1024:(oc % 2) * 1024 + 1024]
            q, dq = ("sp", "dsp") if oc % 2 == 0 else ("gp", "dgp")
            if oc >= 2:
                p.wait(q, "pe", gu_hist[oc - 2])
            else:
                p.wait(q, "pe")
            h_sl = p.dma(q, dq, slab, wu_t[oc, :, :])
            pst = ps_GU if oc % 2 == 0 else ps_GU2
            p.wdma("pe", h_sl)
            if oc >= 2:
                p.wait("pe", "dve")

            col = (oc % 2) * 256

            def gu(e, oc=oc, col=col, slab=slab):
                ins = None
                for f in range(8):
                    e.matmul(ps_GU[:, col:col + 256],
                             wg_sb[:, f * INNER + oc * 128: f * INNER + (oc + 1) * 128],
                             xq_deq[:, f * 256:(f + 1) * 256],
                             start=(f == 0), stop=(f == 7))
                    ins = e.matmul(ps_GU2[:, col:col + 256], slab[:, f * 128:(f + 1) * 128],
                                   xq_deq[:, f * 256:(f + 1) * 256],
                                   start=(f == 0), stop=(f == 7))
                return ins
            gu_hist.append(p.step("pe", gu))
            p.wait("act", "pe")
            p.wait("act", "dve")

            def sil(e, col=col):
                return e.activation(gsil[:, :], ps_GU[:, col:col + 256], AF.Silu,
                                    scale=wsc[:, 4:5])
            sv = p.step("act", sil)
            p.wait("dve", "act", sv)

            def hm(e, oc=oc, col=col):
                return e.scalar_tensor_tensor(hbuf[:, oc * 256:(oc + 1) * 256],
                                              ps_GU2[:, col:col + 256], wsc[:, 5:6],
                                              gsil[:, :], OP.mult, OP.mult)
            p.step("dve", hm)

        hch = [hbuf[:, c * 256:(c + 1) * 256] for c in range(32)]
        emit_scales(hch, bs2, bsd)

        p.wait("pool", "dve")
        for c in range(32):
            eng = "dve" if c % 2 == 0 else "pool"
            tb = tscr if eng == "dve" else oscr

            def hqf(e, c=c, tb=tb):
                e.tensor_tensor(tb[:, 0:256], hch[c], bs2[:, :], OP.mult)
                e.tensor_scalar(tb[:, 256:512], tb[:, 0:256], MRND, MRND + 127.0,
                                OP.add, OP.min)
                return e.tensor_scalar(hq[:, c * 256:(c + 1) * 256], tb[:, 256:512],
                                       MRND - 128.0, -MRND, OP.max, OP.add)
            p.step(eng, hqf)
        p.wait("pe", "dve")
        p.wait("pe", "pool")

        dn_hist = []
        for mi in range(32):
            slab = wd_buf[:, (mi % 2) * 1024:(mi % 2) * 1024 + 1024]
            q, dq = ("sp", "dsp") if mi % 2 == 0 else ("gp", "dgp")
            if mi >= 2:
                p.wait(q, "pe", dn_hist[mi - 2])
            else:
                p.wait(q, "pe")
            h_sl = p.dma(q, dq, slab, wd_t[mi * 128:(mi + 1) * 128, :])
            p.wdma("pe", h_sl)

            def dn(e, mi=mi, slab=slab):
                if mi == 0:
                    for b in range(4):
                        e.matmul(ps_DN[:, b * 512:(b + 1) * 512], ones_row[:, :],
                                 zrow_bf[0:1, 0:512], start=True, stop=True)
                ins = None
                for oc in range(8):
                    ins = e.matmul(ps_DN[:, oc * 256:(oc + 1) * 256],
                                   slab[:, oc * 128:(oc + 1) * 128],
                                   hq[:, mi * 256:(mi + 1) * 256],
                                   start=False, stop=(mi == 31))
                return ins
            dn_hist.append(p.step("pe", dn))

        p.wait("dve", "pe")
        for c in range(8):
            def r2f(e, c=c):
                e.scalar_tensor_tensor(tscr[:, 0:256], ps_DN[:, c * 256:(c + 1) * 256],
                                       wsc[:, 1:2], bsd[:, :], OP.mult, OP.mult)
                return e.tensor_tensor(ytile[:, c * 256:(c + 1) * 256],
                                       tscr[:, 0:256], xch[c], OP.add)
            p.step("dve", r2f)

        if u == DUMP_U:
            if "xq0" in dbg:
                p.wait("gp", "dve")
                p.dma("gp", "dgp", dbg["xq0"][:, :], xq_deq[:, :])
            if "h0" in dbg:
                p.wait("sp", "dve")
                p.dma("sp", "dsp", dbg["h0"][:, :], hbuf[:, :])
            if "hq0" in dbg:
                p.wait("gp", "dve")
                p.wait("gp", "pool")
                hqh = p.dma("gp", "dgp", dbg["hq0"][:, :], hq[:, :])
                p.wdma("dve", hqh)
            if "r20" in dbg:
                p.wait("sp", "dve")
                p.dma("sp", "dsp", dbg["r20"][:, :], ytile[:, :])
        for c in range(8):
            cl = slice(c * 256, (c + 1) * 256)

            def cp3(e, cl=cl):
                return e.tensor_copy(cln_bf[:, cl], ytile[:, cl])
            p.step("dve", cp3)
            p.wait("act", "dve")

            def sq3(e, cl=cl):
                return e.activation(sq3_bf[:, cl], ytile[:, cl], AF.Square)
            p.step("act", sq3)
        p.wait("pe", "dve")
        p.wait("pe", "act")

        def st3(e):
            ins = None
            for c in range(8):
                cl = slice(c * 256, (c + 1) * 256)
                e.matmul(ps_DN[0:1, 0:256], ones_col[:, :], cln_bf[:, cl],
                         start=(c == 0), stop=(c == 7))
                ins = e.matmul(ps_DN[0:1, 1024:1280], ones_col[:, :], sq3_bf[:, cl],
                               start=(c == 0), stop=(c == 7))
            return ins
        p.step("pe", st3)
        p.wait("dve", "pe")

        def chain3(e):
            L = lnr2
            mu, var, yy, tmp, mrs = (L[0:1, i * 256:(i + 1) * 256]
                                     for i in range(5))
            t2 = tmp
            e.tensor_scalar(mu, ps_DN[0:1, 0:256], 1.0 / E, None, OP.mult)
            e.tensor_scalar(tmp, ps_DN[0:1, 1024:1280], 1.0 / E, EPS, OP.mult, OP.add)
            e.tensor_tensor(var, mu, mu, OP.mult)
            e.tensor_tensor(var, tmp, var, OP.subtract)
            e.tensor_copy(lnr2_bf[0:1, 0:256], var)
            e.tensor_copy(tmp, lnr2_bf[0:1, 0:256].bitcast(U16))
            e.tensor_scalar(tmp, tmp, 0.5, MRND, OP.mult, OP.add)
            e.tensor_scalar(tmp, tmp, -MRND, None, OP.add)
            e.tensor_scalar(tmp, tmp, -1.0, 24375.0, OP.mult, OP.add)
            e.tensor_copy(lnr2_bf[0:1, 0:256].bitcast(U16), tmp)
            e.tensor_copy(yy, lnr2_bf[0:1, 0:256])
            for _ in range(3):
                e.tensor_tensor(tmp, yy, yy, OP.mult)
                e.tensor_tensor(tmp, tmp, var, OP.mult)
                e.tensor_scalar(tmp, tmp, -0.5, 1.5, OP.mult, OP.add)
                e.tensor_tensor(yy, yy, tmp, OP.mult)
            e.tensor_tensor(mrs, mu, yy, OP.mult)
            e.tensor_copy(lnr2_bf[0:1, 0:256], yy)
            e.tensor_tensor(t2, yy, lnr2_bf[0:1, 0:256], OP.subtract)
            e.tensor_copy(lnr2_bf[0:1, 256:512], t2)
            e.tensor_copy(lnr2_bf[0:1, 512:768], mrs)
            e.tensor_tensor(t2, mrs, lnr2_bf[0:1, 512:768], OP.subtract)
            return e.tensor_copy(lnr2_bf[0:1, 768:1024], t2)
        p.step("dve", chain3)
        p.wait("pe", "dve")

        def bc3(e):
            e.matmul(ps_DN[:, 256:512], ones_row[:, :], lnr2_bf[0:1, 0:256],
                     start=True, stop=False)
            e.matmul(ps_DN[:, 256:512], ones_row[:, :], lnr2_bf[0:1, 256:512],
                     start=False, stop=True)
            e.matmul(ps_DN[:, 1280:1536], ones_row[:, :], lnr2_bf[0:1, 512:768],
                     start=True, stop=False)
            return e.matmul(ps_DN[:, 1280:1536], ones_row[:, :], lnr2_bf[0:1, 768:1024],
                            start=False, stop=True)
        p.step("pe", bc3)
        p.wait("dve", "pe")

        def bev3(e):
            e.tensor_copy(br2[:, :], ps_DN[:, 256:512])
            return e.tensor_copy(bm2[:, :], ps_DN[:, 1280:1536])
        p.step("dve", bev3)
        p.wait("pool", "dve")

        for c in range(8):
            cl = slice(c * 256, (c + 1) * 256)
            eng = "dve" if c % 2 == 0 else "pool"
            tb = tscr if eng == "dve" else oscr

            def ap3(e, cl=cl, tb=tb):
                e.tensor_tensor(tb[:, 0:256], ytile[:, cl], br2[:, :], OP.mult)
                return e.tensor_tensor(tb[:, 256:512], tb[:, 0:256], bm2[:, :],
                                       OP.subtract)
            v = p.step(eng, ap3)
            p.wait("sp", eng, v)
            h_y = p.dma("sp", "dsp",
                        y[c * 128:(c + 1) * 128, u * 256:(u + 1) * 256], tb[:, 256:512])
            if c < 6:
                p.wdma(eng, h_y)

    p.wdma_all("sp", "dsp")
    p.wdma_all("gp", "dgp")

    _emit_all(nc, p)
    cps.close()
    cs.close()
    octx.close()
    return nc


def _emit_all(nc, p):
    with nc.Block() as block:
        @block.tensor
        def _(e):
            for _, fn in p.ops["pe"]:
                fn(e)

        @block.scalar
        def _(e):
            e.preload_act_table(AF.Silu)
            for _, fn in p.ops["act"]:
                fn(e)

        @block.vector
        def _(e):
            for _, fn in p.ops["dve"]:
                fn(e)

        @block.gpsimd
        def _(e):
            merged = sorted(p.ops["gp"] + p.ops["pool"], key=lambda t: t[0])
            for _, fn in merged:
                fn(e)

        @block.sync
        def _(e):
            for _, fn in p.ops["sp"]:
                fn(e)


def _host_inputs(full):
    x = np.asarray(full["L0_attn_x"], dtype=np.float32)
    BASE = 10000.0
    invf = BASE ** (-(np.arange(32, dtype=np.float64) * 2.0) / 64.0)
    freqs = invf[np.arange(128) % 32]
    sgn = np.where((np.arange(128) // 32) % 2 == 0, -1.0, 1.0)
    wu = np.asarray(full["L0_ffn_w_up"], np.float32)
    shared = {
        "Wq": np.ascontiguousarray(full["L0_attn_Wq"], dtype=np.float32),
        "Wk": np.ascontiguousarray(full["L0_attn_Wk"], dtype=np.float32),
        "Wv": np.ascontiguousarray(full["L0_attn_Wv"], dtype=np.float32),
        "Wo": np.ascontiguousarray(full["L0_attn_Wo"], dtype=np.float32),
        "Wg1": np.ascontiguousarray(full["L0_attn_Wg1"], dtype=np.float32),
        "Wg2": np.ascontiguousarray(full["L0_attn_Wg2"], dtype=np.float32),
        "wgT": np.ascontiguousarray(
            np.asarray(full["L0_ffn_w_gate"], np.float32).T
            .reshape(8, 128, INNER).transpose(0, 2, 1)
            .reshape(32, 1024, 128).transpose(0, 2, 1).reshape(E, INNER)),
        "wuSL": np.ascontiguousarray(
            wu.T.reshape(E, 32, 128).transpose(1, 0, 2)),
        "wdT": np.ascontiguousarray(np.asarray(full["L0_ffn_w_down"], np.float32).T),
        "gam": np.ascontiguousarray(
            np.asarray(full["L0_attn_gamma"], np.float32).reshape(8, 128).T),
        "bet": np.ascontiguousarray(
            np.asarray(full["L0_attn_beta"], np.float32).reshape(8, 128).T),
    }
    in_maps = []
    for c in range(8):
        b, j = c // 4, c % 4
        pos = np.arange(j * TOK, (j + 1) * TOK, dtype=np.float64)
        ang = freqs[:, None] * pos[None, :]
        im = dict(shared)
        im["xT"] = np.ascontiguousarray(x[b, j * TOK:(j + 1) * TOK, :].T)
        im["cosd"] = np.cos(ang).astype(np.float32)
        im["ssin"] = (sgn[:, None] * np.sin(ang)).astype(np.float32)
        sel = np.zeros((64, 8), np.float32)
        for r in range(8):
            if r // 4 == b and r % 4 < j:
                sel[:, r] = 1.0
        im["selc"] = sel
        in_maps.append(im)
    return in_maps


def kernel(**inputs):
    nc = build_program(DUMPS)
    in_maps = _host_inputs(inputs)
    res = run_bass_kernel_spmd(nc, in_maps, core_ids=list(range(8)))
    out = np.empty((2, 8192, 1024), np.float32)
    for c in range(8):
        b, j = c // 4, c % 4
        out[b, j * TOK:(j + 1) * TOK, :] = res.results[c]["y"].T
    return out


# version 14
# speedup vs baseline: 1.0326x; 1.0210x over previous
import numpy as np
import concourse.bass as bass
import concourse.mybir as mybir
from concourse.bass_utils import run_bass_kernel_spmd

F32 = mybir.dt.float32
BF16 = mybir.dt.bfloat16
I32 = mybir.dt.int32
U16 = mybir.dt.uint16
AF = mybir.ActivationFunctionType
OP = mybir.AluOpType
AX = mybir.AxisListType.X

MRND = 1.5 * 2.0 ** 23
E = 1024
HCNT = 16
D = 64
TOK = 2048
NMAC = 4
MT = 512
NSUB = 8
ST = 256
INNER = 4096
EPS = 1e-5

DUMPS = ()
STAGE = 3
BSTOP = 8
DUMP_U = 0
ASTOP = 5


class P:
    ENG = ("pe", "act", "dve", "pool", "sp", "gp")

    def __init__(self, nc):
        self.nc = nc
        self.ops = {e: [] for e in self.ENG}
        self.cnt = {}
        self.sems = {}
        self.seq = 0

    def _app(self, eng, fn):
        self.ops[eng].append((self.seq, fn))
        self.seq += 1

    def sem(self, name):
        h = self.nc.alloc_semaphore(name=name)
        self.sems[name] = h
        self.cnt[name] = 0

    def wait(self, eng, sname, val=None):
        v = self.cnt[sname] if val is None else val
        if v <= 0:
            return
        h = self.sems[sname]
        self._app(eng, lambda e: e.wait_ge(h, v))

    def step(self, eng, build_fn, sname=None):
        sname = sname or eng
        self.cnt[sname] += 1
        h = self.sems[sname]

        def fn(e):
            build_fn(e).then_inc(h, 1)
        self._app(eng, fn)
        return self.cnt[sname]

    def plain(self, eng, build_fn):
        self._app(eng, build_fn)

    NPOOL = 16

    def dma(self, eng, q, out_ap, in_ap):
        i = self.rot[q]
        self.rot[q] = (i + 1) % self.NPOOL
        s = f"{q}{i}"
        h = self.sems[s]
        prev = self.cnt[s]
        if prev > 0:
            self._app(eng, lambda e: e.wait_ge(h, prev))
        self.cnt[s] = prev + 16
        tgt = self.cnt[s]

        def fn(e):
            e.dma_start(out=out_ap, in_=in_ap).then_inc(h, 16)
        self._app(eng, fn)
        return (s, tgt)

    def wdma(self, eng, *handles):
        for s, tgt in handles:
            self.wait(eng, s, tgt)

    def wdma_all(self, eng, q):
        for i in range(self.NPOOL):
            self.wait(eng, f"{q}{i}")


def build_program(dumps=()):
    nc = bass.Bass("TRN2", target_bir_lowering=False)
    p = P(nc)

    xT = nc.dram_tensor("xT", [E, TOK], F32, kind="ExternalInput").ap()
    Wq = nc.dram_tensor("Wq", [E, E], F32, kind="ExternalInput").ap()
    Wk = nc.dram_tensor("Wk", [E, E], F32, kind="ExternalInput").ap()
    Wv = nc.dram_tensor("Wv", [E, E], F32, kind="ExternalInput").ap()
    Wo = nc.dram_tensor("Wo", [E, E], F32, kind="ExternalInput").ap()
    Wg1 = nc.dram_tensor("Wg1", [E, D], F32, kind="ExternalInput").ap()
    Wg2 = nc.dram_tensor("Wg2", [D, E], F32, kind="ExternalInput").ap()
    wgT = nc.dram_tensor("wgT", [E, INNER], F32, kind="ExternalInput").ap()
    wuSL = nc.dram_tensor("wuSL", [32, E, 128], F32, kind="ExternalInput").ap()
    wdT = nc.dram_tensor("wdT", [INNER, E], F32, kind="ExternalInput").ap()
    gam = nc.dram_tensor("gam", [128, 8], F32, kind="ExternalInput").ap()
    bet = nc.dram_tensor("bet", [128, 8], F32, kind="ExternalInput").ap()
    cosd = nc.dram_tensor("cosd", [128, TOK], F32, kind="ExternalInput").ap()
    ssin = nc.dram_tensor("ssin", [128, TOK], F32, kind="ExternalInput").ap()
    selc = nc.dram_tensor("selc", [64, 8], F32, kind="ExternalInput").ap()
    y = nc.dram_tensor("y", [E, TOK], F32, kind="ExternalOutput").ap()

    kfm_sp = nc.dram_tensor("kfm_sp", [128, 16384], BF16).ap()
    ktm_sp = nc.dram_tensor("ktm_sp", [128, 16384], BF16).ap()
    v_sp = nc.dram_tensor("v_sp", [128, 16384], BF16).ap()
    xmid_sp = nc.dram_tensor("xmid_sp", [128, 16384], F32).ap()
    wd_t = nc.dram_tensor("wd_t", [INNER, E], BF16).ap()
    wu_t = nc.dram_tensor("wu_t", [32, 128, E], BF16).ap()
    cc_in = nc.dram_tensor("cc_in_t", [64, 1024], F32).ap()
    cc_out = nc.dram_tensor("cc_out_t", [512, 1024], F32,
                            addr_space="Shared").ap()

    dbg = {}
    for name, shape in dumps:
        dbg[name] = nc.dram_tensor("dbg_" + name, list(shape), F32,
                                   kind="ExternalOutput").ap()

    for s in ("cc", "pe", "act", "dve", "pool", "gpc"):
        p.sem(s)
    p.rot = {"dsp": 0, "dgp": 0}
    for q in ("dsp", "dgp"):
        for i in range(P.NPOOL):
            p.sem(f"{q}{i}")

    from contextlib import ExitStack
    octx = ExitStack()

    def osb(name, shape, dt=F32):
        return octx.enter_context(nc.sbuf_tensor(name, shape, dt))

    ident = osb("ident", [128, 128], BF16)
    mask4 = osb("mask4", [128, 512], F32)
    ones_col = osb("ones_col", [128, 1], BF16)
    ones_row = osb("ones_row", [1, 128], BF16)
    gam_sb = osb("gam_sb", [128, 8])
    bet_sb = osb("bet_sb", [128, 8])
    mcol = osb("mcol", [128, 1])
    mrow = osb("mrow", [128, 512], BF16)
    wsc = osb("wsc", [128, 8])
    wacc = osb("wacc", [128, 192])
    wab_bf = osb("wab_bf", [128, 128], BF16)
    tiny = osb("tiny", [1, 16])
    tiny_bf = osb("tiny_bf", [1, 16], BF16)
    selc_sb = osb("selc_sb", [64, 8])
    zrow_bf = osb("zrow_bf", [1, 512], BF16)
    cc_bf = osb("cc_bf", [64, 1024], BF16)
    seldg = osb("seldg", [64, 512], BF16)

    def setup(g):
        g.memset(ident[:, :], 0.0)
        g.affine_select(out=ident[:, :], in_=ident[:, :], compare_op=OP.not_equal,
                        fill=1.0, base=0, pattern=[[-1, 128]], channel_multiplier=1)
        g.memset(mask4[:, :], 1.0)
        g.affine_select(out=mask4[:, :].rearrange("p (r n) -> p r n", r=4),
                        in_=mask4[:, :].rearrange("p (r n) -> p r n", r=4),
                        compare_op=OP.is_ge, fill=0.0, base=0,
                        pattern=[[0, 4], [1, 128]], channel_multiplier=-1)
        g.memset(ones_col[:, :], 1.0)
        g.memset(ones_row[:, :], 1.0)
        g.memset(wacc[:, :], 0.0)
        g.memset(zrow_bf[:, :], 0.0)
        g.memset(mrow[:, :], MRND)
        g.memset(mcol[:, :], MRND).then_inc(p.sems["gpc"], 1)
    p.plain("gp", setup)
    p.cnt["gpc"] += 1
    h_cst = [p.dma("gp", "dgp", gam_sb[:, :], gam),
             p.dma("gp", "dgp", bet_sb[:, :], bet),
             p.dma("gp", "dgp", selc_sb[:, :], selc)]
    p.wait("dve", "gpc", 1)
    p.wait("pool", "gpc", 1)
    p.wait("act", "gpc", 1)
    p.wdma("dve", *h_cst)
    p.wdma("pe", *h_cst)

    ab = ExitStack()

    def absb(name, shape, dt=F32):
        return ab.enter_context(nc.sbuf_tensor(name, shape, dt))

    cos_m = absb("cos_m", [128, 512])
    sin_m = absb("sin_m", [128, 512])
    W1 = absb("W1", [128, 8192], BF16)
    W2 = absb("W2", [128, 8192], BF16)
    Wg1_sb = absb("Wg1_sb", [128, 512], BF16)
    Wg2_sb = absb("Wg2_sb", [64, 1024], BF16)
    xt_bf = absb("xt_bf", [128, 4096], BF16)
    FA = absb("FA", [128, 4096])
    FB = absb("FB", [128, 4096])
    FC = absb("FC", [128, 4096])
    krot = absb("krot", [128, 4096], BF16)
    ktm = absb("ktm", [128, 4096], BF16)
    vt = absb("vt", [128, 4096], BF16)
    kfmB = absb("kfmB", [128, 4096], BF16)
    q_oddM = absb("q_oddM", [64, 4096], BF16)
    k_oddM = absb("k_oddM", [64, 4096], BF16)
    eT_sb = absb("eT_sb", [128, 2048], BF16)
    S_bf = absb("S_bf", [64, 1024], BF16)
    stat_bf = absb("stat_bf", [128, 2048], BF16)
    xn_bf = absb("xn_bf", [128, 4096], BF16)
    swd = absb("swd", [128, 512])
    t1d = absb("t1d", [128, 512])
    t2d = absb("t2d", [128, 512])
    swp = absb("swp", [128, 512])
    t1p = absb("t1p", [128, 512])
    t2p = absb("t2p", [128, 512])
    xg1_bf = absb("xg1_bf", [64, 512], BF16)
    lnr = absb("lnr", [1, 2048])
    cc_sb = FB[0:64, 0:1024]
    sin_f = FB[0:64, 1024:2048]
    lnr_bf = absb("lnr_bf", [1, 2048], BF16)
    brstd = absb("brstd", [128, 512])
    bmrs = absb("bmrs", [128, 512])
    wmb = absb("wmb", [128, 1024])
    jA = absb("jA", [128, 512])

    wtb = absb("wtb", [128, 512], BF16)
    wsb_bf = absb("wsb_bf", [128, 2], BF16)
    oscr_b = absb("oscr_b", [128, 512])

    abps = ExitStack()
    ps_S = abps.enter_context(nc.psum_tensor("ps_S", [64, 1024], F32))
    ps_W1 = abps.enter_context(nc.psum_tensor("ps_W1", [128, 512], F32))
    ps_W2 = abps.enter_context(nc.psum_tensor("ps_W2", [128, 512], F32))

    aps = ExitStack()
    ps_V = aps.enter_context(nc.psum_tensor("ps_V", [128, 1024], F32))
    ps_T = aps.enter_context(nc.psum_tensor("ps_T", [128, 128], BF16))

    h_w = [p.dma("gp", "dgp", W1[:, :].rearrange("p (f o) -> p f o", f=8),
                 Wk.rearrange("(f p) o -> p f o", p=128)),
           p.dma("gp", "dgp", W2[:, :].rearrange("p (f o) -> p f o", f=8),
                 Wv.rearrange("(f p) o -> p f o", p=128)),
           p.dma("gp", "dgp", Wg1_sb[:, :].rearrange("p (f o) -> p f o", f=8),
                 Wg1.rearrange("(f p) o -> p f o", p=128)),
           p.dma("gp", "dgp", Wg2_sb[:, :], Wg2)]

    def emit_rotary(eng, cols, src, dst_bf, sw, t1, t2):
        def fn(e):
            e.tensor_copy(sw[0:32, :], src[32:64, cols])
            e.tensor_copy(sw[32:64, :], src[0:32, cols])
            e.tensor_copy(sw[64:96, :], src[96:128, cols])
            e.tensor_copy(sw[96:128, :], src[64:96, cols])
            e.tensor_tensor(t1[:, :], src[:, cols], cos_m[:, :], OP.mult)
            e.tensor_tensor(t2[:, :], sw[:, :], sin_m[:, :], OP.mult)
            return e.tensor_tensor(dst_bf[:, cols], t1[:, :], t2[:, :], OP.add)
        p.step(eng, fn)

    def spacer(e):
        return e.tensor_scalar(jA[:, 256:512], jA[:, 0:256], 1.0, None, OP.mult)

    wj_hist = []

    def emit_wmean_job(src_ap, jcol):
        j = emit_wmean_job.i
        emit_wmean_job.i += 1
        buf = wmb[:, (j % 2) * 512:(j % 2) * 512 + 512]
        q, dq = ("sp", "dsp") if j % 2 == 0 else ("gp", "dgp")
        if j >= 2:
            p.wait(q, wj_hist[j - 2][0], wj_hist[j - 2][1])
        else:
            p.wait(q, "act")
            p.wait(q, "dve")
        hb = p.dma(q, dq, buf, src_ap)
        p.wdma("act", hb)
        p.wait("act", "pool")

        def f1(e, buf=buf, jcol=jcol):
            return e.activation(jA[:, 0:512], buf, AF.Abs,
                                accum_out=wacc[:, jcol:jcol + 1])
        wj_hist.append(("act", p.step("act", f1)))
    emit_wmean_job.i = 0

    def emit_wscale(jc0, n_elems, scol, icol):
        p.wait("dve", "act")

        def sp1(e, jc0=jc0):
            a, b = t1d, t2d
            e.tensor_tensor(a[:, 0:32], wacc[:, jc0:jc0 + 32],
                            wacc[:, jc0 + 32:jc0 + 64], OP.add)
            spacer(e)
            e.tensor_tensor(a[:, 32:48], a[:, 0:16], a[:, 16:32], OP.add)
            spacer(e)
            e.tensor_tensor(a[:, 48:56], a[:, 32:40], a[:, 40:48], OP.add)
            spacer(e)
            e.tensor_tensor(a[:, 56:60], a[:, 48:52], a[:, 52:56], OP.add)
            spacer(e)
            e.tensor_tensor(a[:, 60:62], a[:, 56:58], a[:, 58:60], OP.add)
            spacer(e)
            e.tensor_tensor(a[:, 62:63], a[:, 60:61], a[:, 61:62], OP.add)
            spacer(e)
            e.tensor_copy(b[0:64, 0:1], a[64:128, 62:63])
            spacer(e)
            e.tensor_tensor(b[0:64, 1:2], a[0:64, 62:63], b[0:64, 0:1], OP.add)
            spacer(e)
            e.tensor_copy(b[0:32, 2:3], b[32:64, 1:2])
            spacer(e)
            e.tensor_tensor(b[0:32, 3:4], b[0:32, 1:2], b[0:32, 2:3], OP.add)
            spacer(e)
            e.tensor_scalar(b[0:32, 4:5], b[0:32, 3:4], MRND, None, OP.add)
            spacer(e)
            e.tensor_scalar(b[0:32, 10:11], b[0:32, 4:5], -MRND, None, OP.add)
            spacer(e)
            e.tensor_tensor(b[0:32, 5:6], b[0:32, 3:4], b[0:32, 10:11], OP.subtract)
            spacer(e)
            e.tensor_scalar(b[0:32, 6:7], b[0:32, 10:11], 1.0 / 256.0, MRND,
                            OP.mult, OP.add)
            spacer(e)
            e.tensor_scalar(b[0:32, 7:8], b[0:32, 6:7], -MRND, 256.0,
                            OP.add, OP.mult)
            spacer(e)
            e.tensor_tensor(b[0:32, 8:9], b[0:32, 10:11], b[0:32, 7:8], OP.subtract)
            spacer(e)
            e.tensor_copy(wab_bf[0:32, 0:1], b[0:32, 7:8])
            e.tensor_copy(wab_bf[0:32, 1:2], b[0:32, 8:9])
            e.tensor_copy(wab_bf[0:32, 2:3], b[0:32, 5:6])
            spacer(e)
            e.scalar_tensor_tensor(b[0:32, 9:10], wab_bf[0:32, 2:3], -1.0,
                                   b[0:32, 5:6], OP.mult, OP.add)
            spacer(e)
            return e.tensor_copy(wab_bf[0:32, 3:4], b[0:32, 9:10])
        p.step("dve", sp1)
        p.wait("pe", "dve")

        def sm(e):
            return e.matmul(ps_W1[0:1, 0:4], ones_col[0:32, 0:1],
                            wab_bf[0:32, 0:4], start=True, stop=True)
        p.step("pe", sm)
        p.wait("act", "pe")

        def sacc_f(e):
            return e.activation(jA[0:1, 0:4], ps_W1[0:1, 0:4], AF.Identity,
                                accum_out=tiny[0:1, 0:1])
        p.step("act", sacc_f)
        p.wait("dve", "act")

        def sc(e, n_elems=n_elems):
            e.tensor_scalar(tiny[0:1, 2:3], tiny[0:1, 0:1], 1.0 / n_elems, 1e-5,
                            OP.mult, OP.max)
            spacer(e)
            e.reciprocal(tiny[0:1, 3:4], tiny[0:1, 2:3])
            spacer(e)
            e.tensor_tensor(tiny[0:1, 5:6], tiny[0:1, 2:3], tiny[0:1, 3:4], OP.mult)
            spacer(e)
            e.tensor_scalar(tiny[0:1, 6:7], tiny[0:1, 5:6], -1.0, 2.0,
                            OP.mult, OP.add)
            spacer(e)
            e.tensor_tensor(tiny[0:1, 7:8], tiny[0:1, 3:4], tiny[0:1, 6:7], OP.mult)
            spacer(e)
            e.tensor_copy(tiny_bf[0:1, 0:1], tiny[0:1, 7:8])
            spacer(e)
            e.scalar_tensor_tensor(tiny[0:1, 4:5], tiny_bf[0:1, 0:1], -1.0,
                                   tiny[0:1, 7:8], OP.mult, OP.add)
            spacer(e)
            e.tensor_copy(tiny_bf[0:1, 1:2], tiny[0:1, 4:5])
            e.tensor_copy(tiny_bf[0:1, 2:3], tiny[0:1, 2:3])
            spacer(e)
            e.scalar_tensor_tensor(tiny[0:1, 9:10], tiny_bf[0:1, 1:2], -1.0,
                                   tiny[0:1, 4:5], OP.mult, OP.add)
            e.scalar_tensor_tensor(tiny[0:1, 8:9], tiny_bf[0:1, 2:3], -1.0,
                                   tiny[0:1, 2:3], OP.mult, OP.add)
            spacer(e)
            e.tensor_copy(tiny_bf[0:1, 4:5], tiny[0:1, 9:10])
            e.tensor_copy(tiny_bf[0:1, 3:4], tiny[0:1, 8:9])
            spacer(e)
            e.scalar_tensor_tensor(tiny[0:1, 10:11], tiny_bf[0:1, 3:4], -1.0,
                                   tiny[0:1, 8:9], OP.mult, OP.add)
            spacer(e)
            return e.tensor_copy(tiny_bf[0:1, 5:6], tiny[0:1, 10:11])
        p.step("dve", sc)
        p.wait("pe", "dve")

        def bc(e):
            e.matmul(ps_W2[:, 0:1], ones_row[:, :], tiny_bf[0:1, 0:1], start=True, stop=False)
            e.matmul(ps_W2[:, 0:1], ones_row[:, :], tiny_bf[0:1, 1:2], start=False, stop=False)
            e.matmul(ps_W2[:, 0:1], ones_row[:, :], tiny_bf[0:1, 4:5], start=False, stop=True)
            e.matmul(ps_W2[:, 1:2], ones_row[:, :], tiny_bf[0:1, 2:3],
                     start=True, stop=False)
            e.matmul(ps_W2[:, 1:2], ones_row[:, :], tiny_bf[0:1, 3:4],
                     start=False, stop=False)
            return e.matmul(ps_W2[:, 1:2], ones_row[:, :], tiny_bf[0:1, 5:6],
                            start=False, stop=True)
        p.step("pe", bc)
        p.wait("dve", "pe")

        def cv(e, scol=scol, icol=icol):
            e.tensor_copy(wsc[:, scol:scol + 1], ps_W2[:, 0:1])
            return e.tensor_copy(wsc[:, icol:icol + 1], ps_W2[:, 1:2])
        p.step("dve", cv)

    wd_flatI = wdT.rearrange("a b -> (a b)").rearrange("(c p t) -> c p t",
                                                        p=128, t=512)
    wd_flatO = wd_t.rearrange("a b -> (a b)").rearrange("(c p t) -> c p t",
                                                        p=128, t=512)

    def emit_wtern_job(j):
        ri = emit_wmean_job.i
        emit_wmean_job.i += 1
        buf = wmb[:, (ri % 2) * 512:(ri % 2) * 512 + 512]
        q, dq = ("sp", "dsp") if ri % 2 == 0 else ("gp", "dgp")
        if ri >= 2:
            p.wait(q, wj_hist[ri - 2][0], wj_hist[ri - 2][1])
        else:
            p.wait(q, "act")
            p.wait(q, "dve")
            p.wait(q, "pool")
        hb = p.dma(q, dq, buf, wd_flatI[j, :, :])
        p.wdma("dve", hb)
        p.wait("dve", "pool")

        def f1(e, buf=buf):
            return e.scalar_tensor_tensor(swd[:, :], buf, wsc[:, 0:1],
                                          mrow[:, :], OP.mult, OP.add)
        wj_hist.append(("dve", p.step("dve", f1)))
        p.wait("pool", "dve")

        def f2(e):
            e.tensor_scalar(oscr_b[:, :], swd[:, :], MRND + 1.0, MRND - 1.0,
                            OP.min, OP.max)
            return e.tensor_scalar(wtb[:, :], oscr_b[:, :], -MRND, None, OP.add)
        p.step("pool", f2)
        p.wait("sp", "pool")
        p.dma("sp", "dsp", wd_flatO[j, :, :], wtb[:, :])

    def emit_ln(src, gamma, out_bf=None, out_f32=None):
        for c in range(8):
            cl = slice(c * 512, (c + 1) * 512)
            r = (c % 2) * 512

            def cp(e, cl=cl, r=r):
                return e.tensor_copy(stat_bf[:, r:r + 512], src[:, cl])
            p.step("dve", cp)
            p.wait("act", "dve")

            def sq(e, cl=cl, r=r):
                return e.activation(stat_bf[:, 1024 + r:1024 + r + 512], src[:, cl],
                                    AF.Square)
            p.step("act", sq)
            p.wait("pe", "dve")
            p.wait("pe", "act")

            def st(e, c=c, r=r):
                e.matmul(ps_W1[0:1, :], ones_col[:, :], stat_bf[:, r:r + 512],
                         start=(c == 0), stop=(c == 7))
                return e.matmul(ps_W2[0:1, :], ones_col[:, :],
                                stat_bf[:, 1024 + r:1024 + r + 512],
                                start=(c == 0), stop=(c == 7))
            v = p.step("pe", st)
            if c < 7:
                p.wait("dve", "pe", v)
                p.wait("act", "pe", v)
        p.wait("dve", "pe")

        def chain(e):
            L = lnr
            mu, var, yy, tmp = (L[0:1, i * 512:(i + 1) * 512] for i in range(4))
            mrs = tmp
            t2 = var
            e.tensor_scalar(mu, ps_W1[0:1, :], 1.0 / E, None, OP.mult)
            e.tensor_scalar(tmp, ps_W2[0:1, :], 1.0 / E, EPS, OP.mult, OP.add)
            e.tensor_tensor(var, mu, mu, OP.mult)
            e.tensor_tensor(var, tmp, var, OP.subtract)
            e.tensor_copy(lnr_bf[0:1, 0:512], var)
            e.tensor_copy(tmp, lnr_bf[0:1, 0:512].bitcast(U16))
            e.tensor_scalar(tmp, tmp, 0.5, MRND, OP.mult, OP.add)
            e.tensor_scalar(tmp, tmp, -MRND, None, OP.add)
            e.tensor_scalar(tmp, tmp, -1.0, 24375.0, OP.mult, OP.add)
            e.tensor_copy(lnr_bf[0:1, 0:512].bitcast(U16), tmp)
            e.tensor_copy(yy, lnr_bf[0:1, 0:512])
            for _ in range(3):
                e.tensor_tensor(tmp, yy, yy, OP.mult)
                e.tensor_tensor(tmp, tmp, var, OP.mult)
                e.tensor_scalar(tmp, tmp, -0.5, 1.5, OP.mult, OP.add)
                e.tensor_tensor(yy, yy, tmp, OP.mult)
            e.tensor_tensor(mrs, mu, yy, OP.mult)
            e.tensor_copy(lnr_bf[0:1, 0:512], yy)
            e.tensor_tensor(t2, yy, lnr_bf[0:1, 0:512], OP.subtract)
            e.tensor_copy(lnr_bf[0:1, 512:1024], t2)
            e.tensor_copy(lnr_bf[0:1, 1024:1536], mrs)
            e.tensor_tensor(t2, mrs, lnr_bf[0:1, 1024:1536], OP.subtract)
            return e.tensor_copy(lnr_bf[0:1, 1536:2048], t2)
        p.step("dve", chain)
        p.wait("pe", "dve")

        def bc(e):
            e.matmul(ps_W1[:, :], ones_row[:, :], lnr_bf[0:1, 0:512], start=True, stop=False)
            e.matmul(ps_W1[:, :], ones_row[:, :], lnr_bf[0:1, 512:1024], start=False, stop=True)
            e.matmul(ps_W2[:, :], ones_row[:, :], lnr_bf[0:1, 1024:1536], start=True, stop=False)
            return e.matmul(ps_W2[:, :], ones_row[:, :], lnr_bf[0:1, 1536:2048],
                            start=False, stop=True)
        p.step("pe", bc)
        p.wait("dve", "pe")

        def bev(e):
            e.tensor_copy(brstd[:, :], ps_W1[:, :])
            return e.tensor_copy(bmrs[:, :], ps_W2[:, :])
        p.step("dve", bev)
        p.wait("pool", "dve")
        for c in range(8):
            cl = slice(c * 512, (c + 1) * 512)
            eng = "dve" if c % 2 == 0 else "pool"
            tb = t1d if eng == "dve" else t1p

            def ap_fn(e, cl=cl, c=c, tb=tb):
                e.tensor_tensor(tb[:, :], src[:, cl], brstd[:, :], OP.mult)
                if gamma:
                    e.tensor_tensor(tb[:, :], tb[:, :], bmrs[:, :], OP.subtract)
                    return e.tensor_scalar(out_bf[:, cl], tb[:, :],
                                           gam_sb[:, c:c + 1], bet_sb[:, c:c + 1],
                                           OP.mult, OP.add)
                dst = out_f32 if out_f32 is not None else out_bf
                return e.tensor_tensor(dst[:, cl], tb[:, :], bmrs[:, :], OP.subtract)
            p.step(eng, ap_fn)

    p.wait("pe", "gpc", 1)

    def s_dummy(e):
        e.matmul(ps_S[:, 0:512], ones_row[0:1, 0:64], zrow_bf[0:1, 0:512],
                 start=True, stop=True)
        return e.matmul(ps_S[:, 512:1024], ones_row[0:1, 0:64], zrow_bf[0:1, 0:512],
                        start=True, stop=True)
    p.step("pe", s_dummy)

    for m in range(NMAC):
        mc = slice(m * MT, (m + 1) * MT)
        p.wait("gp", "pe")
        h_xt = p.dma("gp", "dgp", xt_bf[:, :].rearrange("p (f t) -> p f t", f=8),
                     xT.rearrange("(f p) t -> p f t", p=128)[:, :, mc])
        p.wait("sp", "dve")
        p.wait("sp", "pool")
        h_tab = [p.dma("sp", "dsp", cos_m[:, :], cosd[:, mc]),
                 p.dma("sp", "dsp", sin_m[:, :], ssin[:, mc])]

        p.wdma("pe", h_xt)
        if m == 0:
            p.wdma("pe", *h_w)
            p.wait("pe", "gpc", 1)
        kev_hist = []
        for oc in range(8):
            pst = ps_W1 if oc % 2 == 0 else ps_W2

            def kproj(e, oc=oc, pst=pst):
                ins = None
                for f in range(8):
                    ins = e.matmul(pst[:, :],
                                   W1[:, f * 1024 + oc * 128: f * 1024 + (oc + 1) * 128],
                                   xt_bf[:, f * 512:(f + 1) * 512],
                                   start=(f == 0), stop=(f == 7))
                return ins
            if oc >= 2:
                p.wait("pe", "act", kev_hist[oc - 2])
            p.step("pe", kproj)
            p.wait("act", "pe")

            def ev(e, oc=oc, pst=pst):
                return e.activation(FA[:, oc * 512:(oc + 1) * 512], pst[:, :], AF.Silu)
            kev_hist.append(p.step("act", ev))

        p.wait("dve", "act")
        p.wdma("dve", *h_tab)
        p.wait("pool", "act")
        p.wdma("pool", *h_tab)
        if m > 0:
            p.wdma("dve", *h_spill)
            p.wdma("pool", *h_spill)
        for c in range(8):
            cols = slice(c * 512, (c + 1) * 512)
            if c < 5:
                emit_rotary("dve", cols, FA, krot, swd, t1d, t2d)
            else:
                emit_rotary("pool", cols, FA, krot, swp, t1p, oscr_b)

        p.wait("pe", "dve")
        p.wait("pe", "pool")
        for s in range(4):
            for c in range(8):
                def tp(e, s=s, c=c):
                    return e.transpose(ps_T[:, :],
                                       krot[:, c * 512 + s * 128: c * 512 + (s + 1) * 128],
                                       ident[:, :])
                p.step("pe", tp)
                p.wait("dve", "pe")

                def tpe(e, s=s, c=c):
                    return e.tensor_copy(
                        ktm[:, s * 1024 + c * 128: s * 1024 + (c + 1) * 128], ps_T[:, :])
                v = p.step("dve", tpe)
                p.wait("pe", "dve", v)

        p.wait("sp", "dve")
        p.wait("sp", "pool")
        h_spill = [p.dma("sp", "dsp", kfm_sp[:, m * 4096:(m + 1) * 4096], krot[:, :])]
        p.wait("gp", "dve")
        p.wait("gp", "pe")
        p.wait("gp", "pool")
        h_spill.append(p.dma("gp", "dgp", ktm_sp[:, m * 4096:(m + 1) * 4096],
                             ktm[:, :]))

        for s in range(4):
            def vproj(e, s=s):
                ins = None
                for f in range(8):
                    for hf in range(2):
                        ins = e.matmul(ps_V[:, hf * 512:(hf + 1) * 512],
                                       xt_bf[:, f * 512 + s * 128: f * 512 + (s + 1) * 128],
                                       W2[:, f * 1024 + hf * 512: f * 1024 + (hf + 1) * 512],
                                       start=(f == 0), stop=(f == 7))
                return ins
            if s >= 1 or m >= 1:
                p.wait("pe", "dve")
            p.step("pe", vproj)
            p.wait("dve", "pe")

            def vev(e, s=s):
                return e.tensor_copy(vt[:, s * 1024:(s + 1) * 1024], ps_V[:, :])
            p.step("dve", vev)

        p.wait("sp", "dve")
        h_spill.append(p.dma("sp", "dsp", v_sp[:, m * 4096:(m + 1) * 4096], vt[:, :]))
        if "vsp" in dbg:
            p.wait("gp", "dve")
            p.dma("gp", "dgp", dbg["vsp"][:, m * 4096:(m + 1) * 4096], vt[:, :])
        if "kfm" in dbg:
            p.wait("gp", "dve")
            p.wait("gp", "pool")
            p.dma("gp", "dgp", dbg["kfm"][:, m * 4096:(m + 1) * 4096], krot[:, :])
        if "ktm" in dbg:
            p.wait("gp", "dve")
            p.dma("gp", "dgp", dbg["ktm"][:, m * 4096:(m + 1) * 4096], ktm[:, :])

        p.wait("pe", "dve")

        def dsacc(e, m=m):
            ins = None
            for s in range(4):
                for h in range(HCNT):
                    ins = e.matmul(ps_S[:, h * 64:(h + 1) * 64],
                                   ktm[:, s * 1024 + h * 64: s * 1024 + (h + 1) * 64],
                                   vt[:, s * 1024 + h * 64: s * 1024 + (h + 1) * 64],
                                   start=False, stop=(m == 3 and s == 3))
            return ins
        p.step("pe", dsacc)

        wd_flat = wdT.rearrange("a b -> (a b)").rearrange("(c p t) -> c p t",
                                                          p=128, t=512)
        for ci in range(16):
            g = m * 16 + ci
            emit_wmean_job(wd_flat[g, :, :], g)

    emit_wscale(0, INNER * E, 0, 1)

    p.wait("dve", "pe")

    def sa_ev(e):
        return e.tensor_copy(sin_f, ps_S[:, :])
    sa_v = p.step("dve", sa_ev)
    if "sat" in dbg:
        p.wait("sp", "dve", sa_v)
        p.dma("sp", "dsp", dbg["sat"][:, :], sin_f)
    p.wait("gp", "dve", sa_v)
    cc_in_sent = p.dma("gp", "dgp", cc_in, sin_f)
    p.wdma("gp", cc_in_sent)

    def coll(g):
        g.collective_compute(
            "AllGather", OP.bypass, replica_groups=[list(range(8))],
            ins=[cc_in.opt()], outs=[cc_out.opt()]).then_inc(p.sems["cc"], 1)
    p.plain("gp", coll)
    p.cnt["cc"] += 1
    p.wait("gp", "cc", 1)

    if "ccg" in dbg:
        p.dma("gp", "dgp", dbg["ccg"], cc_out)

    def mk_seldg(e):
        ins = None
        for rr in range(8):
            ins = e.tensor_scalar(seldg[:, rr * 64:(rr + 1) * 64],
                                  ident[0:64, 0:64], selc_sb[:, rr:rr + 1], None,
                                  OP.mult)
        return ins
    p.step("dve", mk_seldg)

    for rr in range(8):
        p.wait("gp", "dve")
        h_blk = p.dma("gp", "dgp", cc_sb, cc_out[rr * 64:(rr + 1) * 64, :])
        p.wdma("dve", h_blk)
        p.wait("dve", "pe")

        def blkcast(e):
            return e.tensor_copy(cc_bf[:, :], cc_sb)
        bc_v = p.step("dve", blkcast)
        p.wait("pe", "dve", bc_v)

        def sacc(e, rr=rr):
            e.matmul(ps_S[:, 0:512], seldg[:, rr * 64:(rr + 1) * 64],
                     cc_bf[:, 0:512], start=(rr == 0), stop=(rr == 7))
            return e.matmul(ps_S[:, 512:1024], seldg[:, rr * 64:(rr + 1) * 64],
                            cc_bf[:, 512:1024], start=(rr == 0), stop=(rr == 7))
        p.step("pe", sacc)

    if "sdg" in dbg:
        p.wait("gp", "dve")
        p.dma("gp", "dgp", dbg["sdg"][:, :], seldg[:, :])
        p.dma("gp", "dgp", dbg["ccbf"][:, :], cc_bf[:, :])
    if "sa" in dbg:
        p.wait("dve", "pe")

        def sadump(e):
            return e.tensor_copy(sin_f, ps_S[:, :])
        sd_v = p.step("dve", sadump)
        p.wait("pe", "dve", sd_v)
        p.wait("sp", "dve", sd_v)
        p.dma("sp", "dsp", dbg["sa"][:, :], sin_f)
        p.wdma_all("pe", "dsp")

    if STAGE < 2:
        for c in range(8):
            p.dma("sp", "dsp", y[c * 128:(c + 1) * 128, :], FA[:, 0:2048])
        p.wdma_all("sp", "dsp")
        p.wdma_all("gp", "dgp")
        _emit_all(nc, p)
        aps.close()
        abps.close()
        ab.close()
        octx.close()
        return nc

    aps.close()
    bps = ExitStack()
    ps_O1 = bps.enter_context(nc.psum_tensor("ps_O1", [64, 1024], F32))
    ps_O2 = bps.enter_context(nc.psum_tensor("ps_O2", [64, 1024], F32))

    p.wait("gp", "pe")
    h_wb = [p.dma("gp", "dgp", W1[:, :].rearrange("p (f o) -> p f o", f=8),
                  Wq.rearrange("(f p) o -> p f o", p=128)),
            p.dma("gp", "dgp", W2[:, :].rearrange("p (f o) -> p f o", f=8),
                  Wo.rearrange("(f p) o -> p f o", p=128))]


    wmean_jobs = []
    wg_flat = wgT.rearrange("a b -> (a b)").rearrange("(c p t) -> c p t",
                                                     p=128, t=512)
    for ci in range(64):
        wmean_jobs.append((wg_flat[ci, :, :], 64 + ci))
    wu_flat = wuSL.rearrange("a b c -> (a b c)").rearrange("(c p t) -> c p t",
                                                           p=128, t=512)
    for ci in range(64):
        wmean_jobs.append((wu_flat[ci, :, :], 128 + ci))
    wj = [0]
    tj = [0]

    for m in range(NMAC):
        mc = slice(m * MT, (m + 1) * MT)
        p.wait("gp", "pe")
        h_xt = p.dma("gp", "dgp", xt_bf[:, :].rearrange("p (f t) -> p f t", f=8),
                     xT.rearrange("(f p) t -> p f t", p=128)[:, :, mc])
        p.wait("sp", "dve")
        p.wait("sp", "pool")
        p.wait("sp", "pe")
        h_kv = [p.dma("sp", "dsp", kfmB[:, :], kfm_sp[:, m * 4096:(m + 1) * 4096]),
                p.dma("sp", "dsp", ktm[:, :], ktm_sp[:, m * 4096:(m + 1) * 4096])]
        p.wait("gp", "dve")
        p.wait("gp", "pe")
        h_kv.append(p.dma("gp", "dgp", vt[:, :], v_sp[:, m * 4096:(m + 1) * 4096]))
        h_tab = [p.dma("sp", "dsp", cos_m[:, :], cosd[:, mc]),
                 p.dma("sp", "dsp", sin_m[:, :], ssin[:, mc])]
        p.wait("dve", "pool")
        p.wait("pool", "dve")
        p.wait("act", "dve")
        p.wait("act", "pool")
        p.wdma("dve", *h_kv)

        qev_hist = []
        p.wdma("pe", h_xt)
        if m == 0:
            p.wdma("pe", *h_wb)
        for oc in range(8):
            pst = ps_W1 if oc % 2 == 0 else ps_W2

            def qproj(e, oc=oc, pst=pst):
                ins = None
                for f in range(8):
                    ins = e.matmul(pst[:, :],
                                   W1[:, f * 1024 + oc * 128: f * 1024 + (oc + 1) * 128],
                                   xt_bf[:, f * 512:(f + 1) * 512],
                                   start=(f == 0), stop=(f == 7))
                return ins
            if oc >= 2:
                p.wait("pe", "act", qev_hist[oc - 2])
            p.step("pe", qproj)
            p.wait("act", "pe")

            def ev(e, oc=oc, pst=pst):
                return e.activation(FA[:, oc * 512:(oc + 1) * 512], pst[:, :], AF.Silu)
            qev_hist.append(p.step("act", ev))

        if BSTOP < 2:
            continue
        p.wait("dve", "act")
        p.wdma("dve", *h_tab)
        p.wait("pool", "act")
        p.wdma("pool", *h_tab)
        for c in range(8):
            cols = slice(c * 512, (c + 1) * 512)
            if c < 5:
                emit_rotary("dve", cols, FA, krot, swd, t1d, t2d)
            else:
                emit_rotary("pool", cols, FA, krot, swp, t1p, oscr_b)
        p.wait("dve", "pool")

        def oddcp(e):
            e.tensor_copy(q_oddM[0:64, :], krot[64:128, :])
            return e.tensor_copy(k_oddM[0:64, :], kfmB[64:128, :])
        ocv = p.step("dve", oddcp)
        if m == 0 and "qod" in dbg:
            p.wait("gp", "dve", ocv)
            p.dma("gp", "dgp", dbg["qod"], q_oddM[0:64, :])
            p.dma("gp", "dgp", dbg["kod"], k_oddM[0:64, :])
            if "qrt" in dbg:
                p.dma("gp", "dgp", dbg["qrt"], krot[:, :])
            if "qsl" in dbg:
                p.dma("gp", "dgp", dbg["qsl"], FA[:, 0:4096])

        if BSTOP < 3:
            continue
        def g1(e):
            ins = None
            for f in range(8):
                ins = e.matmul(ps_O1[:, 0:512], Wg1_sb[:, f * 64:(f + 1) * 64],
                               xt_bf[:, f * 512:(f + 1) * 512],
                               start=(f == 0), stop=(f == 7))
            return ins
        p.wait("pe", "dve")
        p.step("pe", g1)
        p.wait("dve", "pe")

        def g1e(e):
            return e.tensor_copy(xg1_bf[:, :], ps_O1[:, 0:512])
        g1v = p.step("dve", g1e)
        p.wait("pe", "dve", g1v)
        for oc in range(8):
            pst = ps_W1 if oc % 2 == 0 else ps_W2

            def g2(e, oc=oc, pst=pst):
                return e.matmul(pst[:, :], Wg2_sb[:, oc * 128:(oc + 1) * 128],
                                xg1_bf[:, :], start=True, stop=True)
            if oc >= 2:
                p.wait("pe", "dve")
            p.step("pe", g2)
            p.wait("act", "pe")
            p.wait("act", "pool")

            def gt(e, oc=oc, pst=pst):
                return e.activation(jA[:, 0:512], pst[:, :], AF.Tanh, scale=0.5)
            p.step("act", gt)
            p.wait("dve", "act")

            def gs(e, oc=oc):
                return e.tensor_scalar(FB[:, oc * 512:(oc + 1) * 512], jA[:, 0:512],
                                       0.5, 0.5, OP.mult, OP.add)
            v = p.step("dve", gs)
            p.wait("act", "dve", v)

        if BSTOP < 4:
            continue
        p.wdma("pe", *h_kv)
        for s in range(4):
            g = m * 4 + s
            p.wait("dve", "pe")

            def sbf(e):
                return e.tensor_copy(S_bf[:, :], ps_S[:, :])
            sv = p.step("dve", sbf)
            p.wait("pe", "dve", sv)

            if ASTOP < 2:
                continue
            for rnd in range(2):
                def emm(e, s=s, rnd=rnd):
                    ins = None
                    for hh in range(8):
                        h = rnd * 8 + hh
                        c = h // 2
                        pst = ps_W1 if hh < 4 else ps_W2
                        col = (hh % 4) * 128
                        cols = slice(c * 512 + s * 128, c * 512 + (s + 1) * 128)
                        kl = (kfmB[0:64, cols] if h % 2 == 0
                              else k_oddM[0:64, cols])
                        ql = (krot[0:64, cols] if h % 2 == 0
                              else q_oddM[0:64, cols])
                        ins = e.matmul(pst[:, col:col + 128], kl, ql,
                                       start=True, stop=True)
                    return ins
                p.step("pe", emm)
                p.wait("dve", "pe")

                def eev(e, rnd=rnd):
                    e.tensor_tensor(eT_sb[:, rnd * 1024: rnd * 1024 + 512],
                                    ps_W1[:, :], mask4[:, :], OP.mult)
                    return e.tensor_tensor(eT_sb[:, rnd * 1024 + 512: rnd * 1024 + 1024],
                                           ps_W2[:, :], mask4[:, :], OP.mult)
                ev_v = p.step("dve", eev)
                p.wait("pe", "dve", ev_v)

            if ASTOP < 3:
                continue
            def avq(e, s=s):
                ins = None
                for h in range(HCNT):
                    c = h // 2
                    pso = ps_O1 if h % 2 == 0 else ps_O2
                    ocol = c * 128
                    e.matmul(pso[:, ocol:ocol + 128],
                             vt[:, s * 1024 + h * 64: s * 1024 + (h + 1) * 64],
                             eT_sb[:, h * 128:(h + 1) * 128],
                             start=True, stop=False)
                    qsl = (krot[0:64, c * 512 + s * 128: c * 512 + (s + 1) * 128]
                           if h % 2 == 0 else
                           q_oddM[0:64, c * 512 + s * 128: c * 512 + (s + 1) * 128])
                    ins = e.matmul(pso[:, ocol:ocol + 128],
                                   S_bf[:, h * 64:(h + 1) * 64], qsl,
                                   start=False, stop=True)
                return ins
            p.step("pe", avq)

            if ASTOP < 4:
                continue
            def dsu(e, s=s, g=g):
                ins = None
                for h in range(HCNT):
                    ins = e.matmul(ps_S[:, h * 64:(h + 1) * 64],
                                   ktm[:, s * 1024 + h * 64: s * 1024 + (h + 1) * 64],
                                   vt[:, s * 1024 + h * 64: s * 1024 + (h + 1) * 64],
                                   start=False, stop=(g == 15))
                return ins
            p.step("pe", dsu)

            if ASTOP < 5:
                continue
            p.wait("dve", "pe")

            def oge(e, s=s):
                ins = None
                for par, pso in ((0, ps_O1), (1, ps_O2)):
                    ov = FC[par * 64:(par + 1) * 64, :].rearrange(
                        "p (c t) -> p c t", c=8)[:, :, s * 128:(s + 1) * 128]
                    iv = pso[:, :].rearrange("p (c t) -> p c t", c=8)
                    gv = FB[par * 64:(par + 1) * 64, :].rearrange(
                        "p (c t) -> p c t", c=8)[:, :, s * 128:(s + 1) * 128]
                    ins = e.tensor_tensor(ov, iv, gv, OP.mult)
                return ins
            ogv = p.step("dve", oge)
            p.wait("pe", "dve", ogv)

        if BSTOP >= 8:
            for _ in range(32):
                if wj[0] < len(wmean_jobs):
                    srcj, col = wmean_jobs[wj[0]]
                    emit_wmean_job(srcj, col)
                    wj[0] += 1
            for _ in range(16):
                if tj[0] < 64:
                    emit_wtern_job(tj[0])
                    tj[0] += 1

        if "og" in dbg:
            p.wait("sp", "dve")
            p.dma("sp", "dsp", dbg["og"][:, m * 4096:(m + 1) * 4096], FC[:, :])
        if "gate" in dbg:
            p.wait("sp", "dve")
            p.dma("sp", "dsp", dbg["gate"][:, m * 4096:(m + 1) * 4096], FB[:, :])
        if BSTOP < 5:
            continue
        emit_ln(FC, True, out_bf=xn_bf)

        if BSTOP < 6:
            continue
        p.wait("pe", "dve")
        p.wait("pe", "pool")
        r1_hist = []
        for oc in range(8):
            pst = ps_W1 if oc % 2 == 0 else ps_W2
            xslot = (t1d if oc % 2 == 0 else t2d)[:, :]
            p.wait("sp", "dve")
            h_x = p.dma("sp", "dsp", xslot,
                        xT[oc * 128:(oc + 1) * 128, mc])

            def wop(e, oc=oc, pst=pst):
                ins = None
                for f in range(8):
                    ins = e.matmul(pst[:, :],
                                   W2[:, f * 1024 + oc * 128: f * 1024 + (oc + 1) * 128],
                                   xn_bf[:, f * 512:(f + 1) * 512],
                                   start=(f == 0), stop=(f == 7))
                return ins
            if oc >= 2:
                p.wait("pe", "dve", r1_hist[oc - 2])
            p.step("pe", wop)
            p.wait("dve", "pe")
            p.wdma("dve", h_x)

            def r1f(e, oc=oc, pst=pst, xslot=xslot):
                return e.tensor_tensor(FA[:, oc * 512:(oc + 1) * 512], pst[:, :],
                                       xslot, OP.add)
            r1_hist.append(p.step("dve", r1f))

        if BSTOP < 7:
            continue
        emit_ln(FA, False, out_f32=FB)

        p.wait("sp", "dve")
        p.wait("sp", "pool")
        p.dma("sp", "dsp", xmid_sp[:, m * 4096:(m + 1) * 4096], FB[:, :])
        if "xmid" in dbg:
            p.dma("sp", "dsp", dbg["xmid"][:, m * 4096:(m + 1) * 4096], FB[:, :])
        if "r1" in dbg:
            p.dma("sp", "dsp", dbg["r1"][:, m * 4096:(m + 1) * 4096], FA[:, :])
        if "xn" in dbg:
            p.wait("gp", "dve")
            p.wait("gp", "pool")
            p.dma("gp", "dgp", dbg["xn"][:, m * 4096:(m + 1) * 4096], xn_bf[:, :])

    if BSTOP >= 8:
        while wj[0] < len(wmean_jobs):
            srcj, col = wmean_jobs[wj[0]]
            emit_wmean_job(srcj, col)
            wj[0] += 1
        while tj[0] < 64:
            emit_wtern_job(tj[0])
            tj[0] += 1

        emit_wscale(64, INNER * E, 2, 4)
        emit_wscale(128, INNER * E, 3, 5)
        if "wsc" in dbg:
            p.wait("sp", "dve")
            p.dma("sp", "dsp", dbg["wsc"], wsc[:, 0:8])
            p.dma("sp", "dsp", dbg["tinyd"], tiny[0:1, :])
            p.dma("sp", "dsp", dbg["waccd"], wacc[:, :])
            if "spdbg" in dbg:
                p.dma("sp", "dsp", dbg["spdbg"], t2d[0:32, 0:16])
                p.wait("gp", "dve")
                p.dma("gp", "dgp", dbg["wabdbg"], wab_bf[0:32, 0:4])
                p.dma("sp", "dsp", dbg["t1dbg"], t1d[:, 0:64])

    if "sa" in dbg:
        p.wait("sp", "dve")
        p.dma("sp", "dsp", dbg["sa"][:, :], sin_f)

    if STAGE < 3:
        for c in range(8):
            p.dma("sp", "dsp", y[c * 128:(c + 1) * 128, :], FA[:, 0:2048])
        p.wdma_all("sp", "dsp")
        p.wdma_all("gp", "dgp")
        _emit_all(nc, p)
        bps.close()
        abps.close()
        ab.close()
        octx.close()
        return nc

    fin = {e: p.cnt[e] for e in ("pe", "act", "dve", "pool")}
    for e in ("pe", "act", "dve", "pool", "sp", "gp"):
        for s in ("pe", "act", "dve", "pool"):
            if s != e:
                p.wait(e, s, fin[s])
        p.wdma_all(e, "dsp")
        p.wdma_all(e, "dgp")

    bps.close()
    abps.close()
    ab.close()

    cs = ExitStack()

    def csb(name, shape, dt=F32):
        return cs.enter_context(nc.sbuf_tensor(name, shape, dt))

    wg_sb = csb("wg_sb", [128, 8 * INNER], BF16)
    junkc = csb("junkc", [128, 1024])
    junk2 = csb("junk2", [128, 1024])
    wstr = csb("wstr", [128, 2048])
    xmid_c = csb("xmid_c", [128, 2048])
    hbuf = csb("hbuf", [128, 8192])
    hq = csb("hq", [128, 8192], BF16)
    xq_deq = csb("xq_deq", [128, 2048], BF16)
    xq_tmp = csb("xq_tmp", [128, 256], BF16)
    gsil = csb("gsil", [128, 256])
    wd_buf = csb("wd_buf", [128, 4096], BF16)
    wu_buf = csb("wu_buf", [128, 4096], BF16)
    rmx = csb("rmx", [128, 1024])
    rmn = csb("rmn", [128, 1024])
    fold = csb("fold", [64, 512])
    fold_bf = csb("fold_bf", [32, 512], BF16)
    tmv = csb("tmv", [128, 16])
    tmv_bf = csb("tmv_bf", [128, 8], BF16)
    rowv_bf = csb("rowv_bf", [1, 1024], BF16)
    bs1 = csb("bs1", [128, 256])
    bsi1 = csb("bsi1", [128, 256])
    bs2 = csb("bs2", [128, 256])
    bsd = csb("bsd", [128, 256])
    ytile = csb("ytile", [128, 2048])
    lnr2 = csb("lnr2", [1, 1280])
    lnr2_bf = csb("lnr2_bf", [1, 1024], BF16)
    br2 = csb("br2", [128, 256])
    bm2 = csb("bm2", [128, 256])
    tscr = csb("tscr", [128, 512])
    oscr = csb("oscr", [128, 512])

    cps = ExitStack()
    ps_GU = cps.enter_context(nc.psum_tensor("ps_GU", [128, 512], F32))
    ps_GU2 = cps.enter_context(nc.psum_tensor("ps_GU2", [128, 512], F32))
    ps_DN = cps.enter_context(nc.psum_tensor("ps_DN", [128, 2048], F32))
    ps_TP = cps.enter_context(nc.psum_tensor("ps_TP", [128, 512], BF16))

    def spacer_c(e):
        return e.tensor_scalar(junkc[:, 256:512], junkc[:, 0:256], 1.0, None,
                               OP.mult)

    cln_bf = hq[:, 0:2048]
    sq3_bf = hq[:, 2048:4096]

    wg_flatC = wgT.rearrange("a b -> (a b)").rearrange("(c p t) -> c p t",
                                                       p=128, t=1024)
    ca_hist = []
    for ci in range(32):
        buf = wstr[:, (ci % 2) * 1024:(ci % 2) * 1024 + 1024]
        jb = junkc if ci % 2 == 0 else junk2
        q, dq = ("sp", "dsp") if ci % 2 == 0 else ("gp", "dgp")
        if ci >= 2:
            p.wait(q, "dve", ca_hist[ci - 2])
        else:
            p.wait(q, "dve")
            p.wait(q, "pool")
        hb = p.dma(q, dq, buf, wg_flatC[ci, :, :])
        p.wdma("dve", hb)
        p.wait("dve", "pool")

        def ca(e, buf=buf, jb=jb):
            e.scalar_tensor_tensor(jb[:, 0:512], buf[:, 0:512], wsc[:, 2:3],
                                   mrow[:, :], OP.mult, OP.add)
            return e.scalar_tensor_tensor(jb[:, 512:1024], buf[:, 512:1024],
                                          wsc[:, 2:3], mrow[:, :], OP.mult, OP.add)
        ca_hist.append(p.step("dve", ca))
        p.wait("pool", "dve")

        def cb(e, ci=ci, jb=jb):
            e.tensor_scalar(oscr[:, 0:512], jb[:, 0:512], MRND + 1.0, MRND - 1.0,
                            OP.min, OP.max)
            e.tensor_scalar(oscr[:, 0:512], jb[:, 512:1024], MRND + 1.0, MRND - 1.0,
                            OP.min, OP.max) if False else None
            e.tensor_scalar(
                wg_sb[:, :].rearrange("p (c t) -> p c t", c=32)[:, ci, 0:512],
                oscr[:, 0:512], -MRND, None, OP.add)
            e.tensor_scalar(oscr[:, 0:512], jb[:, 512:1024], MRND + 1.0, MRND - 1.0,
                            OP.min, OP.max)
            return e.tensor_scalar(
                wg_sb[:, :].rearrange("p (c t) -> p c t", c=32)[:, ci, 512:1024],
                oscr[:, 0:512], -MRND, None, OP.add)
        p.step("pool", cb)
    for oc in range(32):
        buf = wstr[:, (oc % 2) * 1024:(oc % 2) * 1024 + 1024]
        jb = (junkc if oc % 2 == 0 else junk2)[:, 0:1024]
        tb = wu_buf[:, (oc % 2) * 1024:(oc % 2) * 1024 + 1024]
        q, dq = ("sp", "dsp") if oc % 2 == 0 else ("gp", "dgp")
        if oc >= 2:
            p.wait(q, "dve", ca_hist[32 + oc - 2])
        else:
            p.wait(q, "dve")
            p.wait(q, "pool")
        hb = p.dma(q, dq, buf,
                   wuSL[oc, :, :].rearrange("(f p) o -> p f o", p=128))
        p.wdma("dve", hb)
        p.wait("dve", "pool")

        def cua(e, buf=buf, jb=jb):
            e.scalar_tensor_tensor(jb[:, 0:512], buf[:, 0:512], wsc[:, 3:4],
                                   mrow[:, :], OP.mult, OP.add)
            return e.scalar_tensor_tensor(jb[:, 512:1024], buf[:, 512:1024],
                                          wsc[:, 3:4], mrow[:, :], OP.mult, OP.add)
        ca_hist.append(p.step("dve", cua))
        p.wait("pool", "dve")

        def cub(e, jb=jb, tb=tb):
            e.tensor_scalar(oscr[:, 0:512], jb[:, 0:512], MRND + 1.0, MRND - 1.0,
                            OP.min, OP.max)
            e.tensor_scalar(tb[:, 0:512], oscr[:, 0:512], -MRND, None, OP.add)
            e.tensor_scalar(oscr[:, 0:512], jb[:, 512:1024], MRND + 1.0, MRND - 1.0,
                            OP.min, OP.max)
            return e.tensor_scalar(tb[:, 512:1024], oscr[:, 0:512], -MRND, None,
                                   OP.add)
        v = p.step("pool", cub)
        p.wait("sp", "pool", v)
        p.dma("sp", "dsp", wu_t[oc, :, :], tb)

    def emit_scales(chunks, s_dst, inv_dst):
        n = len(chunks)
        p.wait("dve", "pool")

        def mm0(e):
            e.tensor_copy(rmx[:, 0:256], chunks[0])
            return e.tensor_copy(rmn[:, 0:256], chunks[0])
        p.step("dve", mm0)
        cur = 0
        for i in range(1, n):
            nxt = (cur + 1) % 2

            def mm(e, i=i, cur=cur, nxt=nxt):
                e.tensor_tensor(rmx[:, nxt * 256:(nxt + 1) * 256],
                                rmx[:, cur * 256:(cur + 1) * 256],
                                chunks[i], OP.max)
                return e.tensor_tensor(rmn[:, nxt * 256:(nxt + 1) * 256],
                                       rmn[:, cur * 256:(cur + 1) * 256],
                                       chunks[i], OP.min)
            p.step("dve", mm)
            cur = nxt

        def comb(e, cur=cur):
            e.scalar_tensor_tensor(rmx[:, 512:768], rmn[:, cur * 256:(cur + 1) * 256],
                                   -1.0, rmx[:, cur * 256:(cur + 1) * 256],
                                   OP.mult, OP.max)
            e.tensor_copy(fold[0:64, 0:256], rmx[64:128, 512:768])
            e.tensor_tensor(fold[0:64, 256:512], rmx[0:64, 512:768],
                            fold[0:64, 0:256], OP.max)
            e.tensor_copy(rmx[0:32, 768:1024], fold[32:64, 256:512])
            e.tensor_tensor(fold[0:32, 0:256], fold[0:32, 256:512],
                            rmx[0:32, 768:1024], OP.max)
            e.tensor_copy(fold_bf[:, 0:256], fold[0:32, 0:256])
            e.tensor_tensor(fold[0:32, 256:512], fold[0:32, 0:256],
                            fold_bf[:, 0:256], OP.subtract)
            return e.tensor_copy(fold_bf[:, 256:512], fold[0:32, 256:512])
        p.step("dve", comb)
        p.wait("pe", "dve")

        def tp1(e):
            e.transpose(ps_TP[:, 0:32], fold_bf[:, 0:128], ident[0:32, 0:32])
            e.transpose(ps_TP[:, 64:96], fold_bf[:, 256:384], ident[0:32, 0:32])
            e.transpose(ps_TP[:, 32:64], fold_bf[:, 128:256], ident[0:32, 0:32])
            return e.transpose(ps_TP[:, 96:128], fold_bf[:, 384:512],
                               ident[0:32, 0:32])
        p.step("pe", tp1)
        p.wait("dve", "pe")

        def tp1e(e):
            e.tensor_copy(tscr[:, 0:32], ps_TP[:, 0:32])
            e.tensor_copy(tscr[:, 64:96], ps_TP[:, 64:96])
            e.tensor_copy(tscr[:, 32:64], ps_TP[:, 32:64])
            e.tensor_copy(tscr[:, 96:128], ps_TP[:, 96:128])
            spacer_c(e)
            e.tensor_tensor(tscr[:, 128:192], tscr[:, 0:64], tscr[:, 64:128],
                            OP.add)
            spacer_c(e)
            e.reduce_max(tmv[:, 0:1], tscr[:, 128:160], axis=AX)
            e.reduce_max(tmv[:, 1:2], tscr[:, 160:192], axis=AX)
            spacer_c(e)
            e.tensor_copy(tmv_bf[:, 0:2], tmv[:, 0:2])
            spacer_c(e)
            e.scalar_tensor_tensor(tmv[:, 2:4], tmv_bf[:, 0:2], -1.0,
                                   tmv[:, 0:2], OP.mult, OP.add)
            spacer_c(e)
            return e.tensor_copy(tmv_bf[:, 2:4], tmv[:, 2:4])
        p.step("dve", tp1e)
        p.wait("pe", "dve")

        def tpb(e):
            e.transpose(ps_TP[0:1, 0:128], tmv_bf[:, 0:1], ident[:, :])
            e.transpose(ps_TP[0:1, 128:256], tmv_bf[:, 1:2], ident[:, :])
            e.transpose(ps_TP[0:1, 256:384], tmv_bf[:, 2:3], ident[:, :])
            return e.transpose(ps_TP[0:1, 384:512], tmv_bf[:, 3:4], ident[:, :])
        p.step("pe", tpb)
        p.wait("dve", "pe")

        def chain(e):
            S = [lnr2[0:1, i * 256:(i + 1) * 256] for i in range(5)]
            e.tensor_copy(S[0], ps_TP[0:1, 256:512])
            e.tensor_tensor(S[1], ps_TP[0:1, 0:256], S[0], OP.add)
            e.tensor_scalar(S[2], S[1], 1e-5, None, OP.max)
            e.reciprocal(S[3], S[2])
            e.tensor_scalar(S[4], S[3], 127.0, None, OP.mult)
            e.tensor_scalar(S[1], S[2], 1.0 / 127.0, None, OP.mult)
            e.tensor_copy(rowv_bf[0:1, 0:256], S[4])
            e.tensor_copy(rowv_bf[0:1, 512:768], S[1])
            e.scalar_tensor_tensor(S[0], rowv_bf[0:1, 0:256], -1.0, S[4],
                                   OP.mult, OP.add)
            return e.tensor_copy(rowv_bf[0:1, 256:512], S[0])
        p.step("dve", chain)
        p.wait("pe", "dve")

        def bcst(e):
            e.matmul(ps_GU[:, 0:256], ones_row[:, :], rowv_bf[0:1, 0:256],
                     start=True, stop=False)
            e.matmul(ps_GU[:, 0:256], ones_row[:, :], rowv_bf[0:1, 256:512],
                     start=False, stop=True)
            return e.matmul(ps_GU2[:, 0:256], ones_row[:, :], rowv_bf[0:1, 512:768],
                            start=True, stop=True)
        p.step("pe", bcst)
        p.wait("dve", "pe")

        def bcev(e):
            e.tensor_copy(s_dst[:, :], ps_GU[:, 0:256])
            return e.tensor_copy(inv_dst[:, :], ps_GU2[:, 0:256])
        v = p.step("dve", bcev)
        p.wait("pe", "dve", v)

    for u in range(NSUB):
        m, hh = u // 2, u % 2
        p.wait("dve", "pool")
        p.wait("pool", "dve")
        p.wait("pool", "pe")
        p.wait("act", "dve")
        xq_, dq_ = ("sp", "dsp") if u % 2 == 0 else ("gp", "dgp")
        p.wait(xq_, "dve")
        p.wait(xq_, "pool")
        if u == 0:
            p.wdma_all("sp", "dsp")
        h_xm = p.dma(xq_, dq_, xmid_c[:, :].rearrange("p (c t) -> p c t", c=8),
                     xmid_sp[:, :].rearrange("p (mm c t) -> p mm c t", mm=4, c=8)
                     [:, m, :, hh * 256:(hh + 1) * 256])
        p.wdma("dve", h_xm)
        p.wdma("pool", h_xm)

        xch = [xmid_c[:, c * 256:(c + 1) * 256] for c in range(8)]
        emit_scales(xch, bs1, bsi1)
        if u == DUMP_U and "bs1" in dbg:
            p.wait("sp", "dve")
            p.dma("sp", "dsp", dbg["bs1"], bs1[:, :])
            p.dma("sp", "dsp", dbg["tmv"], tmv[:, :])
            p.dma("sp", "dsp", dbg["habs"], rmx[:, 512:768])
            p.wait("gp", "dve")
            p.dma("gp", "dgp", dbg["bsi1"], bsi1[:, :])
            if "rowv" in dbg:
                p.dma("gp", "dgp", dbg["rowv"], rowv_bf[:, :])
                p.dma("gp", "dgp", dbg["tmvbf"], tmv_bf[:, :])

        for c in range(8):
            def xq1(e, c=c):
                e.tensor_tensor(tscr[:, 0:256], xch[c], bs1[:, :], OP.mult)
                e.tensor_scalar(tscr[:, 256:512], tscr[:, 0:256], MRND, MRND + 127.0,
                                OP.add, OP.min)
                e.tensor_scalar(xq_tmp[:, :], tscr[:, 256:512], MRND - 128.0, -MRND,
                                OP.max, OP.add)
                return e.tensor_tensor(xq_deq[:, c * 256:(c + 1) * 256], xq_tmp[:, :],
                                       bsi1[:, :], OP.mult)
            p.step("dve", xq1)

        p.wait("pe", "dve")
        if u == 0:
            p.wait("pe", "pool")
            p.wdma_all("gp", "dsp")
        gu_hist = []
        for oc in range(32):
            slab = wu_buf[:, (oc % 4) * 1024:(oc % 4) * 1024 + 1024]
            q, dq = ("sp", "dsp") if oc % 2 == 0 else ("gp", "dgp")
            if oc >= 4:
                p.wait(q, "pe", gu_hist[oc - 4])
            else:
                p.wait(q, "pe")
            h_sl = p.dma(q, dq, slab, wu_t[oc, :, :])
            pst = ps_GU if oc % 2 == 0 else ps_GU2
            p.wdma("pe", h_sl)
            if oc >= 2:
                p.wait("pe", "dve")

            col = (oc % 2) * 256

            def gu(e, oc=oc, col=col, slab=slab):
                ins = None
                for f in range(8):
                    e.matmul(ps_GU[:, col:col + 256],
                             wg_sb[:, f * INNER + oc * 128: f * INNER + (oc + 1) * 128],
                             xq_deq[:, f * 256:(f + 1) * 256],
                             start=(f == 0), stop=(f == 7))
                    ins = e.matmul(ps_GU2[:, col:col + 256], slab[:, f * 128:(f + 1) * 128],
                                   xq_deq[:, f * 256:(f + 1) * 256],
                                   start=(f == 0), stop=(f == 7))
                return ins
            gu_hist.append(p.step("pe", gu))
            p.wait("act", "pe")
            p.wait("act", "dve")

            def sil(e, col=col):
                return e.activation(gsil[:, :], ps_GU[:, col:col + 256], AF.Silu,
                                    scale=wsc[:, 4:5])
            sv = p.step("act", sil)
            p.wait("dve", "act", sv)

            def hm(e, oc=oc, col=col):
                return e.scalar_tensor_tensor(hbuf[:, oc * 256:(oc + 1) * 256],
                                              ps_GU2[:, col:col + 256], wsc[:, 5:6],
                                              gsil[:, :], OP.mult, OP.mult)
            p.step("dve", hm)

        hch = [hbuf[:, c * 256:(c + 1) * 256] for c in range(32)]
        emit_scales(hch, bs2, bsd)

        p.wait("pool", "dve")
        for c in range(32):
            eng = "dve" if c % 2 == 0 else "pool"
            tb = tscr if eng == "dve" else oscr

            def hqf(e, c=c, tb=tb):
                e.tensor_tensor(tb[:, 0:256], hch[c], bs2[:, :], OP.mult)
                e.tensor_scalar(tb[:, 256:512], tb[:, 0:256], MRND, MRND + 127.0,
                                OP.add, OP.min)
                return e.tensor_scalar(hq[:, c * 256:(c + 1) * 256], tb[:, 256:512],
                                       MRND - 128.0, -MRND, OP.max, OP.add)
            p.step(eng, hqf)
        p.wait("pe", "dve")
        p.wait("pe", "pool")

        dn_hist = []
        for mi in range(32):
            slab = wd_buf[:, (mi % 4) * 1024:(mi % 4) * 1024 + 1024]
            q, dq = ("sp", "dsp") if mi % 2 == 0 else ("gp", "dgp")
            if mi >= 4:
                p.wait(q, "pe", dn_hist[mi - 4])
            else:
                p.wait(q, "pe")
            h_sl = p.dma(q, dq, slab, wd_t[mi * 128:(mi + 1) * 128, :])
            p.wdma("pe", h_sl)

            def dn(e, mi=mi, slab=slab):
                if mi == 0:
                    for b in range(4):
                        e.matmul(ps_DN[:, b * 512:(b + 1) * 512], ones_row[:, :],
                                 zrow_bf[0:1, 0:512], start=True, stop=True)
                ins = None
                for oc in range(8):
                    ins = e.matmul(ps_DN[:, oc * 256:(oc + 1) * 256],
                                   slab[:, oc * 128:(oc + 1) * 128],
                                   hq[:, mi * 256:(mi + 1) * 256],
                                   start=False, stop=(mi == 31))
                return ins
            dn_hist.append(p.step("pe", dn))

        p.wait("dve", "pe")
        for c in range(8):
            def r2f(e, c=c):
                e.scalar_tensor_tensor(tscr[:, 0:256], ps_DN[:, c * 256:(c + 1) * 256],
                                       wsc[:, 1:2], bsd[:, :], OP.mult, OP.mult)
                return e.tensor_tensor(ytile[:, c * 256:(c + 1) * 256],
                                       tscr[:, 0:256], xch[c], OP.add)
            p.step("dve", r2f)

        if u == DUMP_U:
            if "xq0" in dbg:
                p.wait("gp", "dve")
                p.dma("gp", "dgp", dbg["xq0"][:, :], xq_deq[:, :])
            if "h0" in dbg:
                p.wait("sp", "dve")
                p.dma("sp", "dsp", dbg["h0"][:, :], hbuf[:, :])
            if "hq0" in dbg:
                p.wait("gp", "dve")
                p.wait("gp", "pool")
                hqh = p.dma("gp", "dgp", dbg["hq0"][:, :], hq[:, :])
                p.wdma("dve", hqh)
            if "r20" in dbg:
                p.wait("sp", "dve")
                p.dma("sp", "dsp", dbg["r20"][:, :], ytile[:, :])
        for c in range(8):
            cl = slice(c * 256, (c + 1) * 256)

            def cp3(e, cl=cl):
                return e.tensor_copy(cln_bf[:, cl], ytile[:, cl])
            p.step("dve", cp3)
            p.wait("act", "dve")

            def sq3(e, cl=cl):
                return e.activation(sq3_bf[:, cl], ytile[:, cl], AF.Square)
            p.step("act", sq3)
        p.wait("pe", "dve")
        p.wait("pe", "act")

        def st3(e):
            ins = None
            for c in range(8):
                cl = slice(c * 256, (c + 1) * 256)
                e.matmul(ps_DN[0:1, 0:256], ones_col[:, :], cln_bf[:, cl],
                         start=(c == 0), stop=(c == 7))
                ins = e.matmul(ps_DN[0:1, 1024:1280], ones_col[:, :], sq3_bf[:, cl],
                               start=(c == 0), stop=(c == 7))
            return ins
        p.step("pe", st3)
        p.wait("dve", "pe")

        def chain3(e):
            L = lnr2
            mu, var, yy, tmp, mrs = (L[0:1, i * 256:(i + 1) * 256]
                                     for i in range(5))
            t2 = tmp
            e.tensor_scalar(mu, ps_DN[0:1, 0:256], 1.0 / E, None, OP.mult)
            e.tensor_scalar(tmp, ps_DN[0:1, 1024:1280], 1.0 / E, EPS, OP.mult, OP.add)
            e.tensor_tensor(var, mu, mu, OP.mult)
            e.tensor_tensor(var, tmp, var, OP.subtract)
            e.tensor_copy(lnr2_bf[0:1, 0:256], var)
            e.tensor_copy(tmp, lnr2_bf[0:1, 0:256].bitcast(U16))
            e.tensor_scalar(tmp, tmp, 0.5, MRND, OP.mult, OP.add)
            e.tensor_scalar(tmp, tmp, -MRND, None, OP.add)
            e.tensor_scalar(tmp, tmp, -1.0, 24375.0, OP.mult, OP.add)
            e.tensor_copy(lnr2_bf[0:1, 0:256].bitcast(U16), tmp)
            e.tensor_copy(yy, lnr2_bf[0:1, 0:256])
            for _ in range(3):
                e.tensor_tensor(tmp, yy, yy, OP.mult)
                e.tensor_tensor(tmp, tmp, var, OP.mult)
                e.tensor_scalar(tmp, tmp, -0.5, 1.5, OP.mult, OP.add)
                e.tensor_tensor(yy, yy, tmp, OP.mult)
            e.tensor_tensor(mrs, mu, yy, OP.mult)
            e.tensor_copy(lnr2_bf[0:1, 0:256], yy)
            e.tensor_tensor(t2, yy, lnr2_bf[0:1, 0:256], OP.subtract)
            e.tensor_copy(lnr2_bf[0:1, 256:512], t2)
            e.tensor_copy(lnr2_bf[0:1, 512:768], mrs)
            e.tensor_tensor(t2, mrs, lnr2_bf[0:1, 512:768], OP.subtract)
            return e.tensor_copy(lnr2_bf[0:1, 768:1024], t2)
        p.step("dve", chain3)
        p.wait("pe", "dve")

        def bc3(e):
            e.matmul(ps_DN[:, 256:512], ones_row[:, :], lnr2_bf[0:1, 0:256],
                     start=True, stop=False)
            e.matmul(ps_DN[:, 256:512], ones_row[:, :], lnr2_bf[0:1, 256:512],
                     start=False, stop=True)
            e.matmul(ps_DN[:, 1280:1536], ones_row[:, :], lnr2_bf[0:1, 512:768],
                     start=True, stop=False)
            return e.matmul(ps_DN[:, 1280:1536], ones_row[:, :], lnr2_bf[0:1, 768:1024],
                            start=False, stop=True)
        p.step("pe", bc3)
        p.wait("dve", "pe")

        def bev3(e):
            e.tensor_copy(br2[:, :], ps_DN[:, 256:512])
            return e.tensor_copy(bm2[:, :], ps_DN[:, 1280:1536])
        p.step("dve", bev3)
        p.wait("pool", "dve")

        for c in range(8):
            cl = slice(c * 256, (c + 1) * 256)
            eng = "dve" if c % 2 == 0 else "pool"
            tb = tscr if eng == "dve" else oscr

            def ap3(e, cl=cl, tb=tb):
                e.tensor_tensor(tb[:, 0:256], ytile[:, cl], br2[:, :], OP.mult)
                return e.tensor_tensor(tb[:, 256:512], tb[:, 0:256], bm2[:, :],
                                       OP.subtract)
            v = p.step(eng, ap3)
            p.wait("sp", eng, v)
            h_y = p.dma("sp", "dsp",
                        y[c * 128:(c + 1) * 128, u * 256:(u + 1) * 256], tb[:, 256:512])
            if c < 6:
                p.wdma(eng, h_y)

    p.wdma_all("sp", "dsp")
    p.wdma_all("gp", "dgp")

    _emit_all(nc, p)
    cps.close()
    cs.close()
    octx.close()
    return nc


def _emit_all(nc, p):
    with nc.Block() as block:
        @block.tensor
        def _(e):
            for _, fn in p.ops["pe"]:
                fn(e)

        @block.scalar
        def _(e):
            e.preload_act_table(AF.Silu)
            for _, fn in p.ops["act"]:
                fn(e)

        @block.vector
        def _(e):
            for _, fn in p.ops["dve"]:
                fn(e)

        @block.gpsimd
        def _(e):
            merged = sorted(p.ops["gp"] + p.ops["pool"], key=lambda t: t[0])
            for _, fn in merged:
                fn(e)

        @block.sync
        def _(e):
            for _, fn in p.ops["sp"]:
                fn(e)


def _host_inputs(full):
    x = np.asarray(full["L0_attn_x"], dtype=np.float32)
    BASE = 10000.0
    invf = BASE ** (-(np.arange(32, dtype=np.float64) * 2.0) / 64.0)
    freqs = invf[np.arange(128) % 32]
    sgn = np.where((np.arange(128) // 32) % 2 == 0, -1.0, 1.0)
    wu = np.asarray(full["L0_ffn_w_up"], np.float32)
    shared = {
        "Wq": np.ascontiguousarray(full["L0_attn_Wq"], dtype=np.float32),
        "Wk": np.ascontiguousarray(full["L0_attn_Wk"], dtype=np.float32),
        "Wv": np.ascontiguousarray(full["L0_attn_Wv"], dtype=np.float32),
        "Wo": np.ascontiguousarray(full["L0_attn_Wo"], dtype=np.float32),
        "Wg1": np.ascontiguousarray(full["L0_attn_Wg1"], dtype=np.float32),
        "Wg2": np.ascontiguousarray(full["L0_attn_Wg2"], dtype=np.float32),
        "wgT": np.ascontiguousarray(
            np.asarray(full["L0_ffn_w_gate"], np.float32).T
            .reshape(8, 128, INNER).transpose(0, 2, 1)
            .reshape(32, 1024, 128).transpose(0, 2, 1).reshape(E, INNER)),
        "wuSL": np.ascontiguousarray(
            wu.T.reshape(E, 32, 128).transpose(1, 0, 2)),
        "wdT": np.ascontiguousarray(np.asarray(full["L0_ffn_w_down"], np.float32).T),
        "gam": np.ascontiguousarray(
            np.asarray(full["L0_attn_gamma"], np.float32).reshape(8, 128).T),
        "bet": np.ascontiguousarray(
            np.asarray(full["L0_attn_beta"], np.float32).reshape(8, 128).T),
    }
    in_maps = []
    for c in range(8):
        b, j = c // 4, c % 4
        pos = np.arange(j * TOK, (j + 1) * TOK, dtype=np.float64)
        ang = freqs[:, None] * pos[None, :]
        im = dict(shared)
        im["xT"] = np.ascontiguousarray(x[b, j * TOK:(j + 1) * TOK, :].T)
        im["cosd"] = np.cos(ang).astype(np.float32)
        im["ssin"] = (sgn[:, None] * np.sin(ang)).astype(np.float32)
        sel = np.zeros((64, 8), np.float32)
        for r in range(8):
            if r // 4 == b and r % 4 < j:
                sel[:, r] = 1.0
        im["selc"] = sel
        in_maps.append(im)
    return in_maps


def kernel(**inputs):
    nc = build_program(DUMPS)
    in_maps = _host_inputs(inputs)
    res = run_bass_kernel_spmd(nc, in_maps, core_ids=list(range(8)))
    out = np.empty((2, 8192, 1024), np.float32)
    for c in range(8):
        b, j = c // 4, c % 4
        out[b, j * TOK:(j + 1) * TOK, :] = res.results[c]["y"].T
    return out


# version 15 (final)
# speedup vs baseline: 1.0397x; 1.0069x over previous
import numpy as np
import concourse.bass as bass
import concourse.mybir as mybir
from concourse.bass_utils import run_bass_kernel_spmd

F32 = mybir.dt.float32
BF16 = mybir.dt.bfloat16
I32 = mybir.dt.int32
U16 = mybir.dt.uint16
AF = mybir.ActivationFunctionType
OP = mybir.AluOpType
AX = mybir.AxisListType.X

MRND = 1.5 * 2.0 ** 23
E = 1024
HCNT = 16
D = 64
TOK = 2048
NMAC = 4
MT = 512
NSUB = 8
ST = 256
INNER = 4096
EPS = 1e-5

DUMPS = ()
STAGE = 3
BSTOP = 8
DUMP_U = 0
ASTOP = 5


class P:
    ENG = ("pe", "act", "dve", "pool", "sp", "gp")

    def __init__(self, nc):
        self.nc = nc
        self.ops = {e: [] for e in self.ENG}
        self.cnt = {}
        self.sems = {}
        self.seq = 0

    def _app(self, eng, fn):
        self.ops[eng].append((self.seq, fn))
        self.seq += 1

    def sem(self, name):
        h = self.nc.alloc_semaphore(name=name)
        self.sems[name] = h
        self.cnt[name] = 0

    def wait(self, eng, sname, val=None):
        v = self.cnt[sname] if val is None else val
        if v <= 0:
            return
        h = self.sems[sname]
        self._app(eng, lambda e: e.wait_ge(h, v))

    def step(self, eng, build_fn, sname=None):
        sname = sname or eng
        self.cnt[sname] += 1
        h = self.sems[sname]

        def fn(e):
            build_fn(e).then_inc(h, 1)
        self._app(eng, fn)
        return self.cnt[sname]

    def plain(self, eng, build_fn):
        self._app(eng, build_fn)

    NPOOL = 16

    def dma(self, eng, q, out_ap, in_ap):
        i = self.rot[q]
        self.rot[q] = (i + 1) % self.NPOOL
        s = f"{q}{i}"
        h = self.sems[s]
        prev = self.cnt[s]
        if prev > 0:
            self._app(eng, lambda e: e.wait_ge(h, prev))
        self.cnt[s] = prev + 16
        tgt = self.cnt[s]

        def fn(e):
            e.dma_start(out=out_ap, in_=in_ap).then_inc(h, 16)
        self._app(eng, fn)
        return (s, tgt)

    def wdma(self, eng, *handles):
        for s, tgt in handles:
            self.wait(eng, s, tgt)

    def wdma_all(self, eng, q):
        for i in range(self.NPOOL):
            self.wait(eng, f"{q}{i}")


def build_program(dumps=()):
    nc = bass.Bass("TRN2", target_bir_lowering=False)
    p = P(nc)

    xT = nc.dram_tensor("xT", [E, TOK], F32, kind="ExternalInput").ap()
    Wq = nc.dram_tensor("Wq", [E, E], F32, kind="ExternalInput").ap()
    Wk = nc.dram_tensor("Wk", [E, E], F32, kind="ExternalInput").ap()
    Wv = nc.dram_tensor("Wv", [E, E], F32, kind="ExternalInput").ap()
    Wo = nc.dram_tensor("Wo", [E, E], F32, kind="ExternalInput").ap()
    Wg1 = nc.dram_tensor("Wg1", [E, D], F32, kind="ExternalInput").ap()
    Wg2 = nc.dram_tensor("Wg2", [D, E], F32, kind="ExternalInput").ap()
    wgT = nc.dram_tensor("wgT", [E, INNER], F32, kind="ExternalInput").ap()
    wuSL = nc.dram_tensor("wuSL", [32, E, 128], F32, kind="ExternalInput").ap()
    wdT = nc.dram_tensor("wdT", [INNER, E], F32, kind="ExternalInput").ap()
    gam = nc.dram_tensor("gam", [128, 8], F32, kind="ExternalInput").ap()
    bet = nc.dram_tensor("bet", [128, 8], F32, kind="ExternalInput").ap()
    cosd = nc.dram_tensor("cosd", [128, TOK], F32, kind="ExternalInput").ap()
    ssin = nc.dram_tensor("ssin", [128, TOK], F32, kind="ExternalInput").ap()
    selc = nc.dram_tensor("selc", [64, 8], F32, kind="ExternalInput").ap()
    y = nc.dram_tensor("y", [E, TOK], F32, kind="ExternalOutput").ap()

    kfm_sp = nc.dram_tensor("kfm_sp", [128, 16384], BF16).ap()
    ktm_sp = nc.dram_tensor("ktm_sp", [128, 16384], BF16).ap()
    v_sp = nc.dram_tensor("v_sp", [128, 16384], BF16).ap()
    xmid_sp = nc.dram_tensor("xmid_sp", [128, 16384], F32).ap()
    wd_t = nc.dram_tensor("wd_t", [INNER, E], BF16).ap()
    wu_t = nc.dram_tensor("wu_t", [32, 128, E], BF16).ap()
    cc_in = nc.dram_tensor("cc_in_t", [64, 1024], F32).ap()
    cc_out = nc.dram_tensor("cc_out_t", [512, 1024], F32,
                            addr_space="Shared").ap()

    dbg = {}
    for name, shape in dumps:
        dbg[name] = nc.dram_tensor("dbg_" + name, list(shape), F32,
                                   kind="ExternalOutput").ap()

    for s in ("cc", "pe", "act", "dve", "pool", "gpc"):
        p.sem(s)
    p.rot = {"dsp": 0, "dgp": 0}
    for q in ("dsp", "dgp"):
        for i in range(P.NPOOL):
            p.sem(f"{q}{i}")

    from contextlib import ExitStack
    octx = ExitStack()

    def osb(name, shape, dt=F32):
        return octx.enter_context(nc.sbuf_tensor(name, shape, dt))

    ident = osb("ident", [128, 128], BF16)
    mask4 = osb("mask4", [128, 512], F32)
    ones_col = osb("ones_col", [128, 1], BF16)
    ones_row = osb("ones_row", [1, 128], BF16)
    gam_sb = osb("gam_sb", [128, 8])
    bet_sb = osb("bet_sb", [128, 8])
    mcol = osb("mcol", [128, 1])
    mrow = osb("mrow", [128, 512], BF16)
    wsc = osb("wsc", [128, 8])
    wacc = osb("wacc", [128, 192])
    wab_bf = osb("wab_bf", [128, 128], BF16)
    tiny = osb("tiny", [1, 16])
    tiny_bf = osb("tiny_bf", [1, 16], BF16)
    selc_sb = osb("selc_sb", [64, 8])
    zrow_bf = osb("zrow_bf", [1, 512], BF16)
    cc_bf = osb("cc_bf", [64, 1024], BF16)
    seldg = osb("seldg", [64, 512], BF16)

    def setup(g):
        g.memset(ident[:, :], 0.0)
        g.affine_select(out=ident[:, :], in_=ident[:, :], compare_op=OP.not_equal,
                        fill=1.0, base=0, pattern=[[-1, 128]], channel_multiplier=1)
        g.memset(mask4[:, :], 1.0)
        g.affine_select(out=mask4[:, :].rearrange("p (r n) -> p r n", r=4),
                        in_=mask4[:, :].rearrange("p (r n) -> p r n", r=4),
                        compare_op=OP.is_ge, fill=0.0, base=0,
                        pattern=[[0, 4], [1, 128]], channel_multiplier=-1)
        g.memset(ones_col[:, :], 1.0)
        g.memset(ones_row[:, :], 1.0)
        g.memset(wacc[:, :], 0.0)
        g.memset(zrow_bf[:, :], 0.0)
        g.memset(mrow[:, :], MRND)
        g.memset(mcol[:, :], MRND).then_inc(p.sems["gpc"], 1)
    p.plain("gp", setup)
    p.cnt["gpc"] += 1
    h_cst = [p.dma("gp", "dgp", gam_sb[:, :], gam),
             p.dma("gp", "dgp", bet_sb[:, :], bet),
             p.dma("gp", "dgp", selc_sb[:, :], selc)]
    p.wait("dve", "gpc", 1)
    p.wait("pool", "gpc", 1)
    p.wait("act", "gpc", 1)
    p.wdma("dve", *h_cst)
    p.wdma("pe", *h_cst)

    ab = ExitStack()

    def absb(name, shape, dt=F32):
        return ab.enter_context(nc.sbuf_tensor(name, shape, dt))

    cos_m = absb("cos_m", [128, 512])
    sin_m = absb("sin_m", [128, 512])
    W1 = absb("W1", [128, 8192], BF16)
    W2 = absb("W2", [128, 8192], BF16)
    Wg1_sb = absb("Wg1_sb", [128, 512], BF16)
    Wg2_sb = absb("Wg2_sb", [64, 1024], BF16)
    xt_bf = absb("xt_bf", [128, 4096], BF16)
    FA = absb("FA", [128, 4096])
    FB = absb("FB", [128, 4096])
    FC = absb("FC", [128, 4096])
    krot = absb("krot", [128, 4096], BF16)
    ktm = absb("ktm", [128, 4096], BF16)
    vt = absb("vt", [128, 4096], BF16)
    kfmB = absb("kfmB", [128, 4096], BF16)
    q_oddM = absb("q_oddM", [64, 4096], BF16)
    k_oddM = absb("k_oddM", [64, 4096], BF16)
    eT_sb = absb("eT_sb", [128, 2048], BF16)
    S_bf = absb("S_bf", [64, 1024], BF16)
    stat_bf = absb("stat_bf", [128, 2048], BF16)
    xn_bf = absb("xn_bf", [128, 4096], BF16)
    swd = absb("swd", [128, 512])
    t1d = absb("t1d", [128, 512])
    t2d = absb("t2d", [128, 512])
    swp = absb("swp", [128, 512])
    t1p = absb("t1p", [128, 512])
    t2p = absb("t2p", [128, 512])
    xg1_bf = absb("xg1_bf", [64, 512], BF16)
    lnr = absb("lnr", [1, 2048])
    cc_sb = FB[0:64, 0:1024]
    sin_f = FB[0:64, 1024:2048]
    lnr_bf = absb("lnr_bf", [1, 2048], BF16)
    brstd = absb("brstd", [128, 512])
    bmrs = absb("bmrs", [128, 512])
    wmb = absb("wmb", [128, 1024])
    jA = absb("jA", [128, 512])

    wtb = absb("wtb", [128, 512], BF16)
    wsb_bf = absb("wsb_bf", [128, 2], BF16)
    oscr_b = absb("oscr_b", [128, 512])

    abps = ExitStack()
    ps_S = abps.enter_context(nc.psum_tensor("ps_S", [64, 1024], F32))
    ps_W1 = abps.enter_context(nc.psum_tensor("ps_W1", [128, 512], F32))
    ps_W2 = abps.enter_context(nc.psum_tensor("ps_W2", [128, 512], F32))

    aps = ExitStack()
    ps_V = aps.enter_context(nc.psum_tensor("ps_V", [128, 1024], F32))
    ps_T = aps.enter_context(nc.psum_tensor("ps_T", [128, 128], BF16))

    h_w = [p.dma("gp", "dgp", W1[:, :].rearrange("p (f o) -> p f o", f=8),
                 Wk.rearrange("(f p) o -> p f o", p=128)),
           p.dma("gp", "dgp", W2[:, :].rearrange("p (f o) -> p f o", f=8),
                 Wv.rearrange("(f p) o -> p f o", p=128)),
           p.dma("gp", "dgp", Wg1_sb[:, :].rearrange("p (f o) -> p f o", f=8),
                 Wg1.rearrange("(f p) o -> p f o", p=128)),
           p.dma("gp", "dgp", Wg2_sb[:, :], Wg2)]

    def emit_rotary(eng, cols, src, dst_bf, sw, t1, t2):
        def fn(e):
            e.tensor_copy(sw[0:32, :], src[32:64, cols])
            e.tensor_copy(sw[32:64, :], src[0:32, cols])
            e.tensor_copy(sw[64:96, :], src[96:128, cols])
            e.tensor_copy(sw[96:128, :], src[64:96, cols])
            e.tensor_tensor(t1[:, :], src[:, cols], cos_m[:, :], OP.mult)
            e.tensor_tensor(t2[:, :], sw[:, :], sin_m[:, :], OP.mult)
            return e.tensor_tensor(dst_bf[:, cols], t1[:, :], t2[:, :], OP.add)
        p.step(eng, fn)

    def spacer(e):
        return e.tensor_scalar(jA[:, 256:512], jA[:, 0:256], 1.0, None, OP.mult)

    wj_hist = []

    def emit_wmean_job(src_ap, jcol):
        j = emit_wmean_job.i
        emit_wmean_job.i += 1
        buf = wmb[:, (j % 2) * 512:(j % 2) * 512 + 512]
        q, dq = ("sp", "dsp") if j % 2 == 0 else ("gp", "dgp")
        if j >= 2:
            p.wait(q, wj_hist[j - 2][0], wj_hist[j - 2][1])
        else:
            p.wait(q, "act")
            p.wait(q, "dve")
        hb = p.dma(q, dq, buf, src_ap)
        p.wdma("act", hb)
        p.wait("act", "pool")

        def f1(e, buf=buf, jcol=jcol):
            return e.activation(jA[:, 0:512], buf, AF.Abs,
                                accum_out=wacc[:, jcol:jcol + 1])
        wj_hist.append(("act", p.step("act", f1)))
    emit_wmean_job.i = 0

    def emit_wscale(jc0, n_elems, scol, icol):
        p.wait("dve", "act")

        def sp1(e, jc0=jc0):
            a, b = t1d, t2d
            e.tensor_tensor(a[:, 0:32], wacc[:, jc0:jc0 + 32],
                            wacc[:, jc0 + 32:jc0 + 64], OP.add)
            spacer(e)
            e.tensor_tensor(a[:, 32:48], a[:, 0:16], a[:, 16:32], OP.add)
            spacer(e)
            e.tensor_tensor(a[:, 48:56], a[:, 32:40], a[:, 40:48], OP.add)
            spacer(e)
            e.tensor_tensor(a[:, 56:60], a[:, 48:52], a[:, 52:56], OP.add)
            spacer(e)
            e.tensor_tensor(a[:, 60:62], a[:, 56:58], a[:, 58:60], OP.add)
            spacer(e)
            e.tensor_tensor(a[:, 62:63], a[:, 60:61], a[:, 61:62], OP.add)
            spacer(e)
            e.tensor_copy(b[0:64, 0:1], a[64:128, 62:63])
            spacer(e)
            e.tensor_tensor(b[0:64, 1:2], a[0:64, 62:63], b[0:64, 0:1], OP.add)
            spacer(e)
            e.tensor_copy(b[0:32, 2:3], b[32:64, 1:2])
            spacer(e)
            e.tensor_tensor(b[0:32, 3:4], b[0:32, 1:2], b[0:32, 2:3], OP.add)
            spacer(e)
            e.tensor_scalar(b[0:32, 4:5], b[0:32, 3:4], MRND, None, OP.add)
            spacer(e)
            e.tensor_scalar(b[0:32, 10:11], b[0:32, 4:5], -MRND, None, OP.add)
            spacer(e)
            e.tensor_tensor(b[0:32, 5:6], b[0:32, 3:4], b[0:32, 10:11], OP.subtract)
            spacer(e)
            e.tensor_scalar(b[0:32, 6:7], b[0:32, 10:11], 1.0 / 256.0, MRND,
                            OP.mult, OP.add)
            spacer(e)
            e.tensor_scalar(b[0:32, 7:8], b[0:32, 6:7], -MRND, 256.0,
                            OP.add, OP.mult)
            spacer(e)
            e.tensor_tensor(b[0:32, 8:9], b[0:32, 10:11], b[0:32, 7:8], OP.subtract)
            spacer(e)
            e.tensor_copy(wab_bf[0:32, 0:1], b[0:32, 7:8])
            e.tensor_copy(wab_bf[0:32, 1:2], b[0:32, 8:9])
            e.tensor_copy(wab_bf[0:32, 2:3], b[0:32, 5:6])
            spacer(e)
            e.scalar_tensor_tensor(b[0:32, 9:10], wab_bf[0:32, 2:3], -1.0,
                                   b[0:32, 5:6], OP.mult, OP.add)
            spacer(e)
            return e.tensor_copy(wab_bf[0:32, 3:4], b[0:32, 9:10])
        p.step("dve", sp1)
        p.wait("pe", "dve")

        def sm(e):
            return e.matmul(ps_W1[0:1, 0:4], ones_col[0:32, 0:1],
                            wab_bf[0:32, 0:4], start=True, stop=True)
        p.step("pe", sm)
        p.wait("act", "pe")

        def sacc_f(e):
            return e.activation(jA[0:1, 0:4], ps_W1[0:1, 0:4], AF.Identity,
                                accum_out=tiny[0:1, 0:1])
        p.step("act", sacc_f)
        p.wait("dve", "act")

        def sc(e, n_elems=n_elems):
            e.tensor_scalar(tiny[0:1, 2:3], tiny[0:1, 0:1], 1.0 / n_elems, 1e-5,
                            OP.mult, OP.max)
            spacer(e)
            e.reciprocal(tiny[0:1, 3:4], tiny[0:1, 2:3])
            spacer(e)
            e.tensor_tensor(tiny[0:1, 5:6], tiny[0:1, 2:3], tiny[0:1, 3:4], OP.mult)
            spacer(e)
            e.tensor_scalar(tiny[0:1, 6:7], tiny[0:1, 5:6], -1.0, 2.0,
                            OP.mult, OP.add)
            spacer(e)
            e.tensor_tensor(tiny[0:1, 7:8], tiny[0:1, 3:4], tiny[0:1, 6:7], OP.mult)
            spacer(e)
            e.tensor_copy(tiny_bf[0:1, 0:1], tiny[0:1, 7:8])
            spacer(e)
            e.scalar_tensor_tensor(tiny[0:1, 4:5], tiny_bf[0:1, 0:1], -1.0,
                                   tiny[0:1, 7:8], OP.mult, OP.add)
            spacer(e)
            e.tensor_copy(tiny_bf[0:1, 1:2], tiny[0:1, 4:5])
            e.tensor_copy(tiny_bf[0:1, 2:3], tiny[0:1, 2:3])
            spacer(e)
            e.scalar_tensor_tensor(tiny[0:1, 9:10], tiny_bf[0:1, 1:2], -1.0,
                                   tiny[0:1, 4:5], OP.mult, OP.add)
            e.scalar_tensor_tensor(tiny[0:1, 8:9], tiny_bf[0:1, 2:3], -1.0,
                                   tiny[0:1, 2:3], OP.mult, OP.add)
            spacer(e)
            e.tensor_copy(tiny_bf[0:1, 4:5], tiny[0:1, 9:10])
            e.tensor_copy(tiny_bf[0:1, 3:4], tiny[0:1, 8:9])
            spacer(e)
            e.scalar_tensor_tensor(tiny[0:1, 10:11], tiny_bf[0:1, 3:4], -1.0,
                                   tiny[0:1, 8:9], OP.mult, OP.add)
            spacer(e)
            return e.tensor_copy(tiny_bf[0:1, 5:6], tiny[0:1, 10:11])
        p.step("dve", sc)
        p.wait("pe", "dve")

        def bc(e):
            e.matmul(ps_W2[:, 0:1], ones_row[:, :], tiny_bf[0:1, 0:1], start=True, stop=False)
            e.matmul(ps_W2[:, 0:1], ones_row[:, :], tiny_bf[0:1, 1:2], start=False, stop=False)
            e.matmul(ps_W2[:, 0:1], ones_row[:, :], tiny_bf[0:1, 4:5], start=False, stop=True)
            e.matmul(ps_W2[:, 1:2], ones_row[:, :], tiny_bf[0:1, 2:3],
                     start=True, stop=False)
            e.matmul(ps_W2[:, 1:2], ones_row[:, :], tiny_bf[0:1, 3:4],
                     start=False, stop=False)
            return e.matmul(ps_W2[:, 1:2], ones_row[:, :], tiny_bf[0:1, 5:6],
                            start=False, stop=True)
        p.step("pe", bc)
        p.wait("dve", "pe")

        def cv(e, scol=scol, icol=icol):
            e.tensor_copy(wsc[:, scol:scol + 1], ps_W2[:, 0:1])
            return e.tensor_copy(wsc[:, icol:icol + 1], ps_W2[:, 1:2])
        p.step("dve", cv)

    wd_flatI = wdT.rearrange("a b -> (a b)").rearrange("(c p t) -> c p t",
                                                        p=128, t=512)
    wd_flatO = wd_t.rearrange("a b -> (a b)").rearrange("(c p t) -> c p t",
                                                        p=128, t=512)

    def emit_wtern_job(j):
        ri = emit_wmean_job.i
        emit_wmean_job.i += 1
        buf = wmb[:, (ri % 2) * 512:(ri % 2) * 512 + 512]
        q, dq = ("sp", "dsp") if ri % 2 == 0 else ("gp", "dgp")
        if ri >= 2:
            p.wait(q, wj_hist[ri - 2][0], wj_hist[ri - 2][1])
        else:
            p.wait(q, "act")
            p.wait(q, "dve")
            p.wait(q, "pool")
        hb = p.dma(q, dq, buf, wd_flatI[j, :, :])
        p.wdma("dve", hb)
        p.wait("dve", "pool")

        def f1(e, buf=buf):
            return e.scalar_tensor_tensor(swd[:, :], buf, wsc[:, 0:1],
                                          mrow[:, :], OP.mult, OP.add)
        wj_hist.append(("dve", p.step("dve", f1)))
        p.wait("pool", "dve")

        def f2(e):
            e.tensor_scalar(oscr_b[:, :], swd[:, :], MRND + 1.0, MRND - 1.0,
                            OP.min, OP.max)
            return e.tensor_scalar(wtb[:, :], oscr_b[:, :], -MRND, None, OP.add)
        p.step("pool", f2)
        p.wait("sp", "pool")
        p.dma("sp", "dsp", wd_flatO[j, :, :], wtb[:, :])

    def emit_ln(src, gamma, out_bf=None, out_f32=None):
        for c in range(8):
            cl = slice(c * 512, (c + 1) * 512)
            r = (c % 2) * 512

            def cp(e, cl=cl, r=r):
                return e.tensor_copy(stat_bf[:, r:r + 512], src[:, cl])
            p.step("dve", cp)
            p.wait("act", "dve")

            def sq(e, cl=cl, r=r):
                return e.activation(stat_bf[:, 1024 + r:1024 + r + 512], src[:, cl],
                                    AF.Square)
            p.step("act", sq)
            p.wait("pe", "dve")
            p.wait("pe", "act")

            def st(e, c=c, r=r):
                e.matmul(ps_W1[0:1, :], ones_col[:, :], stat_bf[:, r:r + 512],
                         start=(c == 0), stop=(c == 7))
                return e.matmul(ps_W2[0:1, :], ones_col[:, :],
                                stat_bf[:, 1024 + r:1024 + r + 512],
                                start=(c == 0), stop=(c == 7))
            v = p.step("pe", st)
            if c < 7:
                p.wait("dve", "pe", v)
                p.wait("act", "pe", v)
        p.wait("dve", "pe")

        def chain(e):
            L = lnr
            mu, var, yy, tmp = (L[0:1, i * 512:(i + 1) * 512] for i in range(4))
            mrs = tmp
            t2 = var
            e.tensor_scalar(mu, ps_W1[0:1, :], 1.0 / E, None, OP.mult)
            e.tensor_scalar(tmp, ps_W2[0:1, :], 1.0 / E, EPS, OP.mult, OP.add)
            e.tensor_tensor(var, mu, mu, OP.mult)
            e.tensor_tensor(var, tmp, var, OP.subtract)
            e.tensor_copy(lnr_bf[0:1, 0:512], var)
            e.tensor_copy(tmp, lnr_bf[0:1, 0:512].bitcast(U16))
            e.tensor_scalar(tmp, tmp, 0.5, MRND, OP.mult, OP.add)
            e.tensor_scalar(tmp, tmp, -MRND, None, OP.add)
            e.tensor_scalar(tmp, tmp, -1.0, 24375.0, OP.mult, OP.add)
            e.tensor_copy(lnr_bf[0:1, 0:512].bitcast(U16), tmp)
            e.tensor_copy(yy, lnr_bf[0:1, 0:512])
            for _ in range(3):
                e.tensor_tensor(tmp, yy, yy, OP.mult)
                e.tensor_tensor(tmp, tmp, var, OP.mult)
                e.tensor_scalar(tmp, tmp, -0.5, 1.5, OP.mult, OP.add)
                e.tensor_tensor(yy, yy, tmp, OP.mult)
            e.tensor_tensor(mrs, mu, yy, OP.mult)
            e.tensor_copy(lnr_bf[0:1, 0:512], yy)
            e.tensor_tensor(t2, yy, lnr_bf[0:1, 0:512], OP.subtract)
            e.tensor_copy(lnr_bf[0:1, 512:1024], t2)
            e.tensor_copy(lnr_bf[0:1, 1024:1536], mrs)
            e.tensor_tensor(t2, mrs, lnr_bf[0:1, 1024:1536], OP.subtract)
            return e.tensor_copy(lnr_bf[0:1, 1536:2048], t2)
        p.step("dve", chain)
        p.wait("pe", "dve")

        def bc(e):
            e.matmul(ps_W1[:, :], ones_row[:, :], lnr_bf[0:1, 0:512], start=True, stop=False)
            e.matmul(ps_W1[:, :], ones_row[:, :], lnr_bf[0:1, 512:1024], start=False, stop=True)
            e.matmul(ps_W2[:, :], ones_row[:, :], lnr_bf[0:1, 1024:1536], start=True, stop=False)
            return e.matmul(ps_W2[:, :], ones_row[:, :], lnr_bf[0:1, 1536:2048],
                            start=False, stop=True)
        p.step("pe", bc)
        p.wait("dve", "pe")

        def bev(e):
            e.tensor_copy(brstd[:, :], ps_W1[:, :])
            return e.tensor_copy(bmrs[:, :], ps_W2[:, :])
        p.step("dve", bev)
        p.wait("pool", "dve")
        for c in range(8):
            cl = slice(c * 512, (c + 1) * 512)
            eng = "dve" if c % 2 == 0 else "pool"
            tb = t1d if eng == "dve" else t1p

            def ap_fn(e, cl=cl, c=c, tb=tb):
                e.tensor_tensor(tb[:, :], src[:, cl], brstd[:, :], OP.mult)
                if gamma:
                    e.tensor_tensor(tb[:, :], tb[:, :], bmrs[:, :], OP.subtract)
                    return e.tensor_scalar(out_bf[:, cl], tb[:, :],
                                           gam_sb[:, c:c + 1], bet_sb[:, c:c + 1],
                                           OP.mult, OP.add)
                dst = out_f32 if out_f32 is not None else out_bf
                return e.tensor_tensor(dst[:, cl], tb[:, :], bmrs[:, :], OP.subtract)
            p.step(eng, ap_fn)

    p.wait("pe", "gpc", 1)

    def s_dummy(e):
        e.matmul(ps_S[:, 0:512], ones_row[0:1, 0:64], zrow_bf[0:1, 0:512],
                 start=True, stop=True)
        return e.matmul(ps_S[:, 512:1024], ones_row[0:1, 0:64], zrow_bf[0:1, 0:512],
                        start=True, stop=True)
    p.step("pe", s_dummy)

    for m in range(NMAC):
        mc = slice(m * MT, (m + 1) * MT)
        p.wait("gp", "pe")
        h_xt = p.dma("gp", "dgp", xt_bf[:, :].rearrange("p (f t) -> p f t", f=8),
                     xT.rearrange("(f p) t -> p f t", p=128)[:, :, mc])
        p.wait("sp", "dve")
        p.wait("sp", "pool")
        h_tab = [p.dma("sp", "dsp", cos_m[:, :], cosd[:, mc]),
                 p.dma("sp", "dsp", sin_m[:, :], ssin[:, mc])]

        p.wdma("pe", h_xt)
        if m == 0:
            p.wdma("pe", *h_w)
            p.wait("pe", "gpc", 1)
        kev_hist = []
        for oc in range(8):
            pst = ps_W1 if oc % 2 == 0 else ps_W2

            def kproj(e, oc=oc, pst=pst):
                ins = None
                for f in range(8):
                    ins = e.matmul(pst[:, :],
                                   W1[:, f * 1024 + oc * 128: f * 1024 + (oc + 1) * 128],
                                   xt_bf[:, f * 512:(f + 1) * 512],
                                   start=(f == 0), stop=(f == 7))
                return ins
            if oc >= 2:
                p.wait("pe", "act", kev_hist[oc - 2])
            p.step("pe", kproj)
            p.wait("act", "pe")

            def ev(e, oc=oc, pst=pst):
                return e.activation(FA[:, oc * 512:(oc + 1) * 512], pst[:, :], AF.Silu)
            kev_hist.append(p.step("act", ev))

        p.wait("dve", "act")
        p.wdma("dve", *h_tab)
        p.wait("pool", "act")
        p.wdma("pool", *h_tab)
        if m > 0:
            p.wdma("dve", *h_spill)
            p.wdma("pool", *h_spill)
        for c in range(8):
            cols = slice(c * 512, (c + 1) * 512)
            if c < 5:
                emit_rotary("dve", cols, FA, krot, swd, t1d, t2d)
            else:
                emit_rotary("pool", cols, FA, krot, swp, t1p, oscr_b)

        p.wait("pe", "dve")
        p.wait("pe", "pool")
        for s in range(4):
            for c in range(8):
                def tp(e, s=s, c=c):
                    return e.transpose(ps_T[:, :],
                                       krot[:, c * 512 + s * 128: c * 512 + (s + 1) * 128],
                                       ident[:, :])
                p.step("pe", tp)
                p.wait("dve", "pe")

                def tpe(e, s=s, c=c):
                    return e.tensor_copy(
                        ktm[:, s * 1024 + c * 128: s * 1024 + (c + 1) * 128], ps_T[:, :])
                v = p.step("dve", tpe)
                p.wait("pe", "dve", v)

        p.wait("sp", "dve")
        p.wait("sp", "pool")
        h_spill = [p.dma("sp", "dsp", kfm_sp[:, m * 4096:(m + 1) * 4096], krot[:, :])]
        p.wait("gp", "dve")
        p.wait("gp", "pe")
        p.wait("gp", "pool")
        h_spill.append(p.dma("gp", "dgp", ktm_sp[:, m * 4096:(m + 1) * 4096],
                             ktm[:, :]))

        for s in range(4):
            def vproj(e, s=s):
                ins = None
                for f in range(8):
                    for hf in range(2):
                        ins = e.matmul(ps_V[:, hf * 512:(hf + 1) * 512],
                                       xt_bf[:, f * 512 + s * 128: f * 512 + (s + 1) * 128],
                                       W2[:, f * 1024 + hf * 512: f * 1024 + (hf + 1) * 512],
                                       start=(f == 0), stop=(f == 7))
                return ins
            if s >= 1 or m >= 1:
                p.wait("pe", "dve")
            p.step("pe", vproj)
            p.wait("dve", "pe")

            def vev(e, s=s):
                return e.tensor_copy(vt[:, s * 1024:(s + 1) * 1024], ps_V[:, :])
            p.step("dve", vev)

        p.wait("sp", "dve")
        h_spill.append(p.dma("sp", "dsp", v_sp[:, m * 4096:(m + 1) * 4096], vt[:, :]))
        if "vsp" in dbg:
            p.wait("gp", "dve")
            p.dma("gp", "dgp", dbg["vsp"][:, m * 4096:(m + 1) * 4096], vt[:, :])
        if "kfm" in dbg:
            p.wait("gp", "dve")
            p.wait("gp", "pool")
            p.dma("gp", "dgp", dbg["kfm"][:, m * 4096:(m + 1) * 4096], krot[:, :])
        if "ktm" in dbg:
            p.wait("gp", "dve")
            p.dma("gp", "dgp", dbg["ktm"][:, m * 4096:(m + 1) * 4096], ktm[:, :])

        p.wait("pe", "dve")

        def dsacc(e, m=m):
            ins = None
            for s in range(4):
                for h in range(HCNT):
                    ins = e.matmul(ps_S[:, h * 64:(h + 1) * 64],
                                   ktm[:, s * 1024 + h * 64: s * 1024 + (h + 1) * 64],
                                   vt[:, s * 1024 + h * 64: s * 1024 + (h + 1) * 64],
                                   start=False, stop=(m == 3 and s == 3))
            return ins
        p.step("pe", dsacc)

        wd_flat = wdT.rearrange("a b -> (a b)").rearrange("(c p t) -> c p t",
                                                          p=128, t=512)
        for ci in range(16):
            g = m * 16 + ci
            emit_wmean_job(wd_flat[g, :, :], g)

    emit_wscale(0, INNER * E, 0, 1)

    p.wait("dve", "pe")

    def sa_ev(e):
        return e.tensor_copy(sin_f, ps_S[:, :])
    sa_v = p.step("dve", sa_ev)
    if "sat" in dbg:
        p.wait("sp", "dve", sa_v)
        p.dma("sp", "dsp", dbg["sat"][:, :], sin_f)
    p.wait("gp", "dve", sa_v)
    cc_in_sent = p.dma("gp", "dgp", cc_in, sin_f)
    p.wdma("gp", cc_in_sent)

    def coll(g):
        g.collective_compute(
            "AllGather", OP.bypass, replica_groups=[list(range(8))],
            ins=[cc_in.opt()], outs=[cc_out.opt()]).then_inc(p.sems["cc"], 1)
    p.plain("gp", coll)
    p.cnt["cc"] += 1
    p.wait("gp", "cc", 1)

    if "ccg" in dbg:
        p.dma("gp", "dgp", dbg["ccg"], cc_out)

    def mk_seldg(e):
        ins = None
        for rr in range(8):
            ins = e.tensor_scalar(seldg[:, rr * 64:(rr + 1) * 64],
                                  ident[0:64, 0:64], selc_sb[:, rr:rr + 1], None,
                                  OP.mult)
        return ins
    p.step("dve", mk_seldg)

    for rr in range(8):
        p.wait("gp", "dve")
        h_blk = p.dma("gp", "dgp", cc_sb, cc_out[rr * 64:(rr + 1) * 64, :])
        p.wdma("dve", h_blk)
        p.wait("dve", "pe")

        def blkcast(e):
            return e.tensor_copy(cc_bf[:, :], cc_sb)
        bc_v = p.step("dve", blkcast)
        p.wait("pe", "dve", bc_v)

        def sacc(e, rr=rr):
            e.matmul(ps_S[:, 0:512], seldg[:, rr * 64:(rr + 1) * 64],
                     cc_bf[:, 0:512], start=(rr == 0), stop=(rr == 7))
            return e.matmul(ps_S[:, 512:1024], seldg[:, rr * 64:(rr + 1) * 64],
                            cc_bf[:, 512:1024], start=(rr == 0), stop=(rr == 7))
        p.step("pe", sacc)

    if "sdg" in dbg:
        p.wait("gp", "dve")
        p.dma("gp", "dgp", dbg["sdg"][:, :], seldg[:, :])
        p.dma("gp", "dgp", dbg["ccbf"][:, :], cc_bf[:, :])
    if "sa" in dbg:
        p.wait("dve", "pe")

        def sadump(e):
            return e.tensor_copy(sin_f, ps_S[:, :])
        sd_v = p.step("dve", sadump)
        p.wait("pe", "dve", sd_v)
        p.wait("sp", "dve", sd_v)
        p.dma("sp", "dsp", dbg["sa"][:, :], sin_f)
        p.wdma_all("pe", "dsp")

    if STAGE < 2:
        for c in range(8):
            p.dma("sp", "dsp", y[c * 128:(c + 1) * 128, :], FA[:, 0:2048])
        p.wdma_all("sp", "dsp")
        p.wdma_all("gp", "dgp")
        _emit_all(nc, p)
        aps.close()
        abps.close()
        ab.close()
        octx.close()
        return nc

    aps.close()
    bps = ExitStack()
    ps_O1 = bps.enter_context(nc.psum_tensor("ps_O1", [64, 1024], F32))
    ps_O2 = bps.enter_context(nc.psum_tensor("ps_O2", [64, 1024], F32))

    p.wait("gp", "pe")
    h_wb = [p.dma("gp", "dgp", W1[:, :].rearrange("p (f o) -> p f o", f=8),
                  Wq.rearrange("(f p) o -> p f o", p=128)),
            p.dma("gp", "dgp", W2[:, :].rearrange("p (f o) -> p f o", f=8),
                  Wo.rearrange("(f p) o -> p f o", p=128))]


    wmean_jobs = []
    wg_flat = wgT.rearrange("a b -> (a b)").rearrange("(c p t) -> c p t",
                                                     p=128, t=512)
    for ci in range(64):
        wmean_jobs.append((wg_flat[ci, :, :], 64 + ci))
    wu_flat = wuSL.rearrange("a b c -> (a b c)").rearrange("(c p t) -> c p t",
                                                           p=128, t=512)
    for ci in range(64):
        wmean_jobs.append((wu_flat[ci, :, :], 128 + ci))
    wj = [0]
    tj = [0]

    for m in range(NMAC):
        mc = slice(m * MT, (m + 1) * MT)
        p.wait("gp", "pe")
        h_xt = p.dma("gp", "dgp", xt_bf[:, :].rearrange("p (f t) -> p f t", f=8),
                     xT.rearrange("(f p) t -> p f t", p=128)[:, :, mc])
        p.wait("sp", "dve")
        p.wait("sp", "pool")
        p.wait("sp", "pe")
        h_kv = [p.dma("sp", "dsp", kfmB[:, :], kfm_sp[:, m * 4096:(m + 1) * 4096]),
                p.dma("sp", "dsp", ktm[:, :], ktm_sp[:, m * 4096:(m + 1) * 4096])]
        p.wait("gp", "dve")
        p.wait("gp", "pe")
        h_kv.append(p.dma("gp", "dgp", vt[:, :], v_sp[:, m * 4096:(m + 1) * 4096]))
        h_tab = [p.dma("sp", "dsp", cos_m[:, :], cosd[:, mc]),
                 p.dma("sp", "dsp", sin_m[:, :], ssin[:, mc])]
        p.wait("dve", "pool")
        p.wait("pool", "dve")
        p.wait("act", "dve")
        p.wait("act", "pool")
        p.wdma("dve", *h_kv)

        qev_hist = []
        p.wdma("pe", h_xt)
        if m == 0:
            p.wdma("pe", *h_wb)
        for oc in range(8):
            pst = ps_W1 if oc % 2 == 0 else ps_W2

            def qproj(e, oc=oc, pst=pst):
                ins = None
                for f in range(8):
                    ins = e.matmul(pst[:, :],
                                   W1[:, f * 1024 + oc * 128: f * 1024 + (oc + 1) * 128],
                                   xt_bf[:, f * 512:(f + 1) * 512],
                                   start=(f == 0), stop=(f == 7))
                return ins
            if oc >= 2:
                p.wait("pe", "act", qev_hist[oc - 2])
            p.step("pe", qproj)
            p.wait("act", "pe")

            def ev(e, oc=oc, pst=pst):
                return e.activation(FA[:, oc * 512:(oc + 1) * 512], pst[:, :], AF.Silu)
            qev_hist.append(p.step("act", ev))

        if BSTOP < 2:
            continue
        p.wait("dve", "act")
        p.wdma("dve", *h_tab)
        p.wait("pool", "act")
        p.wdma("pool", *h_tab)
        for c in range(8):
            cols = slice(c * 512, (c + 1) * 512)
            if c < 5:
                emit_rotary("dve", cols, FA, krot, swd, t1d, t2d)
            else:
                emit_rotary("pool", cols, FA, krot, swp, t1p, oscr_b)
        p.wait("dve", "pool")

        def oddcp(e):
            e.tensor_copy(q_oddM[0:64, :], krot[64:128, :])
            return e.tensor_copy(k_oddM[0:64, :], kfmB[64:128, :])
        ocv = p.step("dve", oddcp)
        if m == 0 and "qod" in dbg:
            p.wait("gp", "dve", ocv)
            p.dma("gp", "dgp", dbg["qod"], q_oddM[0:64, :])
            p.dma("gp", "dgp", dbg["kod"], k_oddM[0:64, :])
            if "qrt" in dbg:
                p.dma("gp", "dgp", dbg["qrt"], krot[:, :])
            if "qsl" in dbg:
                p.dma("gp", "dgp", dbg["qsl"], FA[:, 0:4096])

        if BSTOP < 3:
            continue
        def g1(e):
            ins = None
            for f in range(8):
                ins = e.matmul(ps_O1[:, 0:512], Wg1_sb[:, f * 64:(f + 1) * 64],
                               xt_bf[:, f * 512:(f + 1) * 512],
                               start=(f == 0), stop=(f == 7))
            return ins
        p.wait("pe", "dve")
        p.step("pe", g1)
        p.wait("dve", "pe")

        def g1e(e):
            return e.tensor_copy(xg1_bf[:, :], ps_O1[:, 0:512])
        g1v = p.step("dve", g1e)
        p.wait("pe", "dve", g1v)
        for oc in range(8):
            pst = ps_W1 if oc % 2 == 0 else ps_W2

            def g2(e, oc=oc, pst=pst):
                return e.matmul(pst[:, :], Wg2_sb[:, oc * 128:(oc + 1) * 128],
                                xg1_bf[:, :], start=True, stop=True)
            if oc >= 2:
                p.wait("pe", "dve")
            p.step("pe", g2)
            p.wait("act", "pe")
            p.wait("act", "pool")

            def gt(e, oc=oc, pst=pst):
                return e.activation(jA[:, 0:512], pst[:, :], AF.Tanh, scale=0.5)
            p.step("act", gt)
            p.wait("dve", "act")

            def gs(e, oc=oc):
                return e.tensor_scalar(FB[:, oc * 512:(oc + 1) * 512], jA[:, 0:512],
                                       0.5, 0.5, OP.mult, OP.add)
            v = p.step("dve", gs)
            p.wait("act", "dve", v)

        if BSTOP < 4:
            continue
        p.wdma("pe", *h_kv)
        for s in range(4):
            g = m * 4 + s
            p.wait("dve", "pe")

            def sbf(e):
                return e.tensor_copy(S_bf[:, :], ps_S[:, :])
            sv = p.step("dve", sbf)
            p.wait("pe", "dve", sv)

            if ASTOP < 2:
                continue
            for rnd in range(2):
                def emm(e, s=s, rnd=rnd):
                    ins = None
                    for hh in range(8):
                        h = rnd * 8 + hh
                        c = h // 2
                        pst = ps_W1 if hh < 4 else ps_W2
                        col = (hh % 4) * 128
                        cols = slice(c * 512 + s * 128, c * 512 + (s + 1) * 128)
                        kl = (kfmB[0:64, cols] if h % 2 == 0
                              else k_oddM[0:64, cols])
                        ql = (krot[0:64, cols] if h % 2 == 0
                              else q_oddM[0:64, cols])
                        ins = e.matmul(pst[:, col:col + 128], kl, ql,
                                       start=True, stop=True)
                    return ins
                p.step("pe", emm)
                p.wait("dve", "pe")

                def eev(e, rnd=rnd):
                    e.tensor_tensor(eT_sb[:, rnd * 1024: rnd * 1024 + 512],
                                    ps_W1[:, :], mask4[:, :], OP.mult)
                    return e.tensor_tensor(eT_sb[:, rnd * 1024 + 512: rnd * 1024 + 1024],
                                           ps_W2[:, :], mask4[:, :], OP.mult)
                ev_v = p.step("dve", eev)
                p.wait("pe", "dve", ev_v)

            if ASTOP < 3:
                continue
            def avq(e, s=s):
                ins = None
                for h in range(HCNT):
                    c = h // 2
                    pso = ps_O1 if h % 2 == 0 else ps_O2
                    ocol = c * 128
                    e.matmul(pso[:, ocol:ocol + 128],
                             vt[:, s * 1024 + h * 64: s * 1024 + (h + 1) * 64],
                             eT_sb[:, h * 128:(h + 1) * 128],
                             start=True, stop=False)
                    qsl = (krot[0:64, c * 512 + s * 128: c * 512 + (s + 1) * 128]
                           if h % 2 == 0 else
                           q_oddM[0:64, c * 512 + s * 128: c * 512 + (s + 1) * 128])
                    ins = e.matmul(pso[:, ocol:ocol + 128],
                                   S_bf[:, h * 64:(h + 1) * 64], qsl,
                                   start=False, stop=True)
                return ins
            p.step("pe", avq)

            if ASTOP < 4:
                continue
            def dsu(e, s=s, g=g):
                ins = None
                for h in range(HCNT):
                    ins = e.matmul(ps_S[:, h * 64:(h + 1) * 64],
                                   ktm[:, s * 1024 + h * 64: s * 1024 + (h + 1) * 64],
                                   vt[:, s * 1024 + h * 64: s * 1024 + (h + 1) * 64],
                                   start=False, stop=(g == 15))
                return ins
            p.step("pe", dsu)

            if ASTOP < 5:
                continue
            p.wait("dve", "pe")

            def oge(e, s=s):
                ins = None
                for par, pso in ((0, ps_O1), (1, ps_O2)):
                    ov = FC[par * 64:(par + 1) * 64, :].rearrange(
                        "p (c t) -> p c t", c=8)[:, :, s * 128:(s + 1) * 128]
                    iv = pso[:, :].rearrange("p (c t) -> p c t", c=8)
                    gv = FB[par * 64:(par + 1) * 64, :].rearrange(
                        "p (c t) -> p c t", c=8)[:, :, s * 128:(s + 1) * 128]
                    ins = e.tensor_tensor(ov, iv, gv, OP.mult)
                return ins
            ogv = p.step("dve", oge)
            p.wait("pe", "dve", ogv)

        if BSTOP >= 8:
            for _ in range(32):
                if wj[0] < len(wmean_jobs):
                    srcj, col = wmean_jobs[wj[0]]
                    emit_wmean_job(srcj, col)
                    wj[0] += 1
            for _ in range(16):
                if tj[0] < 64:
                    emit_wtern_job(tj[0])
                    tj[0] += 1

        if "og" in dbg:
            p.wait("sp", "dve")
            p.dma("sp", "dsp", dbg["og"][:, m * 4096:(m + 1) * 4096], FC[:, :])
        if "gate" in dbg:
            p.wait("sp", "dve")
            p.dma("sp", "dsp", dbg["gate"][:, m * 4096:(m + 1) * 4096], FB[:, :])
        if BSTOP < 5:
            continue
        emit_ln(FC, True, out_bf=xn_bf)

        if BSTOP < 6:
            continue
        p.wait("pe", "dve")
        p.wait("pe", "pool")
        r1_hist = []
        for oc in range(8):
            pst = ps_W1 if oc % 2 == 0 else ps_W2
            xslot = (t1d if oc % 2 == 0 else t2d)[:, :]
            p.wait("sp", "dve")
            h_x = p.dma("sp", "dsp", xslot,
                        xT[oc * 128:(oc + 1) * 128, mc])

            def wop(e, oc=oc, pst=pst):
                ins = None
                for f in range(8):
                    ins = e.matmul(pst[:, :],
                                   W2[:, f * 1024 + oc * 128: f * 1024 + (oc + 1) * 128],
                                   xn_bf[:, f * 512:(f + 1) * 512],
                                   start=(f == 0), stop=(f == 7))
                return ins
            if oc >= 2:
                p.wait("pe", "dve", r1_hist[oc - 2])
            p.step("pe", wop)
            p.wait("dve", "pe")
            p.wdma("dve", h_x)

            def r1f(e, oc=oc, pst=pst, xslot=xslot):
                return e.tensor_tensor(FA[:, oc * 512:(oc + 1) * 512], pst[:, :],
                                       xslot, OP.add)
            r1_hist.append(p.step("dve", r1f))

        if BSTOP < 7:
            continue
        emit_ln(FA, False, out_f32=FB)

        p.wait("sp", "dve")
        p.wait("sp", "pool")
        p.dma("sp", "dsp", xmid_sp[:, m * 4096:(m + 1) * 4096], FB[:, :])
        if "xmid" in dbg:
            p.dma("sp", "dsp", dbg["xmid"][:, m * 4096:(m + 1) * 4096], FB[:, :])
        if "r1" in dbg:
            p.dma("sp", "dsp", dbg["r1"][:, m * 4096:(m + 1) * 4096], FA[:, :])
        if "xn" in dbg:
            p.wait("gp", "dve")
            p.wait("gp", "pool")
            p.dma("gp", "dgp", dbg["xn"][:, m * 4096:(m + 1) * 4096], xn_bf[:, :])

    if BSTOP >= 8:
        while wj[0] < len(wmean_jobs):
            srcj, col = wmean_jobs[wj[0]]
            emit_wmean_job(srcj, col)
            wj[0] += 1
        while tj[0] < 64:
            emit_wtern_job(tj[0])
            tj[0] += 1

        emit_wscale(64, INNER * E, 2, 4)
        emit_wscale(128, INNER * E, 3, 5)
        if "wsc" in dbg:
            p.wait("sp", "dve")
            p.dma("sp", "dsp", dbg["wsc"], wsc[:, 0:8])
            p.dma("sp", "dsp", dbg["tinyd"], tiny[0:1, :])
            p.dma("sp", "dsp", dbg["waccd"], wacc[:, :])
            if "spdbg" in dbg:
                p.dma("sp", "dsp", dbg["spdbg"], t2d[0:32, 0:16])
                p.wait("gp", "dve")
                p.dma("gp", "dgp", dbg["wabdbg"], wab_bf[0:32, 0:4])
                p.dma("sp", "dsp", dbg["t1dbg"], t1d[:, 0:64])

    if "sa" in dbg:
        p.wait("sp", "dve")
        p.dma("sp", "dsp", dbg["sa"][:, :], sin_f)

    if STAGE < 3:
        for c in range(8):
            p.dma("sp", "dsp", y[c * 128:(c + 1) * 128, :], FA[:, 0:2048])
        p.wdma_all("sp", "dsp")
        p.wdma_all("gp", "dgp")
        _emit_all(nc, p)
        bps.close()
        abps.close()
        ab.close()
        octx.close()
        return nc

    fin = {e: p.cnt[e] for e in ("pe", "act", "dve", "pool")}
    for e in ("pe", "act", "dve", "pool", "sp", "gp"):
        for s in ("pe", "act", "dve", "pool"):
            if s != e:
                p.wait(e, s, fin[s])
        p.wdma_all(e, "dsp")
        p.wdma_all(e, "dgp")

    bps.close()
    abps.close()
    ab.close()

    cs = ExitStack()

    def csb(name, shape, dt=F32):
        return cs.enter_context(nc.sbuf_tensor(name, shape, dt))

    wg_sb = csb("wg_sb", [128, 8 * INNER], BF16)
    junkc = csb("junkc", [128, 1024])
    junk2 = csb("junk2", [128, 1024])
    wstr = csb("wstr", [128, 3072])
    xmid_c = csb("xmid_c", [128, 2048])
    hbuf = csb("hbuf", [128, 8192])
    hq = csb("hq", [128, 8192], BF16)
    xq_deq = csb("xq_deq", [128, 2048], BF16)
    xq_tmp = csb("xq_tmp", [128, 256], BF16)
    gsil = csb("gsil", [128, 256])
    wd_buf = csb("wd_buf", [128, 4096], BF16)
    wu_buf = csb("wu_buf", [128, 4096], BF16)
    rmx = csb("rmx", [128, 1024])
    rmn = csb("rmn", [128, 512])
    fold = csb("fold", [64, 512])
    fold_bf = csb("fold_bf", [32, 512], BF16)
    tmv = csb("tmv", [128, 16])
    tmv_bf = csb("tmv_bf", [128, 8], BF16)
    rowv_bf = csb("rowv_bf", [1, 1024], BF16)
    bs1 = csb("bs1", [128, 256])
    bsi1 = csb("bsi1", [128, 256])
    bs2 = csb("bs2", [128, 256])
    bsd = csb("bsd", [128, 256])
    ytile = csb("ytile", [128, 2048])
    lnr2 = csb("lnr2", [1, 1280])
    lnr2_bf = csb("lnr2_bf", [1, 1024], BF16)
    br2 = csb("br2", [128, 256])
    bm2 = csb("bm2", [128, 256])
    tscr = csb("tscr", [128, 512])
    oscr = csb("oscr", [128, 512])

    cps = ExitStack()
    ps_GU = cps.enter_context(nc.psum_tensor("ps_GU", [128, 512], F32))
    ps_GU2 = cps.enter_context(nc.psum_tensor("ps_GU2", [128, 512], F32))
    ps_DN = cps.enter_context(nc.psum_tensor("ps_DN", [128, 2048], F32))
    ps_TP = cps.enter_context(nc.psum_tensor("ps_TP", [128, 512], BF16))

    def spacer_c(e):
        return e.tensor_scalar(junkc[:, 256:512], junkc[:, 0:256], 1.0, None,
                               OP.mult)

    cln_bf = hq[:, 0:2048]
    sq3_bf = hq[:, 2048:4096]

    wg_flatC = wgT.rearrange("a b -> (a b)").rearrange("(c p t) -> c p t",
                                                       p=128, t=1024)
    ca_hist = []
    for ci in range(32):
        buf = wstr[:, (ci % 3) * 1024:(ci % 3) * 1024 + 1024]
        jb = junkc if ci % 2 == 0 else junk2
        q, dq = ("sp", "dsp") if ci % 2 == 0 else ("gp", "dgp")
        if ci >= 3:
            p.wait(q, "dve", ca_hist[ci - 3])
        else:
            p.wait(q, "dve")
            p.wait(q, "pool")
        hb = p.dma(q, dq, buf, wg_flatC[ci, :, :])
        p.wdma("dve", hb)
        p.wait("dve", "pool")

        def ca(e, buf=buf, jb=jb):
            e.scalar_tensor_tensor(jb[:, 0:512], buf[:, 0:512], wsc[:, 2:3],
                                   mrow[:, :], OP.mult, OP.add)
            return e.scalar_tensor_tensor(jb[:, 512:1024], buf[:, 512:1024],
                                          wsc[:, 2:3], mrow[:, :], OP.mult, OP.add)
        ca_hist.append(p.step("dve", ca))
        p.wait("pool", "dve")

        def cb(e, ci=ci, jb=jb):
            e.tensor_scalar(oscr[:, 0:512], jb[:, 0:512], MRND + 1.0, MRND - 1.0,
                            OP.min, OP.max)
            e.tensor_scalar(oscr[:, 0:512], jb[:, 512:1024], MRND + 1.0, MRND - 1.0,
                            OP.min, OP.max) if False else None
            e.tensor_scalar(
                wg_sb[:, :].rearrange("p (c t) -> p c t", c=32)[:, ci, 0:512],
                oscr[:, 0:512], -MRND, None, OP.add)
            e.tensor_scalar(oscr[:, 0:512], jb[:, 512:1024], MRND + 1.0, MRND - 1.0,
                            OP.min, OP.max)
            return e.tensor_scalar(
                wg_sb[:, :].rearrange("p (c t) -> p c t", c=32)[:, ci, 512:1024],
                oscr[:, 0:512], -MRND, None, OP.add)
        p.step("pool", cb)
    for oc in range(32):
        buf = wstr[:, (oc % 3) * 1024:(oc % 3) * 1024 + 1024]
        jb = (junkc if oc % 2 == 0 else junk2)[:, 0:1024]
        tb = wu_buf[:, (oc % 2) * 1024:(oc % 2) * 1024 + 1024]
        q, dq = ("sp", "dsp") if oc % 2 == 0 else ("gp", "dgp")
        if oc >= 3:
            p.wait(q, "dve", ca_hist[32 + oc - 3])
        else:
            p.wait(q, "dve")
            p.wait(q, "pool")
        hb = p.dma(q, dq, buf,
                   wuSL[oc, :, :].rearrange("(f p) o -> p f o", p=128))
        p.wdma("dve", hb)
        p.wait("dve", "pool")

        def cua(e, buf=buf, jb=jb):
            e.scalar_tensor_tensor(jb[:, 0:512], buf[:, 0:512], wsc[:, 3:4],
                                   mrow[:, :], OP.mult, OP.add)
            return e.scalar_tensor_tensor(jb[:, 512:1024], buf[:, 512:1024],
                                          wsc[:, 3:4], mrow[:, :], OP.mult, OP.add)
        ca_hist.append(p.step("dve", cua))
        p.wait("pool", "dve")

        def cub(e, jb=jb, tb=tb):
            e.tensor_scalar(oscr[:, 0:512], jb[:, 0:512], MRND + 1.0, MRND - 1.0,
                            OP.min, OP.max)
            e.tensor_scalar(tb[:, 0:512], oscr[:, 0:512], -MRND, None, OP.add)
            e.tensor_scalar(oscr[:, 0:512], jb[:, 512:1024], MRND + 1.0, MRND - 1.0,
                            OP.min, OP.max)
            return e.tensor_scalar(tb[:, 512:1024], oscr[:, 0:512], -MRND, None,
                                   OP.add)
        v = p.step("pool", cub)
        p.wait("sp", "pool", v)
        p.dma("sp", "dsp", wu_t[oc, :, :], tb)

    def emit_scales(chunks, s_dst, inv_dst):
        n = len(chunks)
        p.wait("dve", "pool")

        def mm0(e):
            e.tensor_copy(rmx[:, 0:256], chunks[0])
            return e.tensor_copy(rmn[:, 0:256], chunks[0])
        p.step("dve", mm0)
        cur = 0
        for i in range(1, n):
            nxt = (cur + 1) % 2

            def mm(e, i=i, cur=cur, nxt=nxt):
                e.tensor_tensor(rmx[:, nxt * 256:(nxt + 1) * 256],
                                rmx[:, cur * 256:(cur + 1) * 256],
                                chunks[i], OP.max)
                return e.tensor_tensor(rmn[:, nxt * 256:(nxt + 1) * 256],
                                       rmn[:, cur * 256:(cur + 1) * 256],
                                       chunks[i], OP.min)
            p.step("dve", mm)
            cur = nxt

        def comb(e, cur=cur):
            e.scalar_tensor_tensor(rmx[:, 512:768], rmn[:, cur * 256:(cur + 1) * 256],
                                   -1.0, rmx[:, cur * 256:(cur + 1) * 256],
                                   OP.mult, OP.max)
            e.tensor_copy(fold[0:64, 0:256], rmx[64:128, 512:768])
            e.tensor_tensor(fold[0:64, 256:512], rmx[0:64, 512:768],
                            fold[0:64, 0:256], OP.max)
            e.tensor_copy(rmx[0:32, 768:1024], fold[32:64, 256:512])
            e.tensor_tensor(fold[0:32, 0:256], fold[0:32, 256:512],
                            rmx[0:32, 768:1024], OP.max)
            e.tensor_copy(fold_bf[:, 0:256], fold[0:32, 0:256])
            e.tensor_tensor(fold[0:32, 256:512], fold[0:32, 0:256],
                            fold_bf[:, 0:256], OP.subtract)
            return e.tensor_copy(fold_bf[:, 256:512], fold[0:32, 256:512])
        p.step("dve", comb)
        p.wait("pe", "dve")

        def tp1(e):
            e.transpose(ps_TP[:, 0:32], fold_bf[:, 0:128], ident[0:32, 0:32])
            e.transpose(ps_TP[:, 64:96], fold_bf[:, 256:384], ident[0:32, 0:32])
            e.transpose(ps_TP[:, 32:64], fold_bf[:, 128:256], ident[0:32, 0:32])
            return e.transpose(ps_TP[:, 96:128], fold_bf[:, 384:512],
                               ident[0:32, 0:32])
        p.step("pe", tp1)
        p.wait("dve", "pe")

        def tp1e(e):
            e.tensor_copy(tscr[:, 0:32], ps_TP[:, 0:32])
            e.tensor_copy(tscr[:, 64:96], ps_TP[:, 64:96])
            e.tensor_copy(tscr[:, 32:64], ps_TP[:, 32:64])
            e.tensor_copy(tscr[:, 96:128], ps_TP[:, 96:128])
            spacer_c(e)
            e.tensor_tensor(tscr[:, 128:192], tscr[:, 0:64], tscr[:, 64:128],
                            OP.add)
            spacer_c(e)
            e.reduce_max(tmv[:, 0:1], tscr[:, 128:160], axis=AX)
            e.reduce_max(tmv[:, 1:2], tscr[:, 160:192], axis=AX)
            spacer_c(e)
            e.tensor_copy(tmv_bf[:, 0:2], tmv[:, 0:2])
            spacer_c(e)
            e.scalar_tensor_tensor(tmv[:, 2:4], tmv_bf[:, 0:2], -1.0,
                                   tmv[:, 0:2], OP.mult, OP.add)
            spacer_c(e)
            return e.tensor_copy(tmv_bf[:, 2:4], tmv[:, 2:4])
        p.step("dve", tp1e)
        p.wait("pe", "dve")

        def tpb(e):
            e.transpose(ps_TP[0:1, 0:128], tmv_bf[:, 0:1], ident[:, :])
            e.transpose(ps_TP[0:1, 128:256], tmv_bf[:, 1:2], ident[:, :])
            e.transpose(ps_TP[0:1, 256:384], tmv_bf[:, 2:3], ident[:, :])
            return e.transpose(ps_TP[0:1, 384:512], tmv_bf[:, 3:4], ident[:, :])
        p.step("pe", tpb)
        p.wait("dve", "pe")

        def chain(e):
            S = [lnr2[0:1, i * 256:(i + 1) * 256] for i in range(5)]
            e.tensor_copy(S[0], ps_TP[0:1, 256:512])
            e.tensor_tensor(S[1], ps_TP[0:1, 0:256], S[0], OP.add)
            e.tensor_scalar(S[2], S[1], 1e-5, None, OP.max)
            e.reciprocal(S[3], S[2])
            e.tensor_scalar(S[4], S[3], 127.0, None, OP.mult)
            e.tensor_scalar(S[1], S[2], 1.0 / 127.0, None, OP.mult)
            e.tensor_copy(rowv_bf[0:1, 0:256], S[4])
            e.tensor_copy(rowv_bf[0:1, 512:768], S[1])
            e.scalar_tensor_tensor(S[0], rowv_bf[0:1, 0:256], -1.0, S[4],
                                   OP.mult, OP.add)
            return e.tensor_copy(rowv_bf[0:1, 256:512], S[0])
        p.step("dve", chain)
        p.wait("pe", "dve")

        def bcst(e):
            e.matmul(ps_GU[:, 0:256], ones_row[:, :], rowv_bf[0:1, 0:256],
                     start=True, stop=False)
            e.matmul(ps_GU[:, 0:256], ones_row[:, :], rowv_bf[0:1, 256:512],
                     start=False, stop=True)
            return e.matmul(ps_GU2[:, 0:256], ones_row[:, :], rowv_bf[0:1, 512:768],
                            start=True, stop=True)
        p.step("pe", bcst)
        p.wait("dve", "pe")

        def bcev(e):
            e.tensor_copy(s_dst[:, :], ps_GU[:, 0:256])
            return e.tensor_copy(inv_dst[:, :], ps_GU2[:, 0:256])
        v = p.step("dve", bcev)
        p.wait("pe", "dve", v)

    for u in range(NSUB):
        m, hh = u // 2, u % 2
        p.wait("dve", "pool")
        p.wait("pool", "dve")
        p.wait("pool", "pe")
        p.wait("act", "dve")
        xq_, dq_ = ("sp", "dsp") if u % 2 == 0 else ("gp", "dgp")
        p.wait(xq_, "dve")
        p.wait(xq_, "pool")
        if u == 0:
            p.wdma_all("sp", "dsp")
        h_xm = p.dma(xq_, dq_, xmid_c[:, :].rearrange("p (c t) -> p c t", c=8),
                     xmid_sp[:, :].rearrange("p (mm c t) -> p mm c t", mm=4, c=8)
                     [:, m, :, hh * 256:(hh + 1) * 256])
        p.wdma("dve", h_xm)
        p.wdma("pool", h_xm)

        xch = [xmid_c[:, c * 256:(c + 1) * 256] for c in range(8)]
        emit_scales(xch, bs1, bsi1)
        if u == DUMP_U and "bs1" in dbg:
            p.wait("sp", "dve")
            p.dma("sp", "dsp", dbg["bs1"], bs1[:, :])
            p.dma("sp", "dsp", dbg["tmv"], tmv[:, :])
            p.dma("sp", "dsp", dbg["habs"], rmx[:, 512:768])
            p.wait("gp", "dve")
            p.dma("gp", "dgp", dbg["bsi1"], bsi1[:, :])
            if "rowv" in dbg:
                p.dma("gp", "dgp", dbg["rowv"], rowv_bf[:, :])
                p.dma("gp", "dgp", dbg["tmvbf"], tmv_bf[:, :])

        for c in range(8):
            def xq1(e, c=c):
                e.tensor_tensor(tscr[:, 0:256], xch[c], bs1[:, :], OP.mult)
                e.tensor_scalar(tscr[:, 256:512], tscr[:, 0:256], MRND, MRND + 127.0,
                                OP.add, OP.min)
                e.tensor_scalar(xq_tmp[:, :], tscr[:, 256:512], MRND - 128.0, -MRND,
                                OP.max, OP.add)
                return e.tensor_tensor(xq_deq[:, c * 256:(c + 1) * 256], xq_tmp[:, :],
                                       bsi1[:, :], OP.mult)
            p.step("dve", xq1)

        p.wait("pe", "dve")
        if u == 0:
            p.wait("pe", "pool")
            p.wdma_all("gp", "dsp")
        gu_hist = []
        for oc in range(32):
            slab = wu_buf[:, (oc % 4) * 1024:(oc % 4) * 1024 + 1024]
            q, dq = ("sp", "dsp") if oc % 2 == 0 else ("gp", "dgp")
            if oc >= 4:
                p.wait(q, "pe", gu_hist[oc - 4])
            else:
                p.wait(q, "pe")
            h_sl = p.dma(q, dq, slab, wu_t[oc, :, :])
            pst = ps_GU if oc % 2 == 0 else ps_GU2
            p.wdma("pe", h_sl)
            if oc >= 2:
                p.wait("pe", "dve")

            col = (oc % 2) * 256

            def gu(e, oc=oc, col=col, slab=slab):
                ins = None
                for f in range(8):
                    e.matmul(ps_GU[:, col:col + 256],
                             wg_sb[:, f * INNER + oc * 128: f * INNER + (oc + 1) * 128],
                             xq_deq[:, f * 256:(f + 1) * 256],
                             start=(f == 0), stop=(f == 7))
                    ins = e.matmul(ps_GU2[:, col:col + 256], slab[:, f * 128:(f + 1) * 128],
                                   xq_deq[:, f * 256:(f + 1) * 256],
                                   start=(f == 0), stop=(f == 7))
                return ins
            gu_hist.append(p.step("pe", gu))
            p.wait("act", "pe")
            p.wait("act", "dve")

            def sil(e, col=col):
                return e.activation(gsil[:, :], ps_GU[:, col:col + 256], AF.Silu,
                                    scale=wsc[:, 4:5])
            sv = p.step("act", sil)
            p.wait("dve", "act", sv)

            def hm(e, oc=oc, col=col):
                return e.scalar_tensor_tensor(hbuf[:, oc * 256:(oc + 1) * 256],
                                              ps_GU2[:, col:col + 256], wsc[:, 5:6],
                                              gsil[:, :], OP.mult, OP.mult)
            p.step("dve", hm)

        hch = [hbuf[:, c * 256:(c + 1) * 256] for c in range(32)]
        emit_scales(hch, bs2, bsd)

        p.wait("pool", "dve")
        for c in range(32):
            eng = "dve" if c % 2 == 0 else "pool"
            tb = tscr if eng == "dve" else oscr

            def hqf(e, c=c, tb=tb):
                e.tensor_tensor(tb[:, 0:256], hch[c], bs2[:, :], OP.mult)
                e.tensor_scalar(tb[:, 256:512], tb[:, 0:256], MRND, MRND + 127.0,
                                OP.add, OP.min)
                return e.tensor_scalar(hq[:, c * 256:(c + 1) * 256], tb[:, 256:512],
                                       MRND - 128.0, -MRND, OP.max, OP.add)
            p.step(eng, hqf)
        p.wait("pe", "dve")
        p.wait("pe", "pool")

        dn_hist = []
        for mi in range(32):
            slab = wd_buf[:, (mi % 4) * 1024:(mi % 4) * 1024 + 1024]
            q, dq = ("sp", "dsp") if mi % 2 == 0 else ("gp", "dgp")
            if mi >= 4:
                p.wait(q, "pe", dn_hist[mi - 4])
            else:
                p.wait(q, "pe")
            h_sl = p.dma(q, dq, slab, wd_t[mi * 128:(mi + 1) * 128, :])
            p.wdma("pe", h_sl)

            def dn(e, mi=mi, slab=slab):
                if mi == 0:
                    for b in range(4):
                        e.matmul(ps_DN[:, b * 512:(b + 1) * 512], ones_row[:, :],
                                 zrow_bf[0:1, 0:512], start=True, stop=True)
                ins = None
                for oc in range(8):
                    ins = e.matmul(ps_DN[:, oc * 256:(oc + 1) * 256],
                                   slab[:, oc * 128:(oc + 1) * 128],
                                   hq[:, mi * 256:(mi + 1) * 256],
                                   start=False, stop=(mi == 31))
                return ins
            dn_hist.append(p.step("pe", dn))

        p.wait("dve", "pe")
        for c in range(8):
            def r2f(e, c=c):
                e.scalar_tensor_tensor(tscr[:, 0:256], ps_DN[:, c * 256:(c + 1) * 256],
                                       wsc[:, 1:2], bsd[:, :], OP.mult, OP.mult)
                return e.tensor_tensor(ytile[:, c * 256:(c + 1) * 256],
                                       tscr[:, 0:256], xch[c], OP.add)
            p.step("dve", r2f)

        if u == DUMP_U:
            if "xq0" in dbg:
                p.wait("gp", "dve")
                p.dma("gp", "dgp", dbg["xq0"][:, :], xq_deq[:, :])
            if "h0" in dbg:
                p.wait("sp", "dve")
                p.dma("sp", "dsp", dbg["h0"][:, :], hbuf[:, :])
            if "hq0" in dbg:
                p.wait("gp", "dve")
                p.wait("gp", "pool")
                hqh = p.dma("gp", "dgp", dbg["hq0"][:, :], hq[:, :])
                p.wdma("dve", hqh)
            if "r20" in dbg:
                p.wait("sp", "dve")
                p.dma("sp", "dsp", dbg["r20"][:, :], ytile[:, :])
        for c in range(8):
            cl = slice(c * 256, (c + 1) * 256)

            def cp3(e, cl=cl):
                return e.tensor_copy(cln_bf[:, cl], ytile[:, cl])
            p.step("dve", cp3)
            p.wait("act", "dve")

            def sq3(e, cl=cl):
                return e.activation(sq3_bf[:, cl], ytile[:, cl], AF.Square)
            p.step("act", sq3)
        p.wait("pe", "dve")
        p.wait("pe", "act")

        def st3(e):
            ins = None
            for c in range(8):
                cl = slice(c * 256, (c + 1) * 256)
                e.matmul(ps_DN[0:1, 0:256], ones_col[:, :], cln_bf[:, cl],
                         start=(c == 0), stop=(c == 7))
                ins = e.matmul(ps_DN[0:1, 1024:1280], ones_col[:, :], sq3_bf[:, cl],
                               start=(c == 0), stop=(c == 7))
            return ins
        p.step("pe", st3)
        p.wait("dve", "pe")

        def chain3(e):
            L = lnr2
            mu, var, yy, tmp, mrs = (L[0:1, i * 256:(i + 1) * 256]
                                     for i in range(5))
            t2 = tmp
            e.tensor_scalar(mu, ps_DN[0:1, 0:256], 1.0 / E, None, OP.mult)
            e.tensor_scalar(tmp, ps_DN[0:1, 1024:1280], 1.0 / E, EPS, OP.mult, OP.add)
            e.tensor_tensor(var, mu, mu, OP.mult)
            e.tensor_tensor(var, tmp, var, OP.subtract)
            e.tensor_copy(lnr2_bf[0:1, 0:256], var)
            e.tensor_copy(tmp, lnr2_bf[0:1, 0:256].bitcast(U16))
            e.tensor_scalar(tmp, tmp, 0.5, MRND, OP.mult, OP.add)
            e.tensor_scalar(tmp, tmp, -MRND, None, OP.add)
            e.tensor_scalar(tmp, tmp, -1.0, 24375.0, OP.mult, OP.add)
            e.tensor_copy(lnr2_bf[0:1, 0:256].bitcast(U16), tmp)
            e.tensor_copy(yy, lnr2_bf[0:1, 0:256])
            for _ in range(3):
                e.tensor_tensor(tmp, yy, yy, OP.mult)
                e.tensor_tensor(tmp, tmp, var, OP.mult)
                e.tensor_scalar(tmp, tmp, -0.5, 1.5, OP.mult, OP.add)
                e.tensor_tensor(yy, yy, tmp, OP.mult)
            e.tensor_tensor(mrs, mu, yy, OP.mult)
            e.tensor_copy(lnr2_bf[0:1, 0:256], yy)
            e.tensor_tensor(t2, yy, lnr2_bf[0:1, 0:256], OP.subtract)
            e.tensor_copy(lnr2_bf[0:1, 256:512], t2)
            e.tensor_copy(lnr2_bf[0:1, 512:768], mrs)
            e.tensor_tensor(t2, mrs, lnr2_bf[0:1, 512:768], OP.subtract)
            return e.tensor_copy(lnr2_bf[0:1, 768:1024], t2)
        p.step("dve", chain3)
        p.wait("pe", "dve")

        def bc3(e):
            e.matmul(ps_DN[:, 256:512], ones_row[:, :], lnr2_bf[0:1, 0:256],
                     start=True, stop=False)
            e.matmul(ps_DN[:, 256:512], ones_row[:, :], lnr2_bf[0:1, 256:512],
                     start=False, stop=True)
            e.matmul(ps_DN[:, 1280:1536], ones_row[:, :], lnr2_bf[0:1, 512:768],
                     start=True, stop=False)
            return e.matmul(ps_DN[:, 1280:1536], ones_row[:, :], lnr2_bf[0:1, 768:1024],
                            start=False, stop=True)
        p.step("pe", bc3)
        p.wait("dve", "pe")

        def bev3(e):
            e.tensor_copy(br2[:, :], ps_DN[:, 256:512])
            return e.tensor_copy(bm2[:, :], ps_DN[:, 1280:1536])
        p.step("dve", bev3)
        p.wait("pool", "dve")

        for c in range(8):
            cl = slice(c * 256, (c + 1) * 256)
            eng = "dve" if c % 2 == 0 else "pool"
            tb = tscr if eng == "dve" else oscr

            def ap3(e, cl=cl, tb=tb):
                e.tensor_tensor(tb[:, 0:256], ytile[:, cl], br2[:, :], OP.mult)
                return e.tensor_tensor(tb[:, 256:512], tb[:, 0:256], bm2[:, :],
                                       OP.subtract)
            v = p.step(eng, ap3)
            p.wait("sp", eng, v)
            h_y = p.dma("sp", "dsp",
                        y[c * 128:(c + 1) * 128, u * 256:(u + 1) * 256], tb[:, 256:512])
            if c < 6:
                p.wdma(eng, h_y)

    p.wdma_all("sp", "dsp")
    p.wdma_all("gp", "dgp")

    _emit_all(nc, p)
    cps.close()
    cs.close()
    octx.close()
    return nc


def _emit_all(nc, p):
    with nc.Block() as block:
        @block.tensor
        def _(e):
            for _, fn in p.ops["pe"]:
                fn(e)

        @block.scalar
        def _(e):
            e.preload_act_table(AF.Silu)
            for _, fn in p.ops["act"]:
                fn(e)

        @block.vector
        def _(e):
            for _, fn in p.ops["dve"]:
                fn(e)

        @block.gpsimd
        def _(e):
            merged = sorted(p.ops["gp"] + p.ops["pool"], key=lambda t: t[0])
            for _, fn in merged:
                fn(e)

        @block.sync
        def _(e):
            for _, fn in p.ops["sp"]:
                fn(e)


def _host_inputs(full):
    x = np.asarray(full["L0_attn_x"], dtype=np.float32)
    BASE = 10000.0
    invf = BASE ** (-(np.arange(32, dtype=np.float64) * 2.0) / 64.0)
    freqs = invf[np.arange(128) % 32]
    sgn = np.where((np.arange(128) // 32) % 2 == 0, -1.0, 1.0)
    wu = np.asarray(full["L0_ffn_w_up"], np.float32)
    shared = {
        "Wq": np.ascontiguousarray(full["L0_attn_Wq"], dtype=np.float32),
        "Wk": np.ascontiguousarray(full["L0_attn_Wk"], dtype=np.float32),
        "Wv": np.ascontiguousarray(full["L0_attn_Wv"], dtype=np.float32),
        "Wo": np.ascontiguousarray(full["L0_attn_Wo"], dtype=np.float32),
        "Wg1": np.ascontiguousarray(full["L0_attn_Wg1"], dtype=np.float32),
        "Wg2": np.ascontiguousarray(full["L0_attn_Wg2"], dtype=np.float32),
        "wgT": np.ascontiguousarray(
            np.asarray(full["L0_ffn_w_gate"], np.float32).T
            .reshape(8, 128, INNER).transpose(0, 2, 1)
            .reshape(32, 1024, 128).transpose(0, 2, 1).reshape(E, INNER)),
        "wuSL": np.ascontiguousarray(
            wu.T.reshape(E, 32, 128).transpose(1, 0, 2)),
        "wdT": np.ascontiguousarray(np.asarray(full["L0_ffn_w_down"], np.float32).T),
        "gam": np.ascontiguousarray(
            np.asarray(full["L0_attn_gamma"], np.float32).reshape(8, 128).T),
        "bet": np.ascontiguousarray(
            np.asarray(full["L0_attn_beta"], np.float32).reshape(8, 128).T),
    }
    in_maps = []
    for c in range(8):
        b, j = c // 4, c % 4
        pos = np.arange(j * TOK, (j + 1) * TOK, dtype=np.float64)
        ang = freqs[:, None] * pos[None, :]
        im = dict(shared)
        im["xT"] = np.ascontiguousarray(x[b, j * TOK:(j + 1) * TOK, :].T)
        im["cosd"] = np.cos(ang).astype(np.float32)
        im["ssin"] = (sgn[:, None] * np.sin(ang)).astype(np.float32)
        sel = np.zeros((64, 8), np.float32)
        for r in range(8):
            if r // 4 == b and r % 4 < j:
                sel[:, r] = 1.0
        im["selc"] = sel
        in_maps.append(im)
    return in_maps


def kernel(**inputs):
    nc = build_program(DUMPS)
    in_maps = _host_inputs(inputs)
    res = run_bass_kernel_spmd(nc, in_maps, core_ids=list(range(8)))
    out = np.empty((2, 8192, 1024), np.float32)
    for c in range(8):
        b, j = c // 4, c % 4
        out[b, j * TOK:(j + 1) * TOK, :] = res.results[c]["y"].T
    return out
